# version 11
# baseline (speedup 1.0000x reference)
import numpy as np
import concourse.bass as bass
import concourse.mybir as mybir
from concourse.bass_utils import run_bass_kernel_spmd

F32 = mybir.dt.float32
BF16 = mybir.dt.bfloat16
F16 = mybir.dt.float16
I8 = mybir.dt.int8
AF = mybir.ActivationFunctionType
ALU = mybir.AluOpType
AXX = mybir.AxisListType.X

D, DH, DK, DV, H, HQK, HV = 1024, 4096, 1024, 2048, 4, 256, 512
KD, MH, MV = D // 128, DH // 128, DV // 128
LN16 = float(np.log(1.0 / 16.0))
EPS = 1e-5
NEG = -1.0e20
MAGIC = 12582912.0
SZ = {F32: 4, BF16: 2, F16: 2, I8: 1}


def ceil_div(a, b):
    return (a + b - 1) // b


def rnd64(x):
    return (x + 63) // 64 * 64


def build_program(SEG=2048, taps=()):
    _, st1 = _build(SEG, taps, None, False)
    nc, _ = _build(SEG, taps, st1, True)
    return nc


def _build(SEG=2048, taps=(), st_seed=None, merge=False):
    OC = SEG // 256
    NJ = SEG // 128
    G = SEG + 4
    GV = SEG + 2
    NT = ceil_div(GV, 512)
    TT = [(i * 512, min(512, GV - i * 512)) for i in range(NT)]
    NTT = max(1, SEG // 512)
    TK = [(1 + i * 512, min(512, SEG - i * 512)) for i in range(NTT)]
    NS = ceil_div(GV, 128)
    NSS = SEG // 128
    NPAIR = max(1, SEG // 512)
    PW = SEG // NPAIR

    nc = bass.Bass("TRN2", target_bir_lowering=False)

    def din(name, shape):
        return nc.dram_tensor(name, list(shape), F32, kind="ExternalInput").ap()

    x_in = din("x", [D, G])
    wq_t, wk_t = din("wq_t", [D, DK]), din("wk_t", [D, DK])
    wv_t, wg_t = din("wv_t", [D, DV]), din("wg_t", [D, DV])
    wgk1_t, wgk2_t = din("wgk1_t", [D, 32]), din("wgk2_t", [32, DK])
    bgk2_i = din("bgk2", [DK])
    wout_t = din("wout_t", [DV, D])
    w1_t, w2_t = din("w1_t", [D, DH]), din("w2_t", [DH, D])
    conv_w_i, conv_b_i = din("conv_w", [3, DH]), din("conv_b", [DH])
    rms_w_i = din("rms_w", [HV])
    ctl_i = din("ctl", [128, 32])

    y_out = nc.dram_tensor("y", [D, SEG], F32, kind="ExternalOutput").ap()
    tap_t = {}
    for name, shape in taps:
        tap_t[name] = nc.dram_tensor("tap_" + name, list(shape), F32, kind="ExternalOutput").ap()

    w1q_d = nc.dram_tensor("w1q_d", [MH * KD * 128 * 128], BF16).ap()
    w2q_d = nc.dram_tensor("w2q_d", [KD * MH * 128 * 128], BF16).ap()
    h2_d = nc.dram_tensor("h2_d", [MH, 128, SEG], BF16).ap()
    o_d = nc.dram_tensor("o_d", [MV, 128, SEG], BF16).ap()
    qt_d = nc.dram_tensor("qt_d", [KD, 128, SEG], BF16).ap()
    v_d = nc.dram_tensor("v_d", [NJ, 128, DV], BF16).ap()
    gate_d = nc.dram_tensor("gate_d", [MV, 128, SEG], BF16).ap()
    x1_d = nc.dram_tensor("x1_d", [D, G], F32).ap()
    cc_s = nc.dram_tensor("cc_s", [H * 257 * 512], F32).ap()
    cc_r = nc.dram_tensor("cc_r", [4 * H * 257 * 512], F32).ap()
    cc2_s = nc.dram_tensor("cc2_s", [2 * D], F32).ap()
    cc2_r = nc.dram_tensor("cc2_r", [4 * 2 * D], F32).ap()

    psA = nc.alloc_psum_tensor("psA", [128, 512], F32)
    psB = nc.alloc_psum_tensor("psB", [128, 512], F32)
    psC = nc.alloc_psum_tensor("psC", [128, 512], F32)
    psD = nc.alloc_psum_tensor("psD", [128, 512], F32)
    psE = nc.alloc_psum_tensor("psE", [128, 512], F32)
    psF = nc.alloc_psum_tensor("psF", [128, 512], F32)
    psG = nc.alloc_psum_tensor("psG", [128, 512], F32)
    psT = nc.alloc_psum_tensor("psT", [128, 1024], BF16)

    SB = nc.alloc_sbuf_tensor
    ident = SB("ident", [128, 128], BF16)
    identf = SB("identf", [128, 128], F32)
    maskJT = SB("maskJT", [128, 128], BF16)
    ones_b = SB("ones_b", [128, 128], BF16)
    onesf = SB("onesf", [128, 128], F32)
    ones1f = SB("ones1f", [1, 128], F32)
    ctl_sb = SB("ctl_sb", [128, 32], F32)
    convb_sb = SB("convb_sb", [128, MH], F32)
    convw_sb = SB("convw_sb", [128, 3 * MH], F32)
    convw_s = SB("convw_s", [128, 3 * MH], F32)
    rmsw_sb = SB("rmsw_sb", [128, 4], F32)
    bgk2_sb = SB("bgk2_sb", [128, KD], F32)
    wsum = SB("wsum", [128, 70], F32)
    scal = SB("scal", [1, 16], F32)
    scalb = SB("scalb", [128, 16], F32)
    Brun = SB("Brun", [128, KD], F32)
    ebias = SB("ebias", [128, 2 * KD], F32)
    stat_tok = SB("stat_tok", [128, 40], F32)
    rows8 = SB("rows8", [128, 8 * KD], F32)
    Lg = SB("Lg", [128, 4 * KD], F32)
    eacc = SB("eacc", [128, KD], F32)
    ecol = SB("ecol", [128, KD], F32)
    sjb = SB("sjb", [128, 512], F32)
    wst = [SB(f"wst{i}", [128, 1024], BF16) for i in range(2)]
    xbfr = SB("xbfr", [128, 1024], BF16)
    x2br = SB("x2br", [128, 1024], BF16)
    wgk1a = [SB(f"wgk1a{k}", [128, 32], BF16) for k in range(KD)]
    wgk2a = SB("wgk2a", [32, DK], BF16)
    zz256 = SB("zz256", [128, 256], F16)
    Abuf = SB("Abuf", [128, 3 * 128], BF16)
    hsel = SB("hsel", [128, 16], F32)
    ytmp = SB("ytmp", [128, 512], F32)
    epsc = SB("epsc", [128, 1], F32)
    zzf2 = SB("zzf2", [128, 2], F32)
    rowsc = SB("rowsc", [1, 512], F32)
    rowsc2 = SB("rowsc2", [1, 512], F32)
    o2scr = SB("o2scr", [128, 512], BF16)
    wvev = SB("wvev", [128, 512], BF16)

    base0 = rnd64(nc.sbuf_base)
    top = nc.sbuf_top

    def at(name, shape, dtype, off):
        per = int(np.prod(shape[1:])) * SZ[dtype]
        assert base0 + off + per <= top, f"{name}: off {off}+{per} exceeds {top - base0}"
        return nc.alloc_sbuf_tensor_at(name, list(shape), dtype, offset=base0 + off)

    RX = 0
    GB = rnd64(G * 4)
    xg = [at(f"xg{k}", [128, G], F32, RX + k * GB) for k in range(KD)]
    SGB = rnd64(SEG * 2)
    g16 = [at(f"g16_{k}", [128, SEG], F16, RX + k * SGB) for k in range(KD)]
    wqa = [at(f"wqa{k}", [128, DK], BF16, RX + 8 * SGB + k * 2048) for k in range(KD)]
    wka = [at(f"wka{k}", [128, DK], BF16, RX + 8 * SGB + 16384 + k * 2048) for k in range(KD)]
    RXE = max(8 * GB, 8 * SGB + 32768)

    RA = RXE
    GVB = rnd64(GV * 4)
    GV2B = rnd64(GV * 2)
    SEGB2 = rnd64(SEG * 2)
    sT1 = at("sT1", [128, GV], F32, RA + 0)
    sT2 = at("sT2", [128, GV], F32, RA + GVB)
    sScr = at("sScr", [128, GV], F32, RA + 2 * GVB)
    maxP = at("maxP", [128, NS * 128], BF16, RA + 3 * GVB)
    minP = at("minP", [128, NS * 128], BF16, RA + 3 * GVB + rnd64(NS * 256))
    hB = [at(f"hB{m}", [128, GV], BF16, RA + m * GV2B) for m in range(8)]
    OF1 = max(8 * GV2B, MH * 1024)
    h2t = at("h2t", [128, SEG], BF16, RA + OF1)
    habs = at("habs", [128, SEG], BF16, RA + OF1 + SEGB2)
    hmax = at("hmax", [128, SEG], BF16, RA + OF1 + 2 * SEGB2)
    OF2 = OF1 + 3 * SEGB2
    diag = [at(f"diag{i}", [128, 128], BF16, RA + OF2 + i * 256) for i in range(24)]
    OF3 = OF2 + 16384
    sT3 = at("sT3", [128, GV], F32, RA + OF3)
    sT4 = at("sT4", [128, GV], F32, RA + OF3 + GVB)
    RAE = RA + OF3 + 2 * GVB
    assert rnd64(PW * 2) == 1024
    xq2A = [at(f"xq2A_{k}", [128, PW], BF16, RA + k * 1024) for k in range(MH)]
    eqc = at("eqc", [128, KD * 256], BF16, RA + 0)
    ekc = at("ekc", [128, KD * 256], BF16, RA + 4224)
    qtc = at("qtc", [128, KD * 256], BF16, RA + 8448)
    ktc = at("ktc", [128, KD * 256], BF16, RA + 12672)
    ktok = at("ktok", [128, 2 * DK], BF16, RA + 16896)
    csc = at("csc", [128, KD * 256], F32, RA + 21120)
    och = at("och", [128, MV * 256], BF16, RA + 29312)
    vtk = [at(f"vtk{i}", [128, 2 * DV], BF16, RA + 37504 + i * 8192) for i in range(2)]
    T32 = at("T32", [128, KD * 512], F32, RA + 53888)
    T16 = at("T16", [128, KD * 512], BF16, RA + 70272)
    ALE = 70272 + 8192
    u_bf = at("u_bf", [32, SEG], BF16, RA + 0)
    sgs = [at(f"sgs_{k}", [128, SEG // 2], F32, RA + 4224 + k * rnd64(SEG * 2)) for k in range(KD)]
    wva = [at(f"wva{k}", [128, DV], BF16, RA + 4224 + 8 * rnd64(SEG * 2) + k * 4096) for k in range(KD)]
    vtmp = at("vtmp", [128, DV], BF16, RA + 4224 + 8 * rnd64(SEG * 2) + 32768)
    APRE = 4224 + 8 * rnd64(SEG * 2) + 32768 + 4096
    Sin = at("Sin", [128, KD * 512], F32, RA + 0)
    msb_t = at("msb_t", [128, H * 256], F32, RA + 4224)
    wga = [at(f"wga{k}", [128, DV], BF16, RA + 37504 + k * 4096) for k in range(KD)]
    woa = [at(f"woa{k}", [128, D], BF16, RA + 37504 + k * 2048) for k in range(MV)]
    gch = at("gch", [128, MV * 256], BF16, RA + 70272)
    S16b = at("S16b", [128, KD * 512], BF16, RA + 16896)
    RAE2 = ALE

    RB = RA + max(RAE - RA, RAE2, APRE, MH * rnd64(PW * 2))
    xq = [at(f"xq{k}", [128, GV], BF16, RB + k * GV2B) for k in range(KD)]
    w2sb = [at(f"w2sb{i}", [128, 4096], BF16, RB + i * 8192) for i in range(2)]
    _xq2b_off = lambda k: (RA + OF1 + k * 1024) if k < 28 else (RB + 16448 + (k - 28) * 1024)
    xq2B = [at(f"xq2B_{k}", [128, PW], BF16, _xq2b_off(k)) for k in range(MH)]
    h2u = [at(f"h2u{i}", [128, PW], BF16, RB + 20544 + i * 1024) for i in range(2)]
    xn2 = [at(f"xn2_{k}", [128, SEG], BF16, RB + k * SGB) for k in range(KD)]
    FZ = RB + 8 * GV2B
    wq32d = [at(f"wq32d{i}", [128, 1024], F32, RA + OF2 + 6144 + i * 4096) for i in range(2)]
    _statE = rnd64(3 * GVB + 2 * rnd64(NS * 256))
    w2q32d = [at(f"w2q32d{i}", [128, 1024], F32, RA + max(OF1, _statE) + i * 4096) for i in range(2)]
    wqbd = at("wqbd", [128, 2048], BF16, FZ)
    absb = at("absb", [128, 1024], BF16, FZ + 4096)
    RBE = FZ + 6144
    assert base0 + RBE <= top, f"total arena {RBE} over {top - base0}"

    class Sem:
        def __init__(self, name, dk=1):
            self.h = nc.alloc_semaphore(name)
            self.n = 0
            self.name = name
            self.dk = dk

        def inc(self, inst, k=None):
            k = self.dk if k is None else k
            inst.then_inc(self.h, k)
            self.n += k
            return inst

    sTE, sVE, sSE, sGC, sCC = (Sem(x) for x in ("sTE", "sVE", "sSE", "sGC", "sCC"))

    marks = {}

    def need(e, en, *deps):
        for d in deps:
            if not d:
                continue
            if isinstance(d, list):
                need(e, en, *d)
                continue
            sem, v = d
            if v is None or v <= 0:
                continue
            key = (en, sem.name)
            if marks.get(key, 0) < v:
                own = {"tensor": sTE, "vector": sVE, "scalar": sSE}.get(en)
                if own is not None and sem is own and v > sem.n:
                    import sys as _s
                    print(f"SELF-FUTURE WAIT: {en} waits {sem.name}>={v} at {sem.n}", file=_s.stderr)
                e.wait_ge(sem.h, v)
                marks[key] = v

    class DQ:

        def __init__(self, name):
            self.s = [Sem(name + "0", 16), Sem(name + "1", 16)]
            self.k = 0
            self.last_end = [0, 0]

        def begin(self, e, en):
            p = self.k % 2
            need(e, en, (self.s[p], self.last_end[p]))

        def inc(self, inst):
            self.s[self.k % 2].inc(inst)
            return inst

        def end(self):
            p = self.k % 2
            tok = (self.s[p], self.s[p].n)
            self.last_end[p] = self.s[p].n
            self.k += 1
            return tok

        def drain(self, e, en):
            need(e, en, (self.s[0], self.s[0].n), (self.s[1], self.s[1].n))

        def fence(self):
            return [(self.s[0], self.s[0].n), (self.s[1], self.s[1].n)]

    qIN, qGP, qOUT = DQ("qIN"), DQ("qGP"), DQ("qOUT")

    block_cm = nc.Block()
    block = block_cm.__enter__()
    equeue = {k: [] for k in ("sync", "gpsimd", "tensor", "vector", "scalar")}

    def on(ename, f):
        if merge:
            equeue[ename].append(f)
        else:
            getattr(block, ename)(lambda e, f=f, ename=ename: f(e, ename))

    def _tokval(v):
        if isinstance(v, tuple):
            return ("t", v[0].name, v[1])
        if isinstance(v, list):
            return tuple(_tokval(x) for x in v)
        return v

    if st_seed:
        sem_by_name = {s.name: s for s in (sTE, sVE, sSE, sGC, sCC)}
        for q in (qIN, qGP, qOUT):
            for s in q.s:
                sem_by_name[s.name] = s

        def _remap(v):
            if isinstance(v, tuple):
                return (sem_by_name[v[0].name], v[1])
            if isinstance(v, list):
                return [_remap(x) for x in v]
            return v

        class _ChkDict(dict):
            def __setitem__(self, k, v):
                if k in self and _tokval(self[k]) != _tokval(v):
                    raise AssertionError(f"two-pass mismatch at {k}: {_tokval(self[k])} -> {_tokval(v)}")
                dict.__setitem__(self, k, _remap(v) if True else v)
        st = _ChkDict({k: _remap(v) for k, v in st_seed.items()})
    else:
        st = {}

    def ph0_sync(e, en):
        for k in range(KD):
            qIN.begin(e, en)
            qIN.inc(e.dma_start(out=xg[k][:, :], in_=x_in[k * 128:(k + 1) * 128, :]))
        st["x_loaded"] = qIN.end()
        qIN.begin(e, en)
        qIN.inc(e.dma_start(out=ctl_sb[:, :], in_=ctl_i))
        for m in range(MH):
            qIN.inc(e.dma_start(out=convb_sb[:, m:m + 1], in_=conv_b_i[m * 128:(m + 1) * 128].rearrange("(p o) -> p o", o=1)))
        for tp in range(3):
            for m in range(MH):
                qIN.inc(e.dma_start(out=convw_sb[:, tp * MH + m:tp * MH + m + 1], in_=conv_w_i[tp, m * 128:(m + 1) * 128].rearrange("(p o) -> p o", o=1)))
        for c in range(4):
            qIN.inc(e.dma_start(out=rmsw_sb[:, c:c + 1], in_=rms_w_i[c * 128:(c + 1) * 128].rearrange("(p o) -> p o", o=1)))
        for k in range(KD):
            qIN.inc(e.dma_start(out=bgk2_sb[:, k:k + 1], in_=bgk2_i[k * 128:(k + 1) * 128].rearrange("(p o) -> p o", o=1)))
        st["consts_loaded"] = qIN.end()

    def ph0_gp(e, en):
        sGC.inc(e.memset(ident[:, :], 0.0))
        sGC.inc(e.memset(identf[:, :], 0.0))
        sGC.inc(e.memset(maskJT[:, :], 0.0))
        need(e, en, (sGC, sGC.n))
        sGC.inc(e.affine_select(out=ident[:, :], in_=ident[:, :], compare_op=ALU.not_equal,
                                fill=1.0, base=0, pattern=[[-1, 128]], channel_multiplier=1))
        sGC.inc(e.affine_select(out=identf[:, :], in_=identf[:, :], compare_op=ALU.not_equal,
                                fill=1.0, base=0, pattern=[[-1, 128]], channel_multiplier=1))
        sGC.inc(e.affine_select(out=maskJT[:, :], in_=maskJT[:, :], compare_op=ALU.is_gt,
                                fill=1.0, base=0, pattern=[[-1, 128]], channel_multiplier=1))
        e.memset(ones_b[:, :], 1.0)
        e.memset(onesf[:, :], 1.0)
        e.memset(ones1f[:, :], 1.0)
        e.memset(zz256[:, :], 0.0)
        e.memset(epsc[:, :], float(EPS))
        e.memset(zzf2[:, :], 0.0)
        e.memset(scal[:, :], 0.0)
        e.memset(wsum[:, :], 0.0)
        e.memset(stat_tok[:, :], 0.0)
        e.memset(rowsc[:, :], 0.0)
        e.memset(rowsc2[:, :], 0.0)
        sGC.inc(e.memset(Brun[:, :], 0.0))
        st["consts_built"] = sGC.n

    on("sync", ph0_sync)
    on("gpsimd", ph0_gp)

    NWT = 32

    def w_src_ap(widx, i):
        if widx == 0:
            return w1_t[(i // 4) * 128:(i // 4 + 1) * 128, (i % 4) * 1024:(i % 4 + 1) * 1024]
        return w2_t[i * 128:(i + 1) * 128, :]

    def w_out_ap(widx, i):
        out_d = w1q_d if widx == 0 else w2q_d
        off = ((i // 4) * 32 + (i % 4) * 8) * 16384 if widx == 0 else i * 8 * 16384
        return out_d[off:off + 8 * 16384].rearrange("(m p q) -> p m q", p=128, q=128)

    def qw_pass1_step(widx, i):
        buf = wq32d if widx == 0 else w2q32d
        q, eng = (qIN, "sync") if widx == 0 else (qGP, "gpsimd")

        def f1(e, en, widx=widx, i=i, q=q, buf=buf):
            need(e, en, (sSE, st.get(f"wabs{widx}_{i-2}", 0)))
            if widx == 1 and i < 2:
                need(e, en, (sSE, st[f"wabs0_{NWT-2+i}"]))
            q.begin(e, en)
            q.inc(e.dma_start(out=buf[i % 2][:, :], in_=w_src_ap(widx, i)))
            st[f"wl{widx}_{i}"] = q.end()

        def f1v(e, en, widx=widx, i=i, buf=buf):
            need(e, en, st[f"wl{widx}_{i}"], (sGC, st["consts_built"]))
            need(e, en, (sSE, sSE.n))
            sSE.inc(e.activation(absb[:, :], buf[i % 2][:, :], AF.Abs,
                                 accum_out=wsum[:, widx * 32 + i:widx * 32 + i + 1]))
            st[f"wabs{widx}_{i}"] = sSE.n
        on(eng, f1)
        on("scalar", f1v)

    def qw_scale(widx):
        def p1f(e, en):
            need(e, en, (sSE, st[f"wabs{widx}_{NWT-1}"]))
            need(e, en, (sVE, sVE.n))
            sVE.inc(e.tensor_reduce(wsum[:, 64 + widx:65 + widx], wsum[:, widx * 32:widx * 32 + NWT], AXX, ALU.add))
            st[f"wcol{widx}"] = sVE.n
        on("vector", p1f)

        def p2t(e, en):
            need(e, en, (sVE, st[f"wcol{widx}"]), (sGC, st["consts_built"]))
            sTE.inc(e.matmul(psF[0:1, 0:128], wsum[:, 64 + widx:65 + widx], onesf[:, 0:128], start=True, stop=True))
            st[f"wtot{widx}"] = sTE.n
        on("tensor", p2t)

        def p3v(e, en):
            need(e, en, (sTE, st[f"wtot{widx}"]), (sGC, st["consts_built"]))
            numel = float(DH * D)
            sVE.inc(e.tensor_scalar(scal[0:1, widx:widx + 1], psF[0:1, 0:1], 1.0 / numel, 1e-5, ALU.mult, ALU.max))
            need(e, en, (sVE, sVE.n))
            sVE.inc(e.reciprocal(scal[0:1, 2 + widx:3 + widx], scal[0:1, widx:widx + 1]))
            st[f"wrec{widx}"] = sVE.n
        on("vector", p3v)

        def p4t(e, en):
            need(e, en, (sVE, st[f"wrec{widx}"]))
            sTE.inc(e.matmul(psF[0:128, 0:16], ones1f[0:1, :], scal[0:1, 0:16], start=True, stop=True))
            st[f"wbc{widx}"] = sTE.n
        on("tensor", p4t)

        def p4v(e, en):
            need(e, en, (sTE, st[f"wbc{widx}"]))
            sVE.inc(e.tensor_copy(scalb[:, 0:16], psF[0:128, 0:16]))
            st[f"wscal{widx}"] = sVE.n
        on("vector", p4v)

    def qw_pass2_step(widx, i):
        buf = wq32d
        q = qIN if widx == 0 else qOUT

        def p5s(e, en, widx=widx, i=i, buf=buf, q=q):
            need(e, en, (sVE, st[f"wcol{widx}"]))
            if i >= 2:
                need(e, en, (sVE, st[f"wqd{widx}_{i-2}"]))
            elif widx == 1:
                need(e, en, (sVE, st[f"wqd0_{NWT-2+i}"]))
            q.begin(e, en)
            q.inc(e.dma_start(out=buf[i % 2][:, :], in_=w_src_ap(widx, i)))
            st[f"wl2{widx}_{i}"] = q.end()

        def p5v(e, en, widx=widx, i=i, buf=buf):
            need(e, en, st[f"wl2{widx}_{i}"], (sVE, st[f"wscal{widx}"]))
            need(e, en, (sVE, sVE.n))
            sVE.inc(e.tensor_scalar(buf[i % 2][:, :], buf[i % 2][:, :], scalb[:, 2 + widx:3 + widx], 1.49, ALU.mult, ALU.min))
            need(e, en, (sVE, sVE.n))
            sVE.inc(e.tensor_scalar(buf[i % 2][:, :], buf[i % 2][:, :], -1.49, MAGIC, ALU.max, ALU.add))
            need(e, en, (sVE, sVE.n))
            if i >= 2:
                need(e, en, st[f"wstr{widx}_{i-2}"])
            elif widx == 1:
                need(e, en, st[f"wstr0_{NWT-2+i}"])
            sVE.inc(e.tensor_scalar(wqbd[:, (i % 2) * 1024:(i % 2 + 1) * 1024], buf[i % 2][:, :], -MAGIC, None, ALU.add))
            st[f"wqd{widx}_{i}"] = sVE.n

        def p5g(e, en, widx=widx, i=i):
            need(e, en, (sVE, st[f"wqd{widx}_{i}"]))
            qGP.begin(e, en)
            qGP.inc(e.dma_start(
                out=w_out_ap(widx, i),
                in_=wqbd[:, (i % 2) * 1024:(i % 2 + 1) * 1024].rearrange("p (m q) -> p m q", q=128)))
            st[f"wstr{widx}_{i}"] = qGP.end()
        on("sync", p5s)
        on("vector", p5v)
        on("gpsimd", p5g)

    for i in range(NWT):
        qw_pass1_step(0, i)
    qw_scale(0)
    for i in range(NWT):
        qw_pass2_step(0, i)
        qw_pass1_step(1, i)
    st["wready0"] = [st[f"wstr0_{NWT-2}"], st[f"wstr0_{NWT-1}"]]


    def convscale(e, en):
        need(e, en, st["consts_loaded"], (sVE, st["wscal0"]))
        sVE.inc(e.tensor_scalar(convw_s[:, :], convw_sb[:, :], scalb[:, 0:1], None, ALU.mult))
        st["convw_scaled"] = sVE.n
    on("vector", convscale)

    def ffn_stats(app, with_quant=True, dep=()):
        a = f"f{app}"

        def sv(e, en):
            need(e, en, st["x_loaded"], *dep)
            if with_quant:
                if NS * 128 > GV:
                    e.memset(maxP[:, GV:NS * 128], 0.0)
                    e.memset(minP[:, GV:NS * 128], 0.0)
                sVE.inc(e.tensor_copy(maxP[:, 0:GV], xg[0][:, 0:GV]))
                sVE.inc(e.tensor_copy(minP[:, 0:GV], xg[0][:, 0:GV]))
                for k in range(1, KD):
                    need(e, en, (sVE, sVE.n))
                    sVE.inc(e.tensor_tensor(maxP[:, 0:GV], xg[k][:, 0:GV], maxP[:, 0:GV], ALU.max))
                    sVE.inc(e.tensor_tensor(minP[:, 0:GV], xg[k][:, 0:GV], minP[:, 0:GV], ALU.min))
                st[a + "trees"] = sVE.n
        on("vector", sv)

        for ti, (c0, w) in enumerate(TT):
            for k in range(KD):
                r = k % 2

                def s1v(e, en, ti=ti, c0=c0, w=w, k=k, r=r):
                    need(e, en, st["x_loaded"], *dep)
                    g = ti * KD + k - 2
                    if g >= 0:
                        pt, pk = divmod(g, KD)
                        need(e, en, (sTE, st.get(f"{a}smm{pt}_{pk}", 0)))
                    sVE.inc(e.tensor_copy(xbfr[:, r * 512:r * 512 + w], xg[k][:, c0:c0 + w]))
                    st[f"{a}sxb{ti}_{k}"] = sVE.n

                def s1se(e, en, ti=ti, c0=c0, w=w, k=k, r=r):
                    need(e, en, st["x_loaded"], *dep)
                    g = ti * KD + k - 2
                    if g >= 0:
                        pt, pk = divmod(g, KD)
                        need(e, en, (sTE, st.get(f"{a}smm{pt}_{pk}", 0)))
                    sSE.inc(e.activation(x2br[:, r * 512:r * 512 + w], xg[k][:, c0:c0 + w], AF.Square))
                    st[f"{a}sx2{ti}_{k}"] = sSE.n

                def s1t(e, en, ti=ti, c0=c0, w=w, k=k, r=r):
                    if k == 0 and ti >= 1:
                        need(e, en, (sVE, st[f"{a}var{ti-1}"]))
                    need(e, en, (sVE, st[f"{a}sxb{ti}_{k}"]), (sSE, st[f"{a}sx2{ti}_{k}"]))
                    e.matmul(psA[:, 0:w], ones_b[:, :], xbfr[:, r * 512:r * 512 + w],
                             start=(k == 0), stop=(k == KD - 1))
                    mm = e.matmul(psB[:, 0:w], ones_b[:, :], x2br[:, r * 512:r * 512 + w],
                                  start=(k == 0), stop=(k == KD - 1))
                    sTE.inc(mm)
                    st[f"{a}smm{ti}_{k}"] = sTE.n
                on("vector", s1v)
                on("scalar", s1se)
                on("tensor", s1t)
            st[f"{a}stats{ti}"] = st[f"{a}smm{ti}_{KD-1}"]

            def s2v(e, en, ti=ti, c0=c0, w=w):
                need(e, en, (sTE, st[f"{a}stats{ti}"]), *dep)
                sVE.inc(e.tensor_scalar(sScr[:, c0:c0 + w], psA[:, 0:w], 1.0 / D, None, ALU.mult))
                need(e, en, (sVE, sVE.n))
                sVE.inc(e.tensor_mul(sT1[:, c0:c0 + w], sScr[:, c0:c0 + w], sScr[:, c0:c0 + w]))
                need(e, en, (sVE, sVE.n))
                sVE.inc(e.scalar_tensor_tensor(sT2[:, c0:c0 + w], psB[:, 0:w], 1.0 / D, sT1[:, c0:c0 + w], ALU.mult, ALU.subtract))
                st[f"{a}var{ti}"] = sVE.n
            on("vector", s2v)

        def s3se(e, en):
            need(e, en, (sVE, st[f"{a}var{NT-1}"]))
            need(e, en, (sGC, st["consts_built"]))
            sSE.inc(e.activation(sT2[:, 0:GV], sT2[:, 0:GV], AF.Ln, bias=epsc[:, 0:1]))
            need(e, en, (sSE, sSE.n))
            sSE.inc(e.activation(sT2[:, 0:GV], sT2[:, 0:GV], AF.Exp, scale=-0.5))
            st[a + "rstd"] = sSE.n
        on("scalar", s3se)

        if not with_quant:
            return

        for i in range(NS):
            def s4t(e, en, i=i):
                need(e, en, (sVE, st[a + "trees"]))
                need(e, en, (sVE, st.get(f"{a}tred{i-1}", 0)))
                e.transpose(psT[:, 0:128], maxP[:, i * 128:(i + 1) * 128], ident[:, :])
                sTE.inc(e.transpose(psT[:, 128:256], minP[:, i * 128:(i + 1) * 128], ident[:, :]))
                st[f"{a}tp{i}"] = sTE.n

            def s4v(e, en, i=i):
                need(e, en, (sTE, st[f"{a}tp{i}"]))
                e.tensor_reduce(stat_tok[:, i:i + 1], psT[:, 0:128], AXX, ALU.max)
                sVE.inc(e.tensor_reduce(stat_tok[:, 20 + i:21 + i], psT[:, 128:256], AXX, ALU.min))
                st[f"{a}tred{i}"] = sVE.n
            on("tensor", s4t)
            on("vector", s4v)

        def s4f(e, en):
            need(e, en, (sVE, st[f"{a}tred{NS-1}"]))
            st[a + "treds"] = sVE.n
        on("vector", s4f)

        NR = ceil_div(NS, 4)
        for rnd in range(NR):
            tl = list(range(rnd * 4, min(rnd * 4 + 4, NS)))
            def b1t(e, en, rnd=rnd, tl=tl):
                need(e, en, (sVE, st[a + "treds"]))
                need(e, en, (sVE, st.get(f"{a}rowev{rnd-1}", 0)))
                for ii, i in enumerate(tl):
                    ins = e.matmul(psF[0:1, (i % 4) * 128:(i % 4) * 128 + 128], stat_tok[:, i:i + 1], identf[:, :],
                                   start=(ii == 0), stop=(ii == len(tl) - 1))
                for ii, i in enumerate(tl):
                    ins = e.matmul(psG[0:1, (i % 4) * 128:(i % 4) * 128 + 128], stat_tok[:, 20 + i:21 + i], identf[:, :],
                                   start=(ii == 0), stop=(ii == len(tl) - 1))
                sTE.inc(ins)
                st[f"{a}rowmm{rnd}"] = sTE.n
            def b1v(e, en, rnd=rnd, tl=tl):
                need(e, en, (sTE, st[f"{a}rowmm{rnd}"]))
                e.tensor_copy(rowsc[0:1, 0:len(tl) * 128], psF[0:1, 0:len(tl) * 128])
                sVE.inc(e.tensor_copy(rowsc2[0:1, 0:len(tl) * 128], psG[0:1, 0:len(tl) * 128]))
                st[f"{a}rowev{rnd}"] = sVE.n
            def b2t(e, en, rnd=rnd, tl=tl):
                need(e, en, (sVE, st[f"{a}rowev{rnd}"]))
                need(e, en, (sVE, st.get(f"{a}bcev{rnd-1}", 0)))
                for ii, i in enumerate(tl):
                    e.matmul(psE[:, (i % 4) * 128:(i % 4) * 128 + 128], ones1f[0:1, :], rowsc[0:1, (i % 4) * 128:(i % 4) * 128 + 128],
                             start=(ii == 0), stop=(ii == len(tl) - 1))
                    ins = e.matmul(psD[:, (i % 4) * 128:(i % 4) * 128 + 128], ones1f[0:1, :], rowsc2[0:1, (i % 4) * 128:(i % 4) * 128 + 128],
                                   start=(ii == 0), stop=(ii == len(tl) - 1))
                sTE.inc(ins)
                st[f"{a}bcmm{rnd}"] = sTE.n
            def b2v(e, en, rnd=rnd, tl=tl):
                need(e, en, (sTE, st[f"{a}bcmm{rnd}"]))
                for i in tl:
                    w = min(128, GV - i * 128)
                    if w <= 0:
                        continue
                    e.tensor_copy(sT3[:, i * 128:i * 128 + w], psE[:, (i % 4) * 128:(i % 4) * 128 + w])
                    ins = e.tensor_copy(sT4[:, i * 128:i * 128 + w], psD[:, (i % 4) * 128:(i % 4) * 128 + w])
                sVE.inc(ins)
                st[f"{a}bcev{rnd}"] = sVE.n
            on("tensor", b1t)
            on("vector", b1v)
            on("tensor", b2t)
            on("vector", b2v)
        def bfin(e, en):
            need(e, en, (sVE, st[f"{a}bcev{NR-1}"]))
            st[a + "bcasts"] = sVE.n
        on("vector", bfin)

        def s7v(e, en):
            need(e, en, (sVE, st[a + "bcasts"]), (sSE, st[a + "rstd"]))
            sVE.inc(e.tensor_sub(sT3[:, 0:GV], sT3[:, 0:GV], sScr[:, 0:GV]))
            sVE.inc(e.tensor_sub(sT4[:, 0:GV], sScr[:, 0:GV], sT4[:, 0:GV]))
            need(e, en, (sVE, sVE.n))
            sVE.inc(e.tensor_tensor(sT3[:, 0:GV], sT3[:, 0:GV], sT4[:, 0:GV], ALU.max))
            need(e, en, (sVE, sVE.n))
            sVE.inc(e.tensor_tensor(sT3[:, 0:GV], sT3[:, 0:GV], sT2[:, 0:GV], ALU.mult))
            need(e, en, (sVE, sVE.n))
            sVE.inc(e.tensor_scalar(sT3[:, 0:GV], sT3[:, 0:GV], 1e-5, None, ALU.max))
            need(e, en, (sVE, sVE.n))
            sVE.inc(e.tensor_scalar(sT4[:, 0:GV], sT3[:, 0:GV], 1.0 / 127.0, None, ALU.mult))
            sVE.inc(e.reciprocal(sT3[:, 0:GV], sT3[:, 0:GV]))
            need(e, en, (sVE, sVE.n))
            sVE.inc(e.tensor_scalar(sT3[:, 0:GV], sT3[:, 0:GV], 127.0, None, ALU.mult))
            need(e, en, (sVE, sVE.n))
            sVE.inc(e.tensor_tensor(sT3[:, 0:GV], sT3[:, 0:GV], sT2[:, 0:GV], ALU.mult))
            need(e, en, (sVE, sVE.n))
            sVE.inc(e.scalar_tensor_tensor(sT1[:, 0:GV], sScr[:, 0:GV], -1.0, sT3[:, 0:GV], ALU.mult, ALU.mult))
            st[a + "quantsc"] = sVE.n
        on("vector", s7v)

    def ffn_quant_apply(app):
        a = f"f{app}"

        def qv(e, en):
            need(e, en, (sVE, st[a + "quantsc"]))
            if app == 1:
                need(e, en, st.get("tap_xq0", 0), st.get("tap_xn2", 0))
            for k in range(KD):
                need(e, en, (sVE, sVE.n))
                sVE.inc(e.tensor_tensor(sScr[:, 0:GV], xg[k][:, 0:GV], sT3[:, 0:GV], ALU.mult))
                need(e, en, (sVE, sVE.n))
                sVE.inc(e.tensor_tensor(sScr[:, 0:GV], sScr[:, 0:GV], sT1[:, 0:GV], ALU.add))
                need(e, en, (sVE, sVE.n))
                sVE.inc(e.tensor_scalar(sScr[:, 0:GV], sScr[:, 0:GV], 127.0, -128.0, ALU.min, ALU.max))
                need(e, en, (sVE, sVE.n))
                sVE.inc(e.tensor_scalar(sScr[:, 0:GV], sScr[:, 0:GV], MAGIC, -MAGIC, ALU.add, ALU.add))
                need(e, en, (sVE, sVE.n))
                sVE.inc(e.tensor_tensor(xq[k][:, 0:GV], sScr[:, 0:GV], sT4[:, 0:GV], ALU.mult))
                st[f"{a}xq{k}"] = sVE.n
            st[a + "xqall"] = sVE.n
        on("vector", qv)

    def ffn_mm1_conv(app, extra_per_blk=None):
        a = f"f{app}"
        for blk in range(4):
            if extra_per_blk:
                extra_per_blk(blk)
            def dv(e, en, blk=blk):
                need(e, en, (sVE, st["convw_scaled"]), (sGC, st["consts_built"]))
                key = f"{a}cvu{blk-1}" if blk > 0 else (f"f{app-1}cvu3" if app > 0 else None)
                if key:
                    need(e, en, (sTE, st[key]))
                for t in range(3):
                    for mloc in range(8):
                        m = blk * 8 + mloc
                        sVE.inc(e.tensor_scalar(diag[t * 8 + mloc][:, :], ident[:, :], convw_s[:, t * MH + m:t * MH + m + 1], None, ALU.mult))
                st[f"{a}diag{blk}"] = sVE.n
            on("vector", dv)

            for mloc in range(8):
                m = blk * 8 + mloc
                r = m % 2

                def ws(e, en, m=m, r=r):
                    need(e, en, st["wready0"])
                    need(e, en, (sTE, st.get(f"{a}mmw{m-2}", 0)))
                    qIN.begin(e, en)
                    qIN.inc(e.dma_start(
                        out=wst[r][:, :].rearrange("p (k q) -> p k q", q=128),
                        in_=w1q_d.rearrange("(t p q) -> t p q", p=128, q=128)[m::MH, :, :].rearrange("k p q -> p k q")))
                    st[f"{a}wl{m}"] = qIN.end()

                on("sync", ws)

                def mt0(e, en, m=m, r=r):
                    need(e, en, st[f"{a}wl{m}"], (sVE, st[a + "xqall"]))
                on("tensor", mt0)
                for ti, (c0, w) in enumerate(TT):
                    def mt(e, en, m=m, r=r, ti=ti, c0=c0, w=w):
                        ps = psA if ti % 2 == 0 else psB
                        need(e, en, (sVE, st.get(f"{a}hev{m}_{ti-2}", 0)))
                        lp = ti + 2 * ((NT - 1 - ti) // 2)
                        need(e, en, (sVE, st.get(f"{a}hev{m-1}_{lp}", 0)))
                        for k in range(KD):
                            ins = e.matmul(ps[:, 0:w], wst[r][:, k * 128:(k + 1) * 128], xq[k][:, c0:c0 + w],
                                           start=(k == 0), stop=(k == KD - 1))
                        sTE.inc(ins)
                        st[f"{a}hmm{m}_{ti}"] = sTE.n
                        if ti == NT - 1:
                            st[f"{a}mmw{m}"] = sTE.n

                    def ev(e, en, m=m, mloc=mloc, ti=ti, c0=c0, w=w):
                        need(e, en, (sTE, st.get(f"{a}cmm{m-8}_{len(TK)-1}", 0)))
                        ps = psA if ti % 2 == 0 else psB
                        need(e, en, (sTE, st[f"{a}hmm{m}_{ti}"]))
                        sVE.inc(e.tensor_copy(hB[mloc][:, c0:c0 + w], ps[:, 0:w]))
                        st[f"{a}hev{m}_{ti}"] = sVE.n
                        if ti == NT - 1:
                            st[f"{a}hband{m}"] = sVE.n
                    on("tensor", mt)
                    on("vector", ev)

            for mloc in range(8):
                m = blk * 8 + mloc
                for ti, (c0, w) in enumerate(TK):
                    g = m * len(TK) + ti

                    def ct(e, en, m=m, mloc=mloc, blk=blk, ti=ti, c0=c0, w=w, g=g):
                        need(e, en, (sVE, st[f"{a}hband{m}"]), (sVE, st[f"{a}diag{blk}"]))
                        gp = g - 2
                        if gp >= 0:
                            pm, pt = divmod(gp, len(TK))
                            need(e, en, (sSE, st.get(f"{a}cse{pm}_{pt}", 0)))
                        ps = psC if g % 2 == 0 else psD
                        e.matmul(ps[:, 0:w], diag[0 * 8 + mloc][:, :], hB[mloc][:, c0 - 1:c0 - 1 + w], start=True, stop=False)
                        e.matmul(ps[:, 0:w], diag[1 * 8 + mloc][:, :], hB[mloc][:, c0:c0 + w], start=False, stop=False)
                        ins = e.matmul(ps[:, 0:w], diag[2 * 8 + mloc][:, :], hB[mloc][:, c0 + 1:c0 + 1 + w], start=False, stop=True)
                        sTE.inc(ins)
                        st[f"{a}cmm{m}_{ti}"] = sTE.n
                        if mloc == 7 and ti == len(TK) - 1:
                            st[f"{a}cvu{blk}"] = sTE.n

                    def cs(e, en, m=m, ti=ti, c0=c0, w=w, g=g):
                        ps = psC if g % 2 == 0 else psD
                        need(e, en, (sTE, st[f"{a}cmm{m}_{ti}"]))
                        if ti == 0:
                            need(e, en, st.get(f"{a}h2st{m-1}", 0))
                        sSE.inc(e.activation(h2t[:, c0 - 1:c0 - 1 + w], ps[:, 0:w], AF.Silu, bias=convb_sb[:, m:m + 1]))
                        st[f"{a}cse{m}_{ti}"] = sSE.n
                    on("tensor", ct)
                    on("scalar", cs)

                def csa(e, en, m=m):
                    need(e, en, (sSE, st[f"{a}cse{m}_{len(TK)-1}"]))
                    need(e, en, (sVE, st.get(f"{a}hmax{m-1}", 0)))
                    sSE.inc(e.activation(habs[:, 0:SEG], h2t[:, 0:SEG], AF.Abs))
                    st[f"{a}habs{m}"] = sSE.n

                def cv(e, en, m=m):
                    need(e, en, (sSE, st[f"{a}habs{m}"]))
                    if m == 0:
                        sVE.inc(e.tensor_copy(hmax[:, 0:SEG], habs[:, 0:SEG]))
                    else:
                        sVE.inc(e.tensor_tensor(hmax[:, 0:SEG], habs[:, 0:SEG], hmax[:, 0:SEG], ALU.max))
                    st[f"{a}hmax{m}"] = sVE.n

                def cg(e, en, m=m):
                    need(e, en, (sSE, st[f"{a}habs{m}"]))
                    qGP.begin(e, en)
                    qGP.inc(e.dma_start(out=h2_d[m, :, :], in_=h2t[:, 0:SEG]))
                    st[f"{a}h2st{m}"] = qGP.end()
                on("scalar", csa)
                on("vector", cv)
                on("gpsimd", cg)

    def ffn_quant2(app):
        a = f"f{app}"

        for i in range(NSS):
            def q2t(e, en, i=i):
                need(e, en, (sVE, st[f"{a}hmax{MH-1}"]))
                need(e, en, (sVE, st.get(f"{a}q2red{i-1}", 0)))
                sTE.inc(e.transpose(psT[:, 0:128], hmax[:, i * 128:(i + 1) * 128], ident[:, :]))
                st[f"{a}q2tp{i}"] = sTE.n

            def q2v(e, en, i=i):
                need(e, en, (sTE, st[f"{a}q2tp{i}"]))
                sVE.inc(e.tensor_reduce(stat_tok[:, i:i + 1], psT[:, 0:128], AXX, ALU.max))
                st[f"{a}q2red{i}"] = sVE.n
            on("tensor", q2t)
            on("vector", q2v)

        def q2f(e, en):
            need(e, en, (sVE, st[f"{a}q2red{NSS-1}"]))
            st[a + "q2reds"] = sVE.n
        on("vector", q2f)

        NR2 = ceil_div(NSS, 4)
        for rnd in range(NR2):
            tl = list(range(rnd * 4, min(rnd * 4 + 4, NSS)))
            def qb1t(e, en, rnd=rnd, tl=tl):
                need(e, en, (sVE, st[a + "q2reds"]))
                need(e, en, (sVE, st.get(f"{a}q2rowev{rnd-1}", 0)))
                for ii, i in enumerate(tl):
                    ins = e.matmul(psF[0:1, (i % 4) * 128:(i % 4) * 128 + 128], stat_tok[:, i:i + 1], identf[:, :],
                                   start=(ii == 0), stop=(ii == len(tl) - 1))
                sTE.inc(ins)
                st[f"{a}q2rowmm{rnd}"] = sTE.n
            def qb1v(e, en, rnd=rnd, tl=tl):
                need(e, en, (sTE, st[f"{a}q2rowmm{rnd}"]))
                sVE.inc(e.tensor_copy(rowsc[0:1, 0:len(tl) * 128], psF[0:1, 0:len(tl) * 128]))
                st[f"{a}q2rowev{rnd}"] = sVE.n
            def qb2t(e, en, rnd=rnd, tl=tl):
                need(e, en, (sVE, st[f"{a}q2rowev{rnd}"]))
                need(e, en, (sVE, st.get(f"{a}q2bcev{rnd-1}", 0)))
                for ii, i in enumerate(tl):
                    ins = e.matmul(psE[:, (i % 4) * 128:(i % 4) * 128 + 128], ones1f[0:1, :], rowsc[0:1, (i % 4) * 128:(i % 4) * 128 + 128],
                                   start=(ii == 0), stop=(ii == len(tl) - 1))
                sTE.inc(ins)
                st[f"{a}q2bcmm{rnd}"] = sTE.n
            def qb2v(e, en, rnd=rnd, tl=tl):
                need(e, en, (sTE, st[f"{a}q2bcmm{rnd}"]))
                for i in tl:
                    ins = e.tensor_copy(sT3[:, 1 + i * 128:1 + (i + 1) * 128], psE[:, (i % 4) * 128:(i % 4) * 128 + 128])
                sVE.inc(ins)
                st[f"{a}q2bcev{rnd}"] = sVE.n
            on("tensor", qb1t)
            on("vector", qb1v)
            on("tensor", qb2t)
            on("vector", qb2v)

        def q4v(e, en):
            need(e, en, (sVE, st[f"{a}q2bcev{NR2-1}"]))
            need(e, en, (sVE, sVE.n))
            sVE.inc(e.tensor_scalar(sT3[:, 1:1 + SEG], sT3[:, 1:1 + SEG], 1e-5, None, ALU.max))
            need(e, en, (sVE, sVE.n))
            sVE.inc(e.tensor_scalar(sT4[:, 1:1 + SEG], sT3[:, 1:1 + SEG], 0.5 / 127.0, None, ALU.mult))
            need(e, en, (sVE, sVE.n))
            sVE.inc(e.tensor_scalar(sT4[:, 1:1 + SEG], sT4[:, 1:1 + SEG], scalb[:, 1:2], None, ALU.mult))
            sVE.inc(e.reciprocal(sT3[:, 1:1 + SEG], sT3[:, 1:1 + SEG]))
            need(e, en, (sVE, sVE.n))
            sVE.inc(e.tensor_scalar(sT3[:, 1:1 + SEG], sT3[:, 1:1 + SEG], 127.0, None, ALU.mult))
            st[a + "q2sc"] = sVE.n
        on("vector", q4v)

    def ffn_mm2(app):
        a = f"f{app}"

        def emit_quant(pr, ks):
            pc0 = 1 + pr * PW
            xq2 = xq2A if pr % 2 == 0 else xq2B
            for k in ks:
                hb = h2u[k % 2]

                def l2g(e, en, k=k, pr=pr, pc0=pc0, hb=hb):
                    need(e, en, st[f"{a}h2st{k}"], st[f"{a}h2st{MH-1}"], (sSE, st[f"{a}habs{MH-1}"]))
                    need(e, en, (sVE, st.get(f"{a}xq2_{pr}_{k-2}", 0)))
                    if k < 2 and pr > 0:
                        need(e, en, (sVE, st.get(f"{a}xq2_{pr-1}_{MH-2+k}", 0)))
                    qGP.begin(e, en)
                    qGP.inc(e.dma_start(out=hb[:, 0:PW], in_=h2_d[k, :, pc0 - 1:pc0 - 1 + PW]))
                    st[f"{a}l2{pr}_{k}"] = qGP.end()

                def l2v(e, en, k=k, pr=pr, pc0=pc0, hb=hb, xq2=xq2):
                    need(e, en, st[f"{a}l2{pr}_{k}"], (sVE, st[a + "q2sc"]), (sTE, st[f"{a}cvu3"]))
                    if app == 0 and pr == 1:
                        need(e, en, st["wready1"])
                    need(e, en, (sTE, st.get(f"{a}ymm{pr-2}_{KD-1}", 0)))
                    need(e, en, (sVE, sVE.n))
                    sVE.inc(e.tensor_tensor(ytmp[:, 0:PW], hb[:, 0:PW], sT3[:, pc0:pc0 + PW], ALU.mult))
                    need(e, en, (sVE, sVE.n))
                    sVE.inc(e.tensor_scalar(xq2[k][:, 0:PW], ytmp[:, 0:PW], MAGIC, -MAGIC, ALU.add, ALU.add))
                    st[f"{a}xq2_{pr}_{k}"] = sVE.n
                on("gpsimd", l2g)
                on("vector", l2v)

        _QSPLIT = [5, 5, 5, 5, 4, 4, 4, 0]

        emit_quant(0, range(MH))
        for pr in range(NPAIR):
            pc0 = 1 + pr * PW
            xq2 = xq2A if pr % 2 == 0 else xq2B
            for mo in range(KD):
                r = mo % 2

                def w2s(e, en, mo=mo, r=r, pr=pr):
                    need(e, en, st["wready1"], (sTE, st[f"{a}cvu3"]))
                    if app == 0:
                        need(e, en, (sVE, st[f"{a}xq{KD-1}"]))
                    if mo >= 2:
                        need(e, en, (sTE, st[f"{a}ymm{pr}_{mo-2}"]))
                    elif pr > 0:
                        need(e, en, (sTE, st[f"{a}ymm{pr-1}_{KD-2+mo}"]))
                    qIN.begin(e, en)
                    qIN.inc(e.dma_start(
                        out=w2sb[r][:, :].rearrange("p (k q) -> p k q", q=128),
                        in_=w2q_d.rearrange("(t p q) -> t p q", p=128, q=128)[mo::KD, :, :].rearrange("k p q -> p k q")))
                    st[f"{a}w2l{pr}_{mo}"] = qIN.end()

                def ymt(e, en, mo=mo, r=r, pr=pr, xq2=xq2):
                    need(e, en, st[f"{a}w2l{pr}_{mo}"],
                         (sVE, st[f"{a}xq2_{pr}_{MH-1}"]))
                    ps = psA if mo % 2 == 0 else psB
                    if mo >= 2:
                        need(e, en, (sVE, st[f"{a}yev{pr}_{mo-2}"]))
                    elif pr > 0:
                        need(e, en, (sVE, st[f"{a}yev{pr-1}_{KD-2+mo}"]))
                    for k in range(MH):
                        ins = e.matmul(ps[:, 0:PW], w2sb[r][:, k * 128:(k + 1) * 128], xq2[k][:, 0:PW],
                                       start=(k == 0), stop=(k == MH - 1))
                    sTE.inc(ins)
                    st[f"{a}ymm{pr}_{mo}"] = sTE.n
                    if mo == KD - 1:
                        st[f"{a}ymms{pr}"] = sTE.n

                def yev(e, en, mo=mo, pr=pr, pc0=pc0):
                    ps = psA if mo % 2 == 0 else psB
                    need(e, en, (sTE, st[f"{a}ymm{pr}_{mo}"]), (sVE, st[a + "q2sc"]))
                    need(e, en, (sVE, sVE.n))
                    sVE.inc(e.tensor_tensor(ytmp[:, 0:PW], ps[:, 0:PW], sT4[:, pc0:pc0 + PW], ALU.mult))
                    need(e, en, (sVE, sVE.n))
                    sVE.inc(e.tensor_tensor(xg[mo][:, pc0:pc0 + PW], ytmp[:, 0:PW], xg[mo][:, pc0:pc0 + PW], ALU.add))
                    st[f"{a}yev{pr}_{mo}"] = sVE.n
                on("sync", w2s)
                on("tensor", ymt)
                on("vector", yev)
                if pr + 1 < NPAIR:
                    s0 = sum(_QSPLIT[:mo])
                    emit_quant(pr + 1, range(s0, s0 + _QSPLIT[mo]))

        def fin(e, en):
            for mo in range(KD):
                need(e, en, (sVE, st[f"{a}yev{NPAIR-1}_{mo}"]))
            st[a + "resid"] = max(st[f"{a}yev{pr}_{mo}"] for pr in range(NPAIR) for mo in range(KD))
        on("vector", fin)

    def ffn_sublayer(app, dep=()):
        ffn_stats(app, with_quant=True, dep=dep)
        ffn_quant_apply(app)
        ffn_mm1_conv(app)
        ffn_quant2(app)
        ffn_mm2(app)

    def tap_sb(name, tiles, cols, deps, eng="sync"):
        if name not in tap_t:
            return
        q = qOUT if eng == "sync" else qGP
        def tf(e, en):
            for d in deps:
                need(e, en, d)
            q.begin(e, en)
            for i, t in enumerate(tiles):
                q.inc(e.dma_start(out=tap_t[name][i * 128:(i + 1) * 128, 0:cols], in_=t[:, 0:cols]))
            st["tap_" + name] = q.end()
        on(eng, tf)

    ffn_stats(0, with_quant=True)
    ffn_quant_apply(0)
    qw_scale(1)
    ffn_mm1_conv(0, extra_per_blk=lambda blk: [qw_pass2_step(1, blk * 8 + j) for j in range(8)])
    st["wready1"] = [st[f"wstr1_{NWT-2}"], st[f"wstr1_{NWT-1}"]]
    ffn_quant2(0)
    ffn_mm2(0)
    tap_sb("x1", xg, G, [(sVE, st["f0resid"])])
    tap_sb("xq0", xq, GV, [(sVE, st["f0xqall"])], eng="gpsimd")
    if "w1q" in tap_t:
        def tf_w1(e, en):
            need(e, en, st["wready0"])
            qGP.begin(e, en)
            for m in range(MH):
                for k in range(KD):
                    off = (k * MH + m) * 16384
                    qGP.inc(e.dma_start(out=tap_t["w1q"][(m * KD + k) * 128:(m * KD + k + 1) * 128, :],
                                        in_=w1q_d[off:off + 16384].rearrange("(p q) -> p q", q=128)))
            st["tap_w1q"] = qGP.end()
        on("gpsimd", tf_w1)
    if "h2d" in tap_t:
        def tf_h2(e, en):
            need(e, en, st["f0h2st31"])
            qGP.begin(e, en)
            for m in range(MH):
                qGP.inc(e.dma_start(out=tap_t["h2d"][m * 128:(m + 1) * 128, :], in_=h2_d[m, :, :]))
            st["tap_h2d"] = qGP.end()
        on("gpsimd", tf_h2)

    ffn_stats("ln2", with_quant=False, dep=[(sVE, st["f0resid"])])

    def xn2v(e, en):
        need(e, en, (sSE, st["fln2rstd"]), st.get("tap_xq0", 0))
        for k in range(KD):
            need(e, en, (sVE, sVE.n))
            sVE.inc(e.tensor_sub(sT1[:, 1:1 + SEG], xg[k][:, 1:1 + SEG], sScr[:, 1:1 + SEG]))
            need(e, en, (sVE, sVE.n))
            sVE.inc(e.tensor_tensor(xn2[k][:, 0:SEG], sT1[:, 1:1 + SEG], sT2[:, 1:1 + SEG], ALU.mult))
            st[f"xn2_{k}"] = sVE.n
        st["xn2all"] = sVE.n
    on("vector", xn2v)
    tap_sb("xn2", xn2, SEG, [(sVE, st["xn2all"])], eng="gpsimd")

    def spx(e, en):
        need(e, en, (sVE, st["xn2all"]), st.get("tap_x1", 0), st.get("tap_xn2", 0))
        for k in range(KD):
            qGP.begin(e, en)
            qGP.inc(e.dma_start(out=x1_d[k * 128:(k + 1) * 128, :], in_=xg[k][:, :]))
        st["xspill"] = qGP.end()
    on("gpsimd", spx)

    def awl(e, en):
        for k in range(KD):
            qGP.begin(e, en)
            qGP.inc(e.dma_start(out=wgk1a[k][:, :], in_=wgk1_t[k * 128:(k + 1) * 128, :]))
        qGP.inc(e.dma_start(out=wgk2a[:, :], in_=wgk2_t))
        st["wgkl"] = qGP.end()
        for k in range(KD):
            qGP.begin(e, en)
            qGP.inc(e.dma_start(out=wva[k][:, :], in_=wv_t[k * 128:(k + 1) * 128, :]))
        st["wvl"] = qGP.end()
        need(e, en, st["xspill"])
        for k in range(KD):
            qGP.begin(e, en)
            qGP.inc(e.dma_start(out=wqa[k][:, :], in_=wq_t[k * 128:(k + 1) * 128, :]))
            qGP.inc(e.dma_start(out=wka[k][:, :], in_=wk_t[k * 128:(k + 1) * 128, :]))
        st["wqkl"] = qGP.end()
    on("gpsimd", awl)

    NT2 = SEG // 512
    for ti in range(NT2):
        def ut(e, en, ti=ti):
            need(e, en, st["wgkl"], (sVE, st["xn2all"]))
            need(e, en, (sVE, st.get(f"uev{ti-1}", 0)))
            for k in range(KD):
                ins = e.matmul(psC[0:32, 0:512], wgk1a[k][:, :], xn2[k][:, ti * 512:(ti + 1) * 512],
                               start=(k == 0), stop=(k == KD - 1))
            sTE.inc(ins)
            st[f"umm{ti}"] = sTE.n

        def uv(e, en, ti=ti):
            need(e, en, (sTE, st[f"umm{ti}"]))
            sVE.inc(e.tensor_copy(u_bf[:, ti * 512:(ti + 1) * 512], psC[0:32, 0:512]))
            st[f"uev{ti}"] = sVE.n
        on("tensor", ut)
        on("vector", uv)

    def uf(e, en):
        need(e, en, (sVE, st[f"uev{NT2-1}"]))
        st["u_done"] = sVE.n
    on("vector", uf)

    for half in range(2):
        hw = SEG // 2
        hc0 = half * hw
        NTH = hw // 512 if hw >= 512 else 1
        TW = hw // NTH
        for m in range(KD):
            for ti in range(NTH):
                g = (half * KD + m) * NTH + ti

                def zt(e, en, half=half, hc0=hc0, m=m, ti=ti, g=g):
                    need(e, en, (sVE, st["u_done"]))
                    gp = g - 2
                    if gp >= 0:
                        need(e, en, (sSE, st.get(f"sgk{gp}", 0)))
                    ps = psD if g % 2 == 0 else psE
                    sTE.inc(e.matmul(ps[:, 0:TW if TW <= 512 else 512], wgk2a[:, m * 128:(m + 1) * 128],
                                     u_bf[:, hc0 + ti * TW:hc0 + (ti + 1) * TW], start=True, stop=True))
                    st[f"zmm{g}"] = sTE.n

                def zs(e, en, half=half, hc0=hc0, m=m, ti=ti, g=g):
                    need(e, en, st["xspill"])
                    need(e, en, (sSE, st.get(f"ghalf{half-1}", 0)))
                    need(e, en, (sTE, st[f"zmm{g}"]))
                    ps = psD if g % 2 == 0 else psE
                    sSE.inc(e.activation(sgs[m][:, ti * TW:(ti + 1) * TW], ps[:, 0:TW], AF.Sigmoid, bias=bgk2_sb[:, m:m + 1]))
                    st[f"sgk{g}"] = sSE.n
                on("tensor", zt)
                on("scalar", zs)

        def zl(e, en, half=half, hc0=hc0):
            for m in range(KD):
                need(e, en, (sSE, st[f"sgk{(half * KD + m) * NTH + NTH - 1}"]))
                sSE.inc(e.activation(g16[m][:, hc0:hc0 + hw], sgs[m][:, 0:hw], AF.Ln))
            st[f"ghalf{half}"] = sSE.n
        on("scalar", zl)
    st["g16done"] = st["ghalf1"]

    for j in range(NJ):
        for n in range(4):
            g = j * 4 + n

            def vt(e, en, j=j, n=n, g=g):
                need(e, en, st["wvl"], (sVE, st["xn2all"]))
                gp = g - 2
                if gp >= 0:
                    need(e, en, (sVE, st.get(f"vev{gp}", 0)))
                ps = psF if g % 2 == 0 else psG
                for k in range(KD):
                    ins = e.matmul(ps[:, 0:512], xn2[k][:, j * 128:(j + 1) * 128], wva[k][:, n * 512:(n + 1) * 512],
                                   start=(k == 0), stop=(k == KD - 1))
                sTE.inc(ins)
                st[f"vmm{g}"] = sTE.n

            def vv(e, en, j=j, n=n, g=g):
                need(e, en, (sTE, st[f"vmm{g}"]))
                if n == 0:
                    need(e, en, st.get(f"vst{j-1}", 0))
                ps = psF if g % 2 == 0 else psG
                sVE.inc(e.tensor_copy(vtmp[:, n * 512:(n + 1) * 512], ps[:, 0:512]))
                st[f"vev{g}"] = sVE.n
            on("tensor", vt)
            on("vector", vv)

        def vg(e, en, j=j):
            need(e, en, (sVE, st[f"vev{j * 4 + 3}"]))
            qGP.begin(e, en)
            qGP.inc(e.dma_start(out=v_d[j, :, :], in_=vtmp[:, :]))
            st[f"vst{j}"] = qGP.end()
        on("gpsimd", vg)
    def vfin(e, en):
        st["v_done"] = qGP.fence()
    on("gpsimd", vfin)

    def tz(e, en):
        need(e, en, st["v_done"], st["wready1"])
        sGC.inc(e.memset(T32[:, :], 0.0))
        st["Tzero"] = sGC.n
    on("gpsimd", tz)

    PS_O0, PS_O1, PS_A, PS_TD0, PS_TD1 = psA, psB, psC, psD, psE
    for c in range(OC):
        c0 = c * 256
        def l1v(e, en, c=c, c0=c0):
            need(e, en, (sSE, st["g16done"]), (sGC, st["Tzero"]))
            need(e, en, (sVE, st.get(f"c{c-1}brun", 0)))
            for k in range(KD):
                e.tensor_scalar(ebias[:, k:k + 1], Brun[:, k:k + 1], 1.0 / 32.0, LN16, ALU.mult, ALU.add)
                sVE.inc(e.tensor_scalar(ebias[:, KD + k:KD + k + 1], Brun[:, k:k + 1], -1.0 / 32.0, None, ALU.mult))
            st[f"c{c}ebias"] = sVE.n
            need(e, en, (sSE, st.get(f"c{c-1}exps", 0)))
            for k in range(KD):
                sVE.inc(e.tensor_tensor_scan(csc[:, k * 256:(k + 1) * 256], g16[k][:, c0:c0 + 256],
                                             zz256[:, 0:256], 0.0, ALU.add, ALU.add))
                st[f"c{c}scan{k}"] = sVE.n
            st[f"c{c}scans"] = sVE.n
        on("vector", l1v)

        def l1s(e, en, c=c):
            need(e, en, (sVE, st[f"c{c}scans"]), (sVE, st[f"c{c}ebias"]))
            need(e, en, (sVE, st.get(f"c{c-1}qk", 0)))
            for k in range(KD):
                e.activation(eqc[:, k * 256:(k + 1) * 256], csc[:, k * 256:(k + 1) * 256], AF.Exp,
                             bias=ebias[:, k:k + 1], scale=1.0 / 32.0)
                sSE.inc(e.activation(ekc[:, k * 256:(k + 1) * 256], csc[:, k * 256:(k + 1) * 256], AF.Exp,
                                     bias=ebias[:, KD + k:KD + k + 1], scale=-1.0 / 32.0))
            st[f"c{c}exps"] = sSE.n
        on("scalar", l1s)

        def l1b(e, en, c=c):
            need(e, en, (sSE, st[f"c{c}exps"]))
            for k in range(KD):
                sVE.inc(e.tensor_tensor(Brun[:, k:k + 1], Brun[:, k:k + 1], csc[:, k * 256 + 255:k * 256 + 256], ALU.add))
            st[f"c{c}brun"] = sVE.n
        on("vector", l1b)

        def lvg(e, en, c=c):
            need(e, en, st["v_done"])
            need(e, en, (sVE, st.get(f"c{c-2}oall", 0)))
            qGP.begin(e, en)
            qGP.inc(e.dma_start(out=vtk[c % 2][:, :].rearrange("p (j q) -> p j q", q=DV),
                                in_=v_d[2 * c:2 * c + 2, :, :].rearrange("j p q -> p j q")))
            st[f"c{c}vl"] = qGP.end()
        on("gpsimd", lvg)

        for qk in range(2):
            wt, key, ec = (wqa, "q", eqc) if qk == 0 else (wka, "k", ekc)
            tg = qtc if qk == 0 else ktc
            for m in range(KD):
                g = qk * KD + m

                def lq1(e, en, c=c, c0=c0, m=m, g=g, wt=wt, key=key):
                    if g == 0:
                        need(e, en, st["wqkl"], (sVE, st["xn2all"]))
                        need(e, en, (sVE, st.get(f"vev{NJ * 4 - 1}", 0)))
                    gp = g - 2
                    if gp >= 0:
                        need(e, en, (sVE, st.get(f"c{c}qkev{gp}", 0)))
                    ps = psF if g % 2 == 0 else psG
                    for k in range(KD):
                        ins = e.matmul(ps[:, 0:256], wt[k][:, m * 128:(m + 1) * 128], xn2[k][:, c0:c0 + 256],
                                       start=(k == 0), stop=(k == KD - 1))
                    sTE.inc(ins)
                    st[f"c{c}qkmm{g}"] = sTE.n

                def lq2(e, en, c=c, m=m, g=g, ec=ec, tg=tg):
                    if g == 0:
                        need(e, en, (sSE, st[f"c{c}exps"]))
                        need(e, en, (sTE, st.get(f"c{c-1}omm{H-1}", 0)), st.get(f"c{c-1}qsp", 0),
                             (sTE, st.get(f"c{c-1}tdmm{H-1}", 0)))
                    need(e, en, (sTE, st[f"c{c}qkmm{g}"]))
                    ps = psF if g % 2 == 0 else psG
                    sVE.inc(e.tensor_tensor(tg[:, m * 256:(m + 1) * 256], ps[:, 0:256], ec[:, m * 256:(m + 1) * 256], ALU.mult))
                    st[f"c{c}qkev{g}"] = sVE.n
                on("tensor", lq1)
                on("vector", lq2)

        def lqf(e, en, c=c):
            need(e, en, (sVE, st[f"c{c}qkev{2 * KD - 1}"]))
            st[f"c{c}qk"] = sVE.n
            st[f"c{c}qev{KD-1}"] = sVE.n
        on("vector", lqf)

        def lqg(e, en, c=c, c0=c0):
            need(e, en, (sVE, st[f"c{c}qev{KD-1}"]))
            qGP.begin(e, en)
            qGP.inc(e.dma_start(out=qt_d[:, :, c0:c0 + 256].rearrange("k p q -> p k q"),
                                in_=qtc[:, :].rearrange("p (k q) -> p k q", q=256)))
            st[f"c{c}qsp"] = qGP.end()
        on("gpsimd", lqg)

        for jt in range(2):
            def ltt(e, en, c=c, jt=jt):
                need(e, en, (sVE, st[f"c{c}qk"]))
                need(e, en, (sVE, st.get(f"c{c}ktokev{jt-1}", 0)))
                if jt == 0:
                    need(e, en, (sVE, st.get(f"c{c-1}ktokev1", 0)))
                for m in range(KD):
                    ins = e.transpose(psT[:, m * 128:(m + 1) * 128], ktc[:, m * 256 + jt * 128:m * 256 + jt * 128 + 128], ident[:, :])
                sTE.inc(ins)
                st[f"c{c}ktok{jt}"] = sTE.n

            def ltv(e, en, c=c, jt=jt):
                need(e, en, (sTE, st[f"c{c}ktok{jt}"]))
                need(e, en, (sTE, st.get(f"c{c-1}tdmm{H-1}", 0)))
                sVE.inc(e.tensor_copy(ktok[:, jt * 1024:(jt + 1) * 1024], psT[:, 0:1024]))
                st[f"c{c}ktokev{jt}"] = sVE.n
            on("tensor", ltt)
            on("vector", ltv)

        def lt16(e, en, c=c):
            need(e, en, (sVE, st.get(f"c{c-1}tupd", 0)), (sTE, st.get(f"c{c-1}omm{H-1}", 0)))
            need(e, en, (sGC, st["Tzero"]), st["wready1"])
            sVE.inc(e.tensor_copy(T16[:, :], T32[:, :]))
            st[f"c{c}t16"] = sVE.n
        on("vector", lt16)

        for h in range(H):
            def lat(e, en, c=c, h=h):
                need(e, en, (sVE, st[f"c{c}qk"]))
                need(e, en, (sVE, st.get(f"c{c}aev{h-1}", 0)))
                first = True
                for bi, (jt, tt_) in enumerate(((0, 0), (1, 1), (0, 1))):
                    for kk in range(2):
                        ch = 2 * h + kk
                        ins = e.matmul(PS_A[:, bi * 128:(bi + 1) * 128],
                                       ktc[:, ch * 256 + jt * 128:ch * 256 + jt * 128 + 128],
                                       qtc[:, ch * 256 + tt_ * 128:ch * 256 + tt_ * 128 + 128],
                                       start=first, stop=(bi == 2 and kk == 1))
                        first = False
                sTE.inc(ins)
                st[f"c{c}amm{h}"] = sTE.n
            on("tensor", lat)

            def lav(e, en, c=c, h=h):
                need(e, en, (sTE, st[f"c{c}amm{h}"]), (sGC, st["consts_built"]))
                need(e, en, (sTE, st.get(f"c{c}omm{h-1}", 0)))
                e.tensor_tensor(Abuf[:, 0:128], PS_A[:, 0:128], maskJT[:, :], ALU.mult)
                e.tensor_tensor(Abuf[:, 128:256], PS_A[:, 128:256], maskJT[:, :], ALU.mult)
                sVE.inc(e.tensor_copy(Abuf[:, 256:384], PS_A[:, 256:384]))
                st[f"c{c}aev{h}"] = sVE.n
            on("vector", lav)

            def lot(e, en, c=c, h=h):
                need(e, en, (sVE, st[f"c{c}aev{h}"]), st[f"c{c}vl"], (sVE, st[f"c{c}t16"]))
                need(e, en, (sVE, st.get(f"c{c}oev{h-1}", 0)))
                vb = vtk[c % 2]
                first = True
                for cv in range(4):
                    e.matmul(PS_O0[:, cv * 128:(cv + 1) * 128], vb[:, 0 * DV + h * 512 + cv * 128:0 * DV + h * 512 + (cv + 1) * 128],
                             Abuf[:, 0:128], start=first, stop=False)
                    first = False
                for kk in range(2):
                    ch = 2 * h + kk
                    for cv in range(4):
                        e.matmul(PS_O0[:, cv * 128:(cv + 1) * 128],
                                 T16[:, ch * 512 + cv * 128:ch * 512 + (cv + 1) * 128],
                                 qtc[:, ch * 256:ch * 256 + 128], start=False, stop=(kk == 1 and cv == 3))
                first = True
                for cv in range(4):
                    e.matmul(PS_O1[:, cv * 128:(cv + 1) * 128], vb[:, 1 * DV + h * 512 + cv * 128:1 * DV + h * 512 + (cv + 1) * 128],
                             Abuf[:, 128:256], start=first, stop=False)
                    first = False
                for cv in range(4):
                    e.matmul(PS_O1[:, cv * 128:(cv + 1) * 128], vb[:, 0 * DV + h * 512 + cv * 128:0 * DV + h * 512 + (cv + 1) * 128],
                             Abuf[:, 256:384], start=False, stop=False)
                last = None
                for kk in range(2):
                    ch = 2 * h + kk
                    for cv in range(4):
                        last = e.matmul(PS_O1[:, cv * 128:(cv + 1) * 128],
                                        T16[:, ch * 512 + cv * 128:ch * 512 + (cv + 1) * 128],
                                        qtc[:, ch * 256 + 128:ch * 256 + 256], start=False, stop=(kk == 1 and cv == 3))
                sTE.inc(last)
                st[f"c{c}omm{h}"] = sTE.n
                need(e, en, (sVE, st[f"c{c}ktokev1"]))
                need(e, en, (sVE, st.get(f"c{c}tup{h-1}", 0)))
                for kk in range(2):
                    ch = 2 * h + kk
                    ps = PS_TD0 if kk == 0 else PS_TD1
                    for jt in range(2):
                        ins = e.matmul(ps[:, 0:512], ktok[:, jt * 1024 + ch * 128:jt * 1024 + (ch + 1) * 128],
                                       vb[:, jt * DV + h * 512:jt * DV + (h + 1) * 512], start=(jt == 0), stop=(jt == 1))
                    sTE.inc(ins)
                st[f"c{c}tdmm{h}"] = sTE.n
            on("tensor", lot)

            def lov(e, en, c=c, h=h):
                need(e, en, (sTE, st[f"c{c}omm{h}"]))
                need(e, en, st.get(f"c{c-1}osp", 0))
                for tau in range(2):
                    ps = PS_O0 if tau == 0 else PS_O1
                    dst = och[:, h * 1024:(h + 1) * 1024].rearrange("p (cv t) -> p cv t", t=256)[:, :, tau * 128:tau * 128 + 128]
                    src = ps[:, 0:512].rearrange("p (cv t) -> p cv t", t=128)
                    sVE.inc(e.tensor_copy(dst, src))
                st[f"c{c}oev{h}"] = sVE.n
                need(e, en, (sTE, st[f"c{c}tdmm{h}"]))
                for kk in range(2):
                    ch = 2 * h + kk
                    ps = PS_TD0 if kk == 0 else PS_TD1
                    sVE.inc(e.tensor_tensor(T32[:, ch * 512:(ch + 1) * 512], ps[:, 0:512], T32[:, ch * 512:(ch + 1) * 512], ALU.add))
                st[f"c{c}tup{h}"] = sVE.n
            on("vector", lov)
        st[f"c{c}tupd"] = st[f"c{c}tup{H-1}"]

        def log_(e, en, c=c, c0=c0):
            need(e, en, (sVE, st[f"c{c}oev{H-1}"]))
            qGP.begin(e, en)
            qGP.inc(e.dma_start(out=o_d[:, :, c0:c0 + 256].rearrange("cv p t -> p cv t"),
                                in_=och[:, :].rearrange("p (cv t) -> p cv t", t=256)))
            st[f"c{c}osp"] = qGP.end()
        on("gpsimd", log_)

        def lfin(e, en, c=c):
            need(e, en, (sVE, st[f"c{c}tupd"]), (sVE, st[f"c{c}oev{H-1}"]))
            st[f"c{c}oall"] = sVE.n
        on("vector", lfin)

    def pcs(e, en):
        need(e, en, (sVE, st[f"c{OC-1}tupd"]), (sVE, st[f"c{OC-1}brun"]))
        for h in range(H):
            for kk in range(2):
                ch = 2 * h + kk
                off = (h * 257 + kk * 128) * 512
                qGP.begin(e, en)
                qGP.inc(e.dma_start(out=cc_s[off:off + 128 * 512].rearrange("(p q) -> p q", q=512),
                                    in_=T32[:, ch * 512:(ch + 1) * 512]))
            offl = (h * 257 + 256) * 512
            for kk in range(2):
                ch = 2 * h + kk
                qGP.inc(e.dma_start(out=cc_s[offl + kk * 128:offl + kk * 128 + 128].rearrange("(p o) -> p o", o=1),
                                    in_=Brun[:, ch:ch + 1]))
            qGP.inc(e.dma_start(out=cc_s[offl + 256:offl + 512].rearrange("(p o) -> p o", o=2),
                                in_=zzf2[:, 0:2]))
        st["ccs"] = qGP.end()
        need(e, en, st["ccs"])
        sCC.inc(e.collective_compute(
            "AllGather", ALU.bypass,
            replica_groups=[[0, 1, 2, 3], [4, 5, 6, 7]],
            ins=[cc_s], outs=[cc_r]))
        st["ag1"] = sCC.n
    on("gpsimd", pcs)

    def gwl(e, en):
        need(e, en, (sVE, st[f"c{OC-1}oall"]))
        need(e, en, st["ccs"])
        for k in range(KD):
            qGP.begin(e, en)
            qGP.inc(e.dma_start(out=wga[k][:, :], in_=wg_t[k * 128:(k + 1) * 128, :]))
        st["wgl"] = qGP.end()
    on("gpsimd", gwl)

    def xrl(e, en):
        need(e, en, st["ccs"])
        need(e, en, st["wqkl"])
        for k in range(KD):
            qIN.begin(e, en)
            qIN.inc(e.dma_start(out=xg[k][:, :], in_=x1_d[k * 128:(k + 1) * 128, :]))
        st["xreload"] = qIN.end()
    on("sync", xrl)

    for mg in range(MV):
        for ti in range(NT2):
            g = mg * NT2 + ti

            def gt(e, en, mg=mg, ti=ti, g=g):
                need(e, en, st["wgl"], (sVE, st["xn2all"]))
                gp = g - 2
                if gp >= 0:
                    need(e, en, (sSE, st.get(f"gse{gp}", 0)))
                ps = psF if g % 2 == 0 else psG
                for k in range(KD):
                    ins = e.matmul(ps[:, 0:512], wga[k][:, mg * 128:(mg + 1) * 128], xn2[k][:, ti * 512:(ti + 1) * 512],
                                   start=(k == 0), stop=(k == KD - 1))
                sTE.inc(ins)
                st[f"gmm{g}"] = sTE.n

            def gs(e, en, mg=mg, ti=ti, g=g):
                need(e, en, (sTE, st[f"gmm{g}"]))
                if ti == 0:
                    need(e, en, st.get(f"gst{mg-1}", 0))
                ps = psF if g % 2 == 0 else psG
                sSE.inc(e.activation(h2t[:, ti * 512:(ti + 1) * 512], ps[:, 0:512], AF.Silu))
                st[f"gse{g}"] = sSE.n
            on("tensor", gt)
            on("scalar", gs)

        def gg(e, en, mg=mg):
            need(e, en, (sSE, st[f"gse{mg * NT2 + NT2 - 1}"]))
            qGP.begin(e, en)
            qGP.inc(e.dma_start(out=gate_d[mg, :, :], in_=h2t[:, 0:SEG]))
            st[f"gst{mg}"] = qGP.end()
        on("gpsimd", gg)
    def gfin(e, en):
        st["gate_done"] = qGP.fence()
    on("gpsimd", gfin)

    def cmb_g(e, en):
        need(e, en, (sCC, st["ag1"]), st["gate_done"])
        for j in range(4):
            for ch in range(KD):
                h, kk = ch // 2, ch % 2
                off = j * H * 257 * 512 + (h * 257 + 256) * 512 + kk * 128
                qGP.begin(e, en)
                qGP.inc(e.dma_start(out=Lg[:, j * KD + ch:j * KD + ch + 1],
                                    in_=cc_r[off:off + 128].rearrange("(p o) -> p o", o=1)))
        st["Lgl"] = qGP.end()
        sGC.inc(e.memset(Sin[:, :], 0.0))
        st["sin0"] = sGC.n
    on("gpsimd", cmb_g)

    for j in range(4):
        def cj_v(e, en, j=j):
            need(e, en, st["Lgl"], st["consts_loaded"])
            need(e, en, (sSE, st.get(f"ec{j-1}", 0)))
            for ch in range(KD):
                for l in range(4):
                    if l == 0:
                        ins = e.tensor_scalar(eacc[:, ch:ch + 1], Lg[:, l * KD + ch:l * KD + ch + 1], ctl_sb[:, 4 * j + l:4 * j + l + 1], None, ALU.mult)
                    else:
                        need(e, en, (sVE, sVE.n))
                        ins = e.scalar_tensor_tensor(eacc[:, ch:ch + 1], Lg[:, l * KD + ch:l * KD + ch + 1], ctl_sb[:, 4 * j + l:4 * j + l + 1], eacc[:, ch:ch + 1], ALU.mult, ALU.add)
                    sVE.inc(ins)
            st[f"eacc{j}"] = sVE.n
        on("vector", cj_v)

        def cj_s(e, en, j=j):
            need(e, en, (sVE, st[f"eacc{j}"]))
            sSE.inc(e.activation(ecol[:, :], eacc[:, :], AF.Exp, bias=ctl_sb[:, 16 + j:17 + j], scale=1.0 / 32.0))
            st[f"ec{j}"] = sSE.n
        on("scalar", cj_s)

        for ch in range(KD):
            def cj_g(e, en, j=j, ch=ch):
                need(e, en, (sCC, st["ag1"]))
                h, kk = ch // 2, ch % 2
                prev = st.get(f"sacc{j}_{ch-1}", st.get(f"sacc{j-1}_{KD-1}", 0) if ch == 0 else 0)
                need(e, en, (sVE, prev))
                off = j * H * 257 * 512 + (h * 257 + kk * 128) * 512
                r = ch % 2
                qGP.begin(e, en)
                qGP.inc(e.dma_start(out=sjb[:, 0:512] if r == 0 else sjb[:, 0:512],
                                    in_=cc_r[off:off + 128 * 512].rearrange("(p q) -> p q", q=512)))
                st[f"sjl{j}_{ch}"] = qGP.end()
            def cj_a(e, en, j=j, ch=ch):
                need(e, en, st[f"sjl{j}_{ch}"], (sSE, st[f"ec{j}"]), (sGC, st["sin0"]))
                need(e, en, (sVE, st.get(f"sacc{j-1}_{ch}", 0)))
                sVE.inc(e.scalar_tensor_tensor(Sin[:, ch * 512:(ch + 1) * 512], sjb[:, 0:512], ecol[:, ch:ch + 1],
                                               Sin[:, ch * 512:(ch + 1) * 512], ALU.mult, ALU.add))
                st[f"sacc{j}_{ch}"] = sVE.n
            on("gpsimd", cj_g)
            on("vector", cj_a)

    def s16v(e, en):
        for ch in range(KD):
            need(e, en, (sVE, st[f"sacc3_{ch}"]))
        sVE.inc(e.tensor_copy(S16b[:, :], Sin[:, :]))
        st["s16"] = sVE.n
    on("vector", s16v)

    def wol(e, en):
        need(e, en, st["gate_done"])
        for cv in range(MV):
            qGP.begin(e, en)
            qGP.inc(e.dma_start(out=woa[cv][:, :], in_=wout_t[cv * 128:(cv + 1) * 128, :]))
        st["wol"] = qGP.end()
    on("gpsimd", wol)

    for c in range(OC):
        c0 = c * 256
        def pb_g(e, en, c=c, c0=c0):
            need(e, en, (sVE, st["s16"]), st[f"c{OC-1}osp"], st[f"c{OC-1}qsp"])
            need(e, en, (sVE, st.get(f"pb{c-1}done", 0)))
            qGP.begin(e, en)
            qGP.inc(e.dma_start(out=och[:, :].rearrange("p (cv t) -> p cv t", t=256),
                                in_=o_d[:, :, c0:c0 + 256].rearrange("cv p t -> p cv t")))
            qGP.inc(e.dma_start(out=eqc[:, :].rearrange("p (k q) -> p k q", q=256),
                                in_=qt_d[:, :, c0:c0 + 256].rearrange("k p q -> p k q")))
            qGP.inc(e.dma_start(out=gch[:, :].rearrange("p (cv t) -> p cv t", t=256),
                                in_=gate_d[:, :, c0:c0 + 256].rearrange("cv p t -> p cv t")))
            st[f"pb{c}in"] = qGP.end()
        on("gpsimd", pb_g)

        for cv in range(MV):
            def pb_t1(e, en, c=c, cv=cv):
                need(e, en, st[f"pb{c}in"], (sVE, st["s16"]))
                need(e, en, (sVE, st.get(f"pb{c}o1_{cv-2}", 0)))
                h = cv // 4
                ps = psF if cv % 2 == 0 else psG
                for kk in range(2):
                    ch = 2 * h + kk
                    ins = e.matmul(ps[:, 0:256], S16b[:, ch * 512 + (cv % 4) * 128:ch * 512 + (cv % 4) * 128 + 128],
                                   eqc[:, ch * 256:(ch + 1) * 256], start=(kk == 0), stop=(kk == 1))
                sTE.inc(ins)
                st[f"pb{c}corr{cv}"] = sTE.n

            def pb_v1(e, en, c=c, cv=cv):
                ps = psF if cv % 2 == 0 else psG
                need(e, en, (sTE, st[f"pb{c}corr{cv}"]))
                sVE.inc(e.tensor_tensor(och[:, cv * 256:(cv + 1) * 256], ps[:, 0:256], och[:, cv * 256:(cv + 1) * 256], ALU.add))
                st[f"pb{c}o1_{cv}"] = sVE.n
            on("tensor", pb_t1)
            on("vector", pb_v1)

        for cv in range(MV):
            h = cv // 4

            def pb_s1(e, en, c=c, cv=cv):
                need(e, en, (sVE, st[f"pb{c}o1_{cv}"]))
                need(e, en, (sTE, st.get(f"pb{c}msmm{cv-2}", 0)))
                sSE.inc(e.activation(o2scr[:, (cv % 2) * 256:(cv % 2) * 256 + 256],
                                     och[:, cv * 256:(cv + 1) * 256], AF.Square))
                st[f"pb{c}sq{cv}"] = sSE.n

            def pb_t2(e, en, c=c, cv=cv, h=h):
                need(e, en, (sSE, st[f"pb{c}sq{cv}"]))
                if cv % 4 == 0:
                    need(e, en, (sVE, st.get(f"pb{c}msv{h-1}", 0)))
                ins = e.matmul(psC[:, 0:256], ones_b[:, :], o2scr[:, (cv % 2) * 256:(cv % 2) * 256 + 256],
                               start=(cv % 4 == 0), stop=(cv % 4 == 3))
                sTE.inc(ins)
                st[f"pb{c}msmm{cv}"] = sTE.n
                if cv % 4 == 3:
                    st[f"pb{c}ms{h}"] = sTE.n
            on("scalar", pb_s1)
            on("tensor", pb_t2)

            if cv % 4 == 3:
                def pb_v2(e, en, c=c, h=h):
                    need(e, en, (sTE, st[f"pb{c}ms{h}"]))
                    sVE.inc(e.tensor_scalar(msb_t[:, h * 256:(h + 1) * 256], psC[:, 0:256], 1.0 / HV, float(EPS), ALU.mult, ALU.add))
                    st[f"pb{c}msv{h}"] = sVE.n
                on("vector", pb_v2)

        def pb_s2(e, en, c=c):
            ins = None
            for h in range(H):
                need(e, en, (sVE, st[f"pb{c}msv{h}"]))
                ins = e.activation(msb_t[:, h * 256:(h + 1) * 256], msb_t[:, h * 256:(h + 1) * 256], AF.Ln)
            sSE.inc(ins)
            need(e, en, (sSE, sSE.n))
            for h in range(H):
                ins = e.activation(msb_t[:, h * 256:(h + 1) * 256], msb_t[:, h * 256:(h + 1) * 256], AF.Exp, scale=-0.5)
            sSE.inc(ins)
            st[f"pb{c}rstd"] = sSE.n
        on("scalar", pb_s2)

        def pb_v3(e, en, c=c):
            need(e, en, (sSE, st[f"pb{c}rstd"]), st[f"pb{c}in"])
            for cv in range(MV):
                h = cv // 4
                need(e, en, (sVE, sVE.n))
                sVE.inc(e.scalar_tensor_tensor(och[:, cv * 256:(cv + 1) * 256], och[:, cv * 256:(cv + 1) * 256],
                                               rmsw_sb[:, cv % 4:cv % 4 + 1], msb_t[:, h * 256:(h + 1) * 256], ALU.mult, ALU.mult))
            need(e, en, (sVE, sVE.n))
            for cv in range(MV):
                need(e, en, (sVE, sVE.n))
                sVE.inc(e.tensor_tensor(och[:, cv * 256:(cv + 1) * 256], och[:, cv * 256:(cv + 1) * 256],
                                        gch[:, cv * 256:(cv + 1) * 256], ALU.mult))
            st[f"pb{c}of"] = sVE.n
        on("vector", pb_v3)

        for mo in range(KD):
            def pb_t3(e, en, c=c, mo=mo):
                need(e, en, (sVE, st[f"pb{c}of"]), st["wol"], st["xreload"])
                need(e, en, (sVE, st.get(f"pb{c}xr{mo-2}", 0)))
                ps = psD if mo % 2 == 0 else psE
                for cv in range(MV):
                    ins = e.matmul(ps[:, 0:256], woa[cv][:, mo * 128:(mo + 1) * 128], och[:, cv * 256:(cv + 1) * 256],
                                   start=(cv == 0), stop=(cv == MV - 1))
                sTE.inc(ins)
                st[f"pb{c}wo{mo}"] = sTE.n

            def pb_v4(e, en, c=c, c0=c0, mo=mo):
                ps = psD if mo % 2 == 0 else psE
                need(e, en, (sTE, st[f"pb{c}wo{mo}"]))
                sVE.inc(e.tensor_tensor(xg[mo][:, 1 + c0:1 + c0 + 256], ps[:, 0:256], xg[mo][:, 1 + c0:1 + c0 + 256], ALU.add))
                st[f"pb{c}xr{mo}"] = sVE.n
            on("tensor", pb_t3)
            on("vector", pb_v4)

        def pb_fin(e, en, c=c):
            need(e, en, (sVE, st[f"pb{c}xr{KD-1}"]))
            st[f"pb{c}done"] = sVE.n
        on("vector", pb_fin)
    st["x2done"] = st[f"pb{OC-1}done"]
    tap_sb("x2", xg, G, [(sVE, st["x2done"])])

    def c2g(e, en):
        need(e, en, (sVE, st["pb0done"]), (sVE, st["x2done"]))
        for k in range(KD):
            qGP.begin(e, en)
            qGP.inc(e.dma_start(out=cc2_s[k * 128:(k + 1) * 128].rearrange("(p o) -> p o", o=1), in_=xg[k][:, 1:2]))
            qGP.inc(e.dma_start(out=cc2_s[D + k * 128:D + (k + 1) * 128].rearrange("(p o) -> p o", o=1), in_=xg[k][:, SEG:SEG + 1]))
        st["cc2s"] = qGP.end()
        need(e, en, st["cc2s"])
        sCC.inc(e.collective_compute(
            "AllGather", ALU.bypass,
            replica_groups=[[0, 1, 2, 3], [4, 5, 6, 7]],
            ins=[cc2_s], outs=[cc2_r]))
        st["ag2"] = sCC.n
        need(e, en, (sCC, st["ag2"]))
        for k in range(KD):
            qGP.begin(e, en)
            qGP.inc(e.dma_start(out=rows8[:, k * 8:(k + 1) * 8].rearrange("p (j f) -> p j f", f=2),
                                in_=cc2_r.rearrange("(j f p) -> p j f", f=2, p=D)[k * 128:(k + 1) * 128, :, :],
                                allow_slow_non_contiguous=True))
        st["rows8l"] = qGP.end()
    on("gpsimd", c2g)

    def c2v(e, en):
        need(e, en, st["rows8l"], st["consts_loaded"], st.get("tap_x2", 0))
        for k in range(KD):
            for j in range(4):
                if j == 0:
                    e.tensor_scalar(xg[k][:, 0:1], rows8[:, k * 8 + j * 2 + 1:k * 8 + j * 2 + 2], ctl_sb[:, 20 + j:21 + j], None, ALU.mult)
                    ins = e.tensor_scalar(xg[k][:, G - 3:G - 2], rows8[:, k * 8 + j * 2:k * 8 + j * 2 + 1], ctl_sb[:, 24 + j:25 + j], None, ALU.mult)
                else:
                    e.scalar_tensor_tensor(xg[k][:, 0:1], rows8[:, k * 8 + j * 2 + 1:k * 8 + j * 2 + 2], ctl_sb[:, 20 + j:21 + j], xg[k][:, 0:1], ALU.mult, ALU.add)
                    ins = e.scalar_tensor_tensor(xg[k][:, G - 3:G - 2], rows8[:, k * 8 + j * 2:k * 8 + j * 2 + 1], ctl_sb[:, 24 + j:25 + j], xg[k][:, G - 3:G - 2], ALU.mult, ALU.add)
                sVE.inc(ins)
                need(e, en, (sVE, sVE.n))
        st["halo_fixed_v"] = sVE.n
    on("vector", c2v)
    st["halo_fixed"] = st["halo_fixed_v"]

    ffn_sublayer(1, dep=[(sVE, st["x2done"]), (sVE, st["halo_fixed"])])

    def out_s(e, en):
        need(e, en, (sVE, st["f1resid"]))
        for k in range(KD):
            qOUT.begin(e, en)
            qOUT.inc(e.dma_start(out=y_out[k * 128:(k + 1) * 128, :], in_=xg[k][:, 1:1 + SEG]))
        st["yout"] = qOUT.end()
        qOUT.drain(e, en)
        qGP.drain(e, en)
        qIN.drain(e, en)
    on("sync", out_s)

    if merge:
        for ename in ("sync", "gpsimd", "tensor", "vector", "scalar"):
            def run_all(e, fs=tuple(equeue[ename]), en=ename):
                for f in fs:
                    f(e, en)
            getattr(block, ename)(run_all)
    block_cm.__exit__(None, None, None)
    return nc, st


def _prep_core_inputs(core, SEG, x_full, w):
    b, s = core // 4, core % 4
    N = 4 * SEG
    t0 = s * SEG
    G = SEG + 4
    xg = np.zeros((G, 1024), np.float32)
    lo, hi = t0 - 1, t0 + SEG + 1
    slo, shi = max(lo, 0), min(hi, N)
    xg[slo - lo:shi - lo, :] = x_full[b, slo:shi, :]
    d = {
        "x": np.ascontiguousarray(xg.T),
        "wq_t": np.ascontiguousarray(w["Wq"].T),
        "wk_t": np.ascontiguousarray(w["Wk"].T),
        "wv_t": np.ascontiguousarray(w["Wv"].T),
        "wg_t": np.ascontiguousarray(w["Wg"].T),
        "wgk1_t": np.ascontiguousarray(w["Wgk1"].T),
        "wgk2_t": np.ascontiguousarray(w["Wgk2"].T),
        "bgk2": np.ascontiguousarray(w["bgk2"]),
        "wout_t": np.ascontiguousarray(w["Wout"].T),
        "w1_t": np.ascontiguousarray(w["w1"].T),
        "w2_t": np.ascontiguousarray(w["w2"].T),
        "conv_w": np.ascontiguousarray(w["conv_w"][:, 0, :].T.reshape(3, -1) if w["conv_w"].ndim == 3 else w["conv_w"]),
        "conv_b": np.ascontiguousarray(w["conv_b"]),
        "rms_w": np.ascontiguousarray(w["rms_w"]),
    }
    cw = w["conv_w"].reshape(4096, 3)
    d["conv_w"] = np.ascontiguousarray(cw.T)
    ctl = np.zeros(32, np.float32)
    g = s
    for j in range(4):
        for l in range(4):
            ctl[4 * j + l] = 1.0 if (j <= l < g) else 0.0
    for j in range(4):
        ctl[16 + j] = 0.0 if j < g else NEG
    for j in range(4):
        ctl[20 + j] = 1.0 if j == g - 1 else 0.0
        ctl[24 + j] = 1.0 if j == g + 1 else 0.0
    d["ctl"] = np.tile(ctl[None, :], (128, 1)).astype(np.float32)
    return d


def kernel(L0_attn_x, L0_attn_Wq, L0_attn_Wk, L0_attn_Wv, L0_attn_Wg,
           L0_attn_Wgk1, L0_attn_Wgk2, L0_attn_bgk2, L0_attn_Wout, L0_attn_rms_w,
           L0_ffn_w1, L0_ffn_conv_w, L0_ffn_conv_b, L0_ffn_w2):
    SEG = 2048
    x_full = np.asarray(L0_attn_x, np.float32)
    w = {
        "Wq": np.asarray(L0_attn_Wq, np.float32), "Wk": np.asarray(L0_attn_Wk, np.float32),
        "Wv": np.asarray(L0_attn_Wv, np.float32), "Wg": np.asarray(L0_attn_Wg, np.float32),
        "Wgk1": np.asarray(L0_attn_Wgk1, np.float32), "Wgk2": np.asarray(L0_attn_Wgk2, np.float32),
        "bgk2": np.asarray(L0_attn_bgk2, np.float32), "Wout": np.asarray(L0_attn_Wout, np.float32),
        "rms_w": np.asarray(L0_attn_rms_w, np.float32),
        "w1": np.asarray(L0_ffn_w1, np.float32), "conv_w": np.asarray(L0_ffn_conv_w, np.float32),
        "conv_b": np.asarray(L0_ffn_conv_b, np.float32), "w2": np.asarray(L0_ffn_w2, np.float32),
    }
    nc = build_program(SEG=SEG)
    in_maps = [_prep_core_inputs(c, SEG, x_full, w) for c in range(8)]
    res = run_bass_kernel_spmd(nc, in_maps, core_ids=list(range(8)))
    out = np.empty((2, 8192, 1024), np.float32)
    for c in range(8):
        b, s = c // 4, c % 4
        out[b, s * SEG:(s + 1) * SEG, :] = res.results[c]["y"].T
    return out


# version 12 (final)
# speedup vs baseline: 1.0146x; 1.0146x over previous
import numpy as np
import concourse.bass as bass
import concourse.mybir as mybir
from concourse.bass_utils import run_bass_kernel_spmd

F32 = mybir.dt.float32
BF16 = mybir.dt.bfloat16
F16 = mybir.dt.float16
I8 = mybir.dt.int8
AF = mybir.ActivationFunctionType
ALU = mybir.AluOpType
AXX = mybir.AxisListType.X

D, DH, DK, DV, H, HQK, HV = 1024, 4096, 1024, 2048, 4, 256, 512
KD, MH, MV = D // 128, DH // 128, DV // 128
LN16 = float(np.log(1.0 / 16.0))
EPS = 1e-5
NEG = -1.0e20
MAGIC = 12582912.0
SZ = {F32: 4, BF16: 2, F16: 2, I8: 1}


def ceil_div(a, b):
    return (a + b - 1) // b


def rnd64(x):
    return (x + 63) // 64 * 64


def build_program(SEG=2048, taps=()):
    _, st1 = _build(SEG, taps, None, False)
    nc, _ = _build(SEG, taps, st1, True)
    return nc


def _build(SEG=2048, taps=(), st_seed=None, merge=False):
    OC = SEG // 256
    NJ = SEG // 128
    G = SEG + 4
    GV = SEG + 2
    NT = ceil_div(GV, 512)
    TT = [(i * 512, min(512, GV - i * 512)) for i in range(NT)]
    NTT = max(1, SEG // 512)
    TK = [(1 + i * 512, min(512, SEG - i * 512)) for i in range(NTT)]
    NS = ceil_div(GV, 128)
    NSS = SEG // 128
    NPAIR = max(1, SEG // 512)
    PW = SEG // NPAIR

    nc = bass.Bass("TRN2", target_bir_lowering=False)

    def din(name, shape):
        return nc.dram_tensor(name, list(shape), F32, kind="ExternalInput").ap()

    x_in = din("x", [D, G])
    wq_t, wk_t = din("wq_t", [D, DK]), din("wk_t", [D, DK])
    wv_t, wg_t = din("wv_t", [D, DV]), din("wg_t", [D, DV])
    wgk1_t, wgk2_t = din("wgk1_t", [D, 32]), din("wgk2_t", [32, DK])
    bgk2_i = din("bgk2", [DK])
    wout_t = din("wout_t", [DV, D])
    w1_t, w2_t = din("w1_t", [D, DH]), din("w2_t", [DH, D])
    conv_w_i, conv_b_i = din("conv_w", [3, DH]), din("conv_b", [DH])
    rms_w_i = din("rms_w", [HV])
    ctl_i = din("ctl", [128, 32])

    y_out = nc.dram_tensor("y", [D, SEG], F32, kind="ExternalOutput").ap()
    tap_t = {}
    for name, shape in taps:
        tap_t[name] = nc.dram_tensor("tap_" + name, list(shape), F32, kind="ExternalOutput").ap()

    w1q_d = nc.dram_tensor("w1q_d", [MH * KD * 128 * 128], BF16).ap()
    w2q_d = nc.dram_tensor("w2q_d", [KD * MH * 128 * 128], BF16).ap()
    h2_d = nc.dram_tensor("h2_d", [MH, 128, SEG], BF16).ap()
    o_d = nc.dram_tensor("o_d", [MV, 128, SEG], BF16).ap()
    qt_d = nc.dram_tensor("qt_d", [KD, 128, SEG], BF16).ap()
    v_d = nc.dram_tensor("v_d", [NJ, 128, DV], BF16).ap()
    gate_d = nc.dram_tensor("gate_d", [MV, 128, SEG], BF16).ap()
    x1_d = nc.dram_tensor("x1_d", [D, G], F32).ap()
    cc_s = nc.dram_tensor("cc_s", [H * 257 * 512], F32).ap()
    cc_r = nc.dram_tensor("cc_r", [4 * H * 257 * 512], F32).ap()
    cc2_s = nc.dram_tensor("cc2_s", [2 * D], F32).ap()
    cc2_r = nc.dram_tensor("cc2_r", [4 * 2 * D], F32).ap()

    psA = nc.alloc_psum_tensor("psA", [128, 512], F32)
    psB = nc.alloc_psum_tensor("psB", [128, 512], F32)
    psC = nc.alloc_psum_tensor("psC", [128, 512], F32)
    psD = nc.alloc_psum_tensor("psD", [128, 512], F32)
    psE = nc.alloc_psum_tensor("psE", [128, 512], F32)
    psF = nc.alloc_psum_tensor("psF", [128, 512], F32)
    psG = nc.alloc_psum_tensor("psG", [128, 512], F32)
    psT = nc.alloc_psum_tensor("psT", [128, 1024], BF16)

    SB = nc.alloc_sbuf_tensor
    ident = SB("ident", [128, 128], BF16)
    identf = SB("identf", [128, 128], F32)
    maskJT = SB("maskJT", [128, 128], BF16)
    ones_b = SB("ones_b", [128, 128], BF16)
    onesf = SB("onesf", [128, 128], F32)
    ones1f = SB("ones1f", [1, 128], F32)
    ctl_sb = SB("ctl_sb", [128, 32], F32)
    convb_sb = SB("convb_sb", [128, MH], F32)
    convw_sb = SB("convw_sb", [128, 3 * MH], F32)
    convw_s = SB("convw_s", [128, 3 * MH], F32)
    rmsw_sb = SB("rmsw_sb", [128, 4], F32)
    bgk2_sb = SB("bgk2_sb", [128, KD], F32)
    wsum = SB("wsum", [128, 70], F32)
    scal = SB("scal", [1, 16], F32)
    scalb = SB("scalb", [128, 16], F32)
    Brun = SB("Brun", [128, KD], F32)
    ebias = SB("ebias", [128, 2 * KD], F32)
    stat_tok = SB("stat_tok", [128, 40], F32)
    rows8 = SB("rows8", [128, 8 * KD], F32)
    Lg = SB("Lg", [128, 4 * KD], F32)
    eacc = SB("eacc", [128, KD], F32)
    ecol = SB("ecol", [128, KD], F32)
    sjb = SB("sjb", [128, 512], F32)
    wst = [SB(f"wst{i}", [128, 1024], BF16) for i in range(2)]
    xbfr = SB("xbfr", [128, 1024], BF16)
    x2br = SB("x2br", [128, 1024], BF16)
    wgk1a = [SB(f"wgk1a{k}", [128, 32], BF16) for k in range(KD)]
    wgk2a = SB("wgk2a", [32, DK], BF16)
    zz256 = SB("zz256", [128, 256], F16)
    Abuf = SB("Abuf", [128, 3 * 128], BF16)
    hsel = SB("hsel", [128, 16], F32)
    ytmp = SB("ytmp", [128, 512], F32)
    epsc = SB("epsc", [128, 1], F32)
    zzf2 = SB("zzf2", [128, 2], F32)
    rowsc = SB("rowsc", [1, 512], F32)
    rowsc2 = SB("rowsc2", [1, 512], F32)
    o2scr = SB("o2scr", [128, 512], BF16)
    wvev = SB("wvev", [128, 512], BF16)

    base0 = rnd64(nc.sbuf_base)
    top = nc.sbuf_top

    def at(name, shape, dtype, off):
        per = int(np.prod(shape[1:])) * SZ[dtype]
        assert base0 + off + per <= top, f"{name}: off {off}+{per} exceeds {top - base0}"
        return nc.alloc_sbuf_tensor_at(name, list(shape), dtype, offset=base0 + off)

    RX = 0
    GB = rnd64(G * 4)
    xg = [at(f"xg{k}", [128, G], F32, RX + k * GB) for k in range(KD)]
    SGB = rnd64(SEG * 2)
    g16 = [at(f"g16_{k}", [128, SEG], F16, RX + k * SGB) for k in range(KD)]
    wqa = [at(f"wqa{k}", [128, DK], BF16, RX + 8 * SGB + k * 2048) for k in range(KD)]
    wka = [at(f"wka{k}", [128, DK], BF16, RX + 8 * SGB + 16384 + k * 2048) for k in range(KD)]
    RXE = max(8 * GB, 8 * SGB + 32768)

    RA = RXE
    GVB = rnd64(GV * 4)
    GV2B = rnd64(GV * 2)
    SEGB2 = rnd64(SEG * 2)
    sT1 = at("sT1", [128, GV], F32, RA + 0)
    sT2 = at("sT2", [128, GV], F32, RA + GVB)
    sScr = at("sScr", [128, GV], F32, RA + 2 * GVB)
    maxP = at("maxP", [128, NS * 128], BF16, RA + 3 * GVB)
    minP = at("minP", [128, NS * 128], BF16, RA + 3 * GVB + rnd64(NS * 256))
    hB = [at(f"hB{m}", [128, GV], BF16, RA + m * GV2B) for m in range(8)]
    OF1 = max(8 * GV2B, MH * 1024)
    h2t = at("h2t", [128, SEG], BF16, RA + OF1)
    habs = at("habs", [128, SEG], BF16, RA + OF1 + SEGB2)
    hmax = at("hmax", [128, SEG], BF16, RA + OF1 + 2 * SEGB2)
    OF2 = OF1 + 3 * SEGB2
    diag = [at(f"diag{i}", [128, 128], BF16, RA + OF2 + i * 256) for i in range(24)]
    OF3 = OF2 + 16384
    sT3 = at("sT3", [128, GV], F32, RA + OF3)
    sT4 = at("sT4", [128, GV], F32, RA + OF3 + GVB)
    RAE = RA + OF3 + 2 * GVB
    assert rnd64(PW * 2) == 1024
    xq2A = [at(f"xq2A_{k}", [128, PW], BF16, RA + k * 1024) for k in range(MH)]
    eqc = at("eqc", [128, KD * 256], BF16, RA + 0)
    ekc = at("ekc", [128, KD * 256], BF16, RA + 4224)
    qtc = at("qtc", [128, KD * 256], BF16, RA + 8448)
    ktc = at("ktc", [128, KD * 256], BF16, RA + 12672)
    ktok = at("ktok", [128, 2 * DK], BF16, RA + 16896)
    csc = at("csc", [128, KD * 256], F32, RA + 21120)
    och = at("och", [128, MV * 256], BF16, RA + 29312)
    vtk = [at(f"vtk{i}", [128, 2 * DV], BF16, RA + 37504 + i * 8192) for i in range(2)]
    T32 = at("T32", [128, KD * 512], F32, RA + 53888)
    T16 = at("T16", [128, KD * 512], BF16, RA + 70272)
    ALE = 70272 + 8192
    u_bf = at("u_bf", [32, SEG], BF16, RA + 0)
    sgs = [at(f"sgs_{k}", [128, SEG // 2], F32, RA + 4224 + k * rnd64(SEG * 2)) for k in range(KD)]
    wva = [at(f"wva{k}", [128, DV], BF16, RA + 4224 + 8 * rnd64(SEG * 2) + k * 4096) for k in range(KD)]
    vtmp = at("vtmp", [128, DV], BF16, RA + 4224 + 8 * rnd64(SEG * 2) + 32768)
    APRE = 4224 + 8 * rnd64(SEG * 2) + 32768 + 4096
    Sin = at("Sin", [128, KD * 512], F32, RA + 0)
    msb_t = at("msb_t", [128, H * 256], F32, RA + 4224)
    wga = [at(f"wga{k}", [128, DV], BF16, RA + 37504 + k * 4096) for k in range(KD)]
    woa = [at(f"woa{k}", [128, D], BF16, RA + 37504 + k * 2048) for k in range(MV)]
    gch = at("gch", [128, MV * 256], BF16, RA + 70272)
    S16b = at("S16b", [128, KD * 512], BF16, RA + 16896)
    RAE2 = ALE

    RB = RA + max(RAE - RA, RAE2, APRE, MH * rnd64(PW * 2))
    xq = [at(f"xq{k}", [128, GV], BF16, RB + k * GV2B) for k in range(KD)]
    w2sb = [at(f"w2sb{i}", [128, 4096], BF16, RB + i * 8192) for i in range(2)]
    _xq2b_off = lambda k: (RA + OF1 + k * 1024) if k < 28 else (RB + 16448 + (k - 28) * 1024)
    xq2B = [at(f"xq2B_{k}", [128, PW], BF16, _xq2b_off(k)) for k in range(MH)]
    h2u = [at(f"h2u{i}", [128, PW], BF16, RB + 20544 + i * 1024) for i in range(2)]
    xn2 = [at(f"xn2_{k}", [128, SEG], BF16, RB + k * SGB) for k in range(KD)]
    FZ = RB + 8 * GV2B
    wq32d = [at(f"wq32d{i}", [128, 1024], F32, RA + OF2 + 6144 + i * 4096) for i in range(2)]
    _statE = rnd64(3 * GVB + 2 * rnd64(NS * 256))
    w2q32d = [at(f"w2q32d{i}", [128, 1024], F32, RA + max(OF1, _statE) + i * 4096) for i in range(2)]
    wqbd = at("wqbd", [128, 2048], BF16, FZ)
    absb = at("absb", [128, 1024], BF16, FZ + 4096)
    RBE = FZ + 6144
    assert base0 + RBE <= top, f"total arena {RBE} over {top - base0}"

    class Sem:
        def __init__(self, name, dk=1):
            self.h = nc.alloc_semaphore(name)
            self.n = 0
            self.name = name
            self.dk = dk

        def inc(self, inst, k=None):
            k = self.dk if k is None else k
            inst.then_inc(self.h, k)
            self.n += k
            return inst

    sTE, sVE, sSE, sGC, sCC = (Sem(x) for x in ("sTE", "sVE", "sSE", "sGC", "sCC"))

    marks = {}

    def need(e, en, *deps):
        for d in deps:
            if not d:
                continue
            if isinstance(d, list):
                need(e, en, *d)
                continue
            sem, v = d
            if v is None or v <= 0:
                continue
            key = (en, sem.name)
            if marks.get(key, 0) < v:
                own = {"tensor": sTE, "vector": sVE, "scalar": sSE}.get(en)
                if own is not None and sem is own and v > sem.n:
                    import sys as _s
                    print(f"SELF-FUTURE WAIT: {en} waits {sem.name}>={v} at {sem.n}", file=_s.stderr)
                e.wait_ge(sem.h, v)
                marks[key] = v

    class DQ:

        def __init__(self, name):
            self.s = [Sem(name + "0", 16), Sem(name + "1", 16)]
            self.k = 0
            self.last_end = [0, 0]

        def begin(self, e, en):
            p = self.k % 2
            need(e, en, (self.s[p], self.last_end[p]))

        def inc(self, inst):
            self.s[self.k % 2].inc(inst)
            return inst

        def end(self):
            p = self.k % 2
            tok = (self.s[p], self.s[p].n)
            self.last_end[p] = self.s[p].n
            self.k += 1
            return tok

        def drain(self, e, en):
            need(e, en, (self.s[0], self.s[0].n), (self.s[1], self.s[1].n))

        def fence(self):
            return [(self.s[0], self.s[0].n), (self.s[1], self.s[1].n)]

    qIN, qGP, qOUT = DQ("qIN"), DQ("qGP"), DQ("qOUT")

    block_cm = nc.Block()
    block = block_cm.__enter__()
    equeue = {k: [] for k in ("sync", "gpsimd", "tensor", "vector", "scalar")}

    def on(ename, f):
        if merge:
            equeue[ename].append(f)
        else:
            getattr(block, ename)(lambda e, f=f, ename=ename: f(e, ename))

    def _tokval(v):
        if isinstance(v, tuple):
            return ("t", v[0].name, v[1])
        if isinstance(v, list):
            return tuple(_tokval(x) for x in v)
        return v

    if st_seed:
        sem_by_name = {s.name: s for s in (sTE, sVE, sSE, sGC, sCC)}
        for q in (qIN, qGP, qOUT):
            for s in q.s:
                sem_by_name[s.name] = s

        def _remap(v):
            if isinstance(v, tuple):
                return (sem_by_name[v[0].name], v[1])
            if isinstance(v, list):
                return [_remap(x) for x in v]
            return v

        class _ChkDict(dict):
            def __setitem__(self, k, v):
                if k in self and _tokval(self[k]) != _tokval(v):
                    raise AssertionError(f"two-pass mismatch at {k}: {_tokval(self[k])} -> {_tokval(v)}")
                dict.__setitem__(self, k, _remap(v) if True else v)
        st = _ChkDict({k: _remap(v) for k, v in st_seed.items()})
    else:
        st = {}

    def ph0_sync(e, en):
        for k in range(KD):
            qIN.begin(e, en)
            qIN.inc(e.dma_start(out=xg[k][:, :], in_=x_in[k * 128:(k + 1) * 128, :]))
        st["x_loaded"] = qIN.end()
        qIN.begin(e, en)
        qIN.inc(e.dma_start(out=ctl_sb[:, :], in_=ctl_i))
        for m in range(MH):
            qIN.inc(e.dma_start(out=convb_sb[:, m:m + 1], in_=conv_b_i[m * 128:(m + 1) * 128].rearrange("(p o) -> p o", o=1)))
        for tp in range(3):
            for m in range(MH):
                qIN.inc(e.dma_start(out=convw_sb[:, tp * MH + m:tp * MH + m + 1], in_=conv_w_i[tp, m * 128:(m + 1) * 128].rearrange("(p o) -> p o", o=1)))
        for c in range(4):
            qIN.inc(e.dma_start(out=rmsw_sb[:, c:c + 1], in_=rms_w_i[c * 128:(c + 1) * 128].rearrange("(p o) -> p o", o=1)))
        for k in range(KD):
            qIN.inc(e.dma_start(out=bgk2_sb[:, k:k + 1], in_=bgk2_i[k * 128:(k + 1) * 128].rearrange("(p o) -> p o", o=1)))
        st["consts_loaded"] = qIN.end()

    def ph0_gp(e, en):
        sGC.inc(e.memset(ident[:, :], 0.0))
        sGC.inc(e.memset(identf[:, :], 0.0))
        sGC.inc(e.memset(maskJT[:, :], 0.0))
        need(e, en, (sGC, sGC.n))
        sGC.inc(e.affine_select(out=ident[:, :], in_=ident[:, :], compare_op=ALU.not_equal,
                                fill=1.0, base=0, pattern=[[-1, 128]], channel_multiplier=1))
        sGC.inc(e.affine_select(out=identf[:, :], in_=identf[:, :], compare_op=ALU.not_equal,
                                fill=1.0, base=0, pattern=[[-1, 128]], channel_multiplier=1))
        sGC.inc(e.affine_select(out=maskJT[:, :], in_=maskJT[:, :], compare_op=ALU.is_gt,
                                fill=1.0, base=0, pattern=[[-1, 128]], channel_multiplier=1))
        e.memset(ones_b[:, :], 1.0)
        e.memset(onesf[:, :], 1.0)
        e.memset(ones1f[:, :], 1.0)
        e.memset(zz256[:, :], 0.0)
        e.memset(epsc[:, :], float(EPS))
        e.memset(zzf2[:, :], 0.0)
        e.memset(scal[:, :], 0.0)
        e.memset(wsum[:, :], 0.0)
        e.memset(stat_tok[:, :], 0.0)
        e.memset(rowsc[:, :], 0.0)
        e.memset(rowsc2[:, :], 0.0)
        sGC.inc(e.memset(Brun[:, :], 0.0))
        st["consts_built"] = sGC.n

    on("sync", ph0_sync)
    on("gpsimd", ph0_gp)

    NWT = 32

    def w_src_ap(widx, i):
        if widx == 0:
            return w1_t[(i // 4) * 128:(i // 4 + 1) * 128, (i % 4) * 1024:(i % 4 + 1) * 1024]
        return w2_t[i * 128:(i + 1) * 128, :]

    def w_out_ap(widx, i):
        out_d = w1q_d if widx == 0 else w2q_d
        off = ((i // 4) * 32 + (i % 4) * 8) * 16384 if widx == 0 else i * 8 * 16384
        return out_d[off:off + 8 * 16384].rearrange("(m p q) -> p m q", p=128, q=128)

    def qw_pass1_step(widx, i):
        buf = wq32d if widx == 0 else w2q32d
        q, eng = (qIN, "sync") if widx == 0 else (qGP, "gpsimd")

        def f1(e, en, widx=widx, i=i, q=q, buf=buf):
            need(e, en, (sSE, st.get(f"wabs{widx}_{i-2}", 0)))
            if widx == 1 and i < 2:
                need(e, en, (sSE, st[f"wabs0_{NWT-2+i}"]))
            q.begin(e, en)
            q.inc(e.dma_start(out=buf[i % 2][:, :], in_=w_src_ap(widx, i)))
            st[f"wl{widx}_{i}"] = q.end()

        def f1v(e, en, widx=widx, i=i, buf=buf):
            need(e, en, st[f"wl{widx}_{i}"], (sGC, st["consts_built"]))
            need(e, en, (sSE, sSE.n))
            sSE.inc(e.activation(absb[:, :], buf[i % 2][:, :], AF.Abs,
                                 accum_out=wsum[:, widx * 32 + i:widx * 32 + i + 1]))
            st[f"wabs{widx}_{i}"] = sSE.n
        on(eng, f1)
        on("scalar", f1v)

    def qw_scale(widx):
        def p1f(e, en):
            need(e, en, (sSE, st[f"wabs{widx}_{NWT-1}"]))
            need(e, en, (sVE, sVE.n))
            sVE.inc(e.tensor_reduce(wsum[:, 64 + widx:65 + widx], wsum[:, widx * 32:widx * 32 + NWT], AXX, ALU.add))
            st[f"wcol{widx}"] = sVE.n
        on("vector", p1f)

        def p2t(e, en):
            need(e, en, (sVE, st[f"wcol{widx}"]), (sGC, st["consts_built"]))
            sTE.inc(e.matmul(psF[0:1, 0:128], wsum[:, 64 + widx:65 + widx], onesf[:, 0:128], start=True, stop=True))
            st[f"wtot{widx}"] = sTE.n
        on("tensor", p2t)

        def p3v(e, en):
            need(e, en, (sTE, st[f"wtot{widx}"]), (sGC, st["consts_built"]))
            numel = float(DH * D)
            sVE.inc(e.tensor_scalar(scal[0:1, widx:widx + 1], psF[0:1, 0:1], 1.0 / numel, 1e-5, ALU.mult, ALU.max))
            need(e, en, (sVE, sVE.n))
            sVE.inc(e.reciprocal(scal[0:1, 2 + widx:3 + widx], scal[0:1, widx:widx + 1]))
            st[f"wrec{widx}"] = sVE.n
        on("vector", p3v)

        def p4t(e, en):
            need(e, en, (sVE, st[f"wrec{widx}"]))
            sTE.inc(e.matmul(psF[0:128, 0:16], ones1f[0:1, :], scal[0:1, 0:16], start=True, stop=True))
            st[f"wbc{widx}"] = sTE.n
        on("tensor", p4t)

        def p4v(e, en):
            need(e, en, (sTE, st[f"wbc{widx}"]))
            sVE.inc(e.tensor_copy(scalb[:, 0:16], psF[0:128, 0:16]))
            st[f"wscal{widx}"] = sVE.n
        on("vector", p4v)

    def qw_pass2_step(widx, i):
        buf = wq32d
        q = qIN if widx == 0 else qOUT

        def p5s(e, en, widx=widx, i=i, buf=buf, q=q):
            need(e, en, (sVE, st[f"wcol{widx}"]))
            if i >= 2:
                need(e, en, (sVE, st[f"wqd{widx}_{i-2}"]))
            elif widx == 1:
                need(e, en, (sVE, st[f"wqd0_{NWT-2+i}"]))
            q.begin(e, en)
            q.inc(e.dma_start(out=buf[i % 2][:, :], in_=w_src_ap(widx, i)))
            st[f"wl2{widx}_{i}"] = q.end()

        def p5v(e, en, widx=widx, i=i, buf=buf):
            need(e, en, st[f"wl2{widx}_{i}"], (sVE, st[f"wscal{widx}"]))
            need(e, en, (sVE, sVE.n))
            sVE.inc(e.tensor_scalar(buf[i % 2][:, :], buf[i % 2][:, :], scalb[:, 2 + widx:3 + widx], 1.49, ALU.mult, ALU.min))
            need(e, en, (sVE, sVE.n))
            sVE.inc(e.tensor_scalar(buf[i % 2][:, :], buf[i % 2][:, :], -1.49, MAGIC, ALU.max, ALU.add))
            need(e, en, (sVE, sVE.n))
            if i >= 2:
                need(e, en, st[f"wstr{widx}_{i-2}"])
            elif widx == 1:
                need(e, en, st[f"wstr0_{NWT-2+i}"])
            sVE.inc(e.tensor_scalar(wqbd[:, (i % 2) * 1024:(i % 2 + 1) * 1024], buf[i % 2][:, :], -MAGIC, None, ALU.add))
            st[f"wqd{widx}_{i}"] = sVE.n

        def p5g(e, en, widx=widx, i=i):
            need(e, en, (sVE, st[f"wqd{widx}_{i}"]))
            qGP.begin(e, en)
            qGP.inc(e.dma_start(
                out=w_out_ap(widx, i),
                in_=wqbd[:, (i % 2) * 1024:(i % 2 + 1) * 1024].rearrange("p (m q) -> p m q", q=128)))
            st[f"wstr{widx}_{i}"] = qGP.end()
        on("sync", p5s)
        on("vector", p5v)
        on("gpsimd", p5g)

    for i in range(NWT):
        qw_pass1_step(0, i)
    qw_scale(0)
    for i in range(NWT):
        qw_pass2_step(0, i)
        qw_pass1_step(1, i)
    st["wready0"] = [st[f"wstr0_{NWT-2}"], st[f"wstr0_{NWT-1}"]]


    def convscale(e, en):
        need(e, en, st["consts_loaded"], (sVE, st["wscal0"]))
        sVE.inc(e.tensor_scalar(convw_s[:, :], convw_sb[:, :], scalb[:, 0:1], None, ALU.mult))
        st["convw_scaled"] = sVE.n
    on("vector", convscale)

    def ffn_stats(app, with_quant=True, dep=()):
        a = f"f{app}"

        def sv(e, en):
            need(e, en, st["x_loaded"], *dep)
            if with_quant:
                if NS * 128 > GV:
                    e.memset(maxP[:, GV:NS * 128], 0.0)
                    e.memset(minP[:, GV:NS * 128], 0.0)
                sVE.inc(e.tensor_copy(maxP[:, 0:GV], xg[0][:, 0:GV]))
                sVE.inc(e.tensor_copy(minP[:, 0:GV], xg[0][:, 0:GV]))
                for k in range(1, KD):
                    need(e, en, (sVE, sVE.n))
                    sVE.inc(e.tensor_tensor(maxP[:, 0:GV], xg[k][:, 0:GV], maxP[:, 0:GV], ALU.max))
                    sVE.inc(e.tensor_tensor(minP[:, 0:GV], xg[k][:, 0:GV], minP[:, 0:GV], ALU.min))
                st[a + "trees"] = sVE.n
        on("vector", sv)

        for ti, (c0, w) in enumerate(TT):
            for k in range(KD):
                r = k % 2

                def s1v(e, en, ti=ti, c0=c0, w=w, k=k, r=r):
                    need(e, en, st["x_loaded"], *dep)
                    g = ti * KD + k - 2
                    if g >= 0:
                        pt, pk = divmod(g, KD)
                        need(e, en, (sTE, st.get(f"{a}smm{pt}_{pk}", 0)))
                    sVE.inc(e.tensor_copy(xbfr[:, r * 512:r * 512 + w], xg[k][:, c0:c0 + w]))
                    st[f"{a}sxb{ti}_{k}"] = sVE.n

                def s1se(e, en, ti=ti, c0=c0, w=w, k=k, r=r):
                    need(e, en, st["x_loaded"], *dep)
                    g = ti * KD + k - 2
                    if g >= 0:
                        pt, pk = divmod(g, KD)
                        need(e, en, (sTE, st.get(f"{a}smm{pt}_{pk}", 0)))
                    sSE.inc(e.activation(x2br[:, r * 512:r * 512 + w], xg[k][:, c0:c0 + w], AF.Square))
                    st[f"{a}sx2{ti}_{k}"] = sSE.n

                def s1t(e, en, ti=ti, c0=c0, w=w, k=k, r=r):
                    if k == 0 and ti >= 1:
                        need(e, en, (sVE, st[f"{a}var{ti-1}"]))
                    need(e, en, (sVE, st[f"{a}sxb{ti}_{k}"]), (sSE, st[f"{a}sx2{ti}_{k}"]))
                    e.matmul(psA[:, 0:w], ones_b[:, :], xbfr[:, r * 512:r * 512 + w],
                             start=(k == 0), stop=(k == KD - 1))
                    mm = e.matmul(psB[:, 0:w], ones_b[:, :], x2br[:, r * 512:r * 512 + w],
                                  start=(k == 0), stop=(k == KD - 1))
                    sTE.inc(mm)
                    st[f"{a}smm{ti}_{k}"] = sTE.n
                on("vector", s1v)
                on("scalar", s1se)
                on("tensor", s1t)
            st[f"{a}stats{ti}"] = st[f"{a}smm{ti}_{KD-1}"]

            def s2v(e, en, ti=ti, c0=c0, w=w):
                need(e, en, (sTE, st[f"{a}stats{ti}"]), *dep)
                sVE.inc(e.tensor_scalar(sScr[:, c0:c0 + w], psA[:, 0:w], 1.0 / D, None, ALU.mult))
                need(e, en, (sVE, sVE.n))
                sVE.inc(e.tensor_mul(sT1[:, c0:c0 + w], sScr[:, c0:c0 + w], sScr[:, c0:c0 + w]))
                need(e, en, (sVE, sVE.n))
                sVE.inc(e.scalar_tensor_tensor(sT2[:, c0:c0 + w], psB[:, 0:w], 1.0 / D, sT1[:, c0:c0 + w], ALU.mult, ALU.subtract))
                st[f"{a}var{ti}"] = sVE.n
            on("vector", s2v)

        def s3se(e, en):
            need(e, en, (sVE, st[f"{a}var{NT-1}"]))
            need(e, en, (sGC, st["consts_built"]))
            sSE.inc(e.activation(sT2[:, 0:GV], sT2[:, 0:GV], AF.Ln, bias=epsc[:, 0:1]))
            need(e, en, (sSE, sSE.n))
            sSE.inc(e.activation(sT2[:, 0:GV], sT2[:, 0:GV], AF.Exp, scale=-0.5))
            st[a + "rstd"] = sSE.n
        on("scalar", s3se)

        if not with_quant:
            return

        for i in range(NS):
            def s4t(e, en, i=i):
                need(e, en, (sVE, st[a + "trees"]))
                need(e, en, (sVE, st.get(f"{a}tred{i-1}", 0)))
                e.transpose(psT[:, 0:128], maxP[:, i * 128:(i + 1) * 128], ident[:, :])
                sTE.inc(e.transpose(psT[:, 128:256], minP[:, i * 128:(i + 1) * 128], ident[:, :]))
                st[f"{a}tp{i}"] = sTE.n

            def s4v(e, en, i=i):
                need(e, en, (sTE, st[f"{a}tp{i}"]))
                e.tensor_reduce(stat_tok[:, i:i + 1], psT[:, 0:128], AXX, ALU.max)
                sVE.inc(e.tensor_reduce(stat_tok[:, 20 + i:21 + i], psT[:, 128:256], AXX, ALU.min))
                st[f"{a}tred{i}"] = sVE.n
            on("tensor", s4t)
            on("vector", s4v)

        def s4f(e, en):
            need(e, en, (sVE, st[f"{a}tred{NS-1}"]))
            st[a + "treds"] = sVE.n
        on("vector", s4f)

        NR = ceil_div(NS, 4)
        for rnd in range(NR):
            tl = list(range(rnd * 4, min(rnd * 4 + 4, NS)))
            def b1t(e, en, rnd=rnd, tl=tl):
                need(e, en, (sVE, st[a + "treds"]))
                need(e, en, (sVE, st.get(f"{a}rowev{rnd-1}", 0)))
                for ii, i in enumerate(tl):
                    ins = e.matmul(psF[0:1, (i % 4) * 128:(i % 4) * 128 + 128], stat_tok[:, i:i + 1], identf[:, :],
                                   start=(ii == 0), stop=(ii == len(tl) - 1))
                for ii, i in enumerate(tl):
                    ins = e.matmul(psG[0:1, (i % 4) * 128:(i % 4) * 128 + 128], stat_tok[:, 20 + i:21 + i], identf[:, :],
                                   start=(ii == 0), stop=(ii == len(tl) - 1))
                sTE.inc(ins)
                st[f"{a}rowmm{rnd}"] = sTE.n
            def b1v(e, en, rnd=rnd, tl=tl):
                need(e, en, (sTE, st[f"{a}rowmm{rnd}"]))
                e.tensor_copy(rowsc[0:1, 0:len(tl) * 128], psF[0:1, 0:len(tl) * 128])
                sVE.inc(e.tensor_copy(rowsc2[0:1, 0:len(tl) * 128], psG[0:1, 0:len(tl) * 128]))
                st[f"{a}rowev{rnd}"] = sVE.n
            def b2t(e, en, rnd=rnd, tl=tl):
                need(e, en, (sVE, st[f"{a}rowev{rnd}"]))
                need(e, en, (sVE, st.get(f"{a}bcev{rnd-1}", 0)))
                for ii, i in enumerate(tl):
                    e.matmul(psE[:, (i % 4) * 128:(i % 4) * 128 + 128], ones1f[0:1, :], rowsc[0:1, (i % 4) * 128:(i % 4) * 128 + 128],
                             start=(ii == 0), stop=(ii == len(tl) - 1))
                    ins = e.matmul(psD[:, (i % 4) * 128:(i % 4) * 128 + 128], ones1f[0:1, :], rowsc2[0:1, (i % 4) * 128:(i % 4) * 128 + 128],
                                   start=(ii == 0), stop=(ii == len(tl) - 1))
                sTE.inc(ins)
                st[f"{a}bcmm{rnd}"] = sTE.n
            def b2v(e, en, rnd=rnd, tl=tl):
                need(e, en, (sTE, st[f"{a}bcmm{rnd}"]))
                for i in tl:
                    w = min(128, GV - i * 128)
                    if w <= 0:
                        continue
                    e.tensor_copy(sT3[:, i * 128:i * 128 + w], psE[:, (i % 4) * 128:(i % 4) * 128 + w])
                    ins = e.tensor_copy(sT4[:, i * 128:i * 128 + w], psD[:, (i % 4) * 128:(i % 4) * 128 + w])
                sVE.inc(ins)
                st[f"{a}bcev{rnd}"] = sVE.n
            on("tensor", b1t)
            on("vector", b1v)
            on("tensor", b2t)
            on("vector", b2v)
        def bfin(e, en):
            need(e, en, (sVE, st[f"{a}bcev{NR-1}"]))
            st[a + "bcasts"] = sVE.n
        on("vector", bfin)

        def s7v(e, en):
            need(e, en, (sVE, st[a + "bcasts"]), (sSE, st[a + "rstd"]))
            sVE.inc(e.tensor_sub(sT3[:, 0:GV], sT3[:, 0:GV], sScr[:, 0:GV]))
            sVE.inc(e.tensor_sub(sT4[:, 0:GV], sScr[:, 0:GV], sT4[:, 0:GV]))
            need(e, en, (sVE, sVE.n))
            sVE.inc(e.tensor_tensor(sT3[:, 0:GV], sT3[:, 0:GV], sT4[:, 0:GV], ALU.max))
            need(e, en, (sVE, sVE.n))
            sVE.inc(e.tensor_tensor(sT3[:, 0:GV], sT3[:, 0:GV], sT2[:, 0:GV], ALU.mult))
            need(e, en, (sVE, sVE.n))
            sVE.inc(e.tensor_scalar(sT3[:, 0:GV], sT3[:, 0:GV], 1e-5, None, ALU.max))
            need(e, en, (sVE, sVE.n))
            sVE.inc(e.tensor_scalar(sT4[:, 0:GV], sT3[:, 0:GV], 1.0 / 127.0, None, ALU.mult))
            sVE.inc(e.reciprocal(sT3[:, 0:GV], sT3[:, 0:GV]))
            need(e, en, (sVE, sVE.n))
            sVE.inc(e.tensor_scalar(sT3[:, 0:GV], sT3[:, 0:GV], 127.0, None, ALU.mult))
            need(e, en, (sVE, sVE.n))
            sVE.inc(e.tensor_tensor(sT3[:, 0:GV], sT3[:, 0:GV], sT2[:, 0:GV], ALU.mult))
            need(e, en, (sVE, sVE.n))
            sVE.inc(e.scalar_tensor_tensor(sT1[:, 0:GV], sScr[:, 0:GV], -1.0, sT3[:, 0:GV], ALU.mult, ALU.mult))
            st[a + "quantsc"] = sVE.n
        on("vector", s7v)

    def ffn_quant_apply(app):
        a = f"f{app}"

        def qv(e, en):
            need(e, en, (sVE, st[a + "quantsc"]))
            if app == 1:
                need(e, en, st.get("tap_xq0", 0), st.get("tap_xn2", 0))
            for k in range(KD):
                need(e, en, (sVE, sVE.n))
                sVE.inc(e.tensor_tensor(sScr[:, 0:GV], xg[k][:, 0:GV], sT3[:, 0:GV], ALU.mult))
                need(e, en, (sVE, sVE.n))
                sVE.inc(e.tensor_tensor(sScr[:, 0:GV], sScr[:, 0:GV], sT1[:, 0:GV], ALU.add))
                need(e, en, (sVE, sVE.n))
                sVE.inc(e.tensor_scalar(sScr[:, 0:GV], sScr[:, 0:GV], 127.0, -128.0, ALU.min, ALU.max))
                need(e, en, (sVE, sVE.n))
                sVE.inc(e.tensor_scalar(sScr[:, 0:GV], sScr[:, 0:GV], MAGIC, -MAGIC, ALU.add, ALU.add))
                need(e, en, (sVE, sVE.n))
                sVE.inc(e.tensor_tensor(xq[k][:, 0:GV], sScr[:, 0:GV], sT4[:, 0:GV], ALU.mult))
                st[f"{a}xq{k}"] = sVE.n
            st[a + "xqall"] = sVE.n
        on("vector", qv)

    def ffn_mm1_conv(app, extra_per_blk=None):
        a = f"f{app}"
        for blk in range(4):
            if extra_per_blk:
                extra_per_blk(blk)
            def dv(e, en, blk=blk):
                need(e, en, (sVE, st["convw_scaled"]), (sGC, st["consts_built"]))
                key = f"{a}cvu{blk-1}" if blk > 0 else (f"f{app-1}cvu3" if app > 0 else None)
                if key:
                    need(e, en, (sTE, st[key]))
                for t in range(3):
                    for mloc in range(8):
                        m = blk * 8 + mloc
                        sVE.inc(e.tensor_scalar(diag[t * 8 + mloc][:, :], ident[:, :], convw_s[:, t * MH + m:t * MH + m + 1], None, ALU.mult))
                st[f"{a}diag{blk}"] = sVE.n
            on("vector", dv)

            for mloc in range(8):
                m = blk * 8 + mloc
                r = m % 2

                def ws(e, en, m=m, r=r):
                    need(e, en, st["wready0"])
                    need(e, en, (sTE, st.get(f"{a}mmw{m-2}", 0)))
                    qIN.begin(e, en)
                    qIN.inc(e.dma_start(
                        out=wst[r][:, :].rearrange("p (k q) -> p k q", q=128),
                        in_=w1q_d.rearrange("(t p q) -> t p q", p=128, q=128)[m::MH, :, :].rearrange("k p q -> p k q")))
                    st[f"{a}wl{m}"] = qIN.end()

                on("sync", ws)

                def mt0(e, en, m=m, r=r):
                    need(e, en, st[f"{a}wl{m}"], (sVE, st[a + "xqall"]))
                on("tensor", mt0)
                for ti, (c0, w) in enumerate(TT):
                    def mt(e, en, m=m, r=r, ti=ti, c0=c0, w=w):
                        ps = psA if ti % 2 == 0 else psB
                        need(e, en, (sVE, st.get(f"{a}hev{m}_{ti-2}", 0)))
                        lp = ti + 2 * ((NT - 1 - ti) // 2)
                        need(e, en, (sVE, st.get(f"{a}hev{m-1}_{lp}", 0)))
                        for k in range(KD):
                            ins = e.matmul(ps[:, 0:w], wst[r][:, k * 128:(k + 1) * 128], xq[k][:, c0:c0 + w],
                                           start=(k == 0), stop=(k == KD - 1))
                        sTE.inc(ins)
                        st[f"{a}hmm{m}_{ti}"] = sTE.n
                        if ti == NT - 1:
                            st[f"{a}mmw{m}"] = sTE.n

                    def ev(e, en, m=m, mloc=mloc, ti=ti, c0=c0, w=w):
                        need(e, en, (sTE, st.get(f"{a}cmm{m-8}_{len(TK)-1}", 0)))
                        ps = psA if ti % 2 == 0 else psB
                        need(e, en, (sTE, st[f"{a}hmm{m}_{ti}"]))
                        sVE.inc(e.tensor_copy(hB[mloc][:, c0:c0 + w], ps[:, 0:w]))
                        st[f"{a}hev{m}_{ti}"] = sVE.n
                        if ti == NT - 1:
                            st[f"{a}hband{m}"] = sVE.n
                    on("tensor", mt)
                    on("vector", ev)

            for mloc in range(8):
                m = blk * 8 + mloc
                for ti, (c0, w) in enumerate(TK):
                    g = m * len(TK) + ti

                    def ct(e, en, m=m, mloc=mloc, blk=blk, ti=ti, c0=c0, w=w, g=g):
                        need(e, en, (sVE, st[f"{a}hband{m}"]), (sVE, st[f"{a}diag{blk}"]))
                        gp = g - 2
                        if gp >= 0:
                            pm, pt = divmod(gp, len(TK))
                            need(e, en, (sSE, st.get(f"{a}cse{pm}_{pt}", 0)))
                        ps = psC if g % 2 == 0 else psD
                        e.matmul(ps[:, 0:w], diag[0 * 8 + mloc][:, :], hB[mloc][:, c0 - 1:c0 - 1 + w], start=True, stop=False)
                        e.matmul(ps[:, 0:w], diag[1 * 8 + mloc][:, :], hB[mloc][:, c0:c0 + w], start=False, stop=False)
                        ins = e.matmul(ps[:, 0:w], diag[2 * 8 + mloc][:, :], hB[mloc][:, c0 + 1:c0 + 1 + w], start=False, stop=True)
                        sTE.inc(ins)
                        st[f"{a}cmm{m}_{ti}"] = sTE.n
                        if mloc == 7 and ti == len(TK) - 1:
                            st[f"{a}cvu{blk}"] = sTE.n

                    def cs(e, en, m=m, ti=ti, c0=c0, w=w, g=g):
                        ps = psC if g % 2 == 0 else psD
                        need(e, en, (sTE, st[f"{a}cmm{m}_{ti}"]))
                        if ti == 0:
                            need(e, en, st.get(f"{a}h2st{m-1}", 0))
                        sSE.inc(e.activation(h2t[:, c0 - 1:c0 - 1 + w], ps[:, 0:w], AF.Silu, bias=convb_sb[:, m:m + 1]))
                        st[f"{a}cse{m}_{ti}"] = sSE.n
                    on("tensor", ct)
                    on("scalar", cs)

                def csa(e, en, m=m):
                    need(e, en, (sSE, st[f"{a}cse{m}_{len(TK)-1}"]))
                    need(e, en, (sVE, st.get(f"{a}hmax{m-1}", 0)))
                    sSE.inc(e.activation(habs[:, 0:SEG], h2t[:, 0:SEG], AF.Abs))
                    st[f"{a}habs{m}"] = sSE.n

                def cv(e, en, m=m):
                    need(e, en, (sSE, st[f"{a}habs{m}"]))
                    if m == 0:
                        sVE.inc(e.tensor_copy(hmax[:, 0:SEG], habs[:, 0:SEG]))
                    else:
                        sVE.inc(e.tensor_tensor(hmax[:, 0:SEG], habs[:, 0:SEG], hmax[:, 0:SEG], ALU.max))
                    st[f"{a}hmax{m}"] = sVE.n

                def cg(e, en, m=m):
                    need(e, en, (sSE, st[f"{a}habs{m}"]))
                    qGP.begin(e, en)
                    qGP.inc(e.dma_start(out=h2_d[m, :, :], in_=h2t[:, 0:SEG]))
                    st[f"{a}h2st{m}"] = qGP.end()
                on("scalar", csa)
                on("vector", cv)
                on("gpsimd", cg)

    def ffn_quant2(app):
        a = f"f{app}"

        for i in range(NSS):
            def q2t(e, en, i=i):
                need(e, en, (sVE, st[f"{a}hmax{MH-1}"]))
                need(e, en, (sVE, st.get(f"{a}q2red{i-1}", 0)))
                sTE.inc(e.transpose(psT[:, 0:128], hmax[:, i * 128:(i + 1) * 128], ident[:, :]))
                st[f"{a}q2tp{i}"] = sTE.n

            def q2v(e, en, i=i):
                need(e, en, (sTE, st[f"{a}q2tp{i}"]))
                sVE.inc(e.tensor_reduce(stat_tok[:, i:i + 1], psT[:, 0:128], AXX, ALU.max))
                st[f"{a}q2red{i}"] = sVE.n
            on("tensor", q2t)
            on("vector", q2v)

        def q2f(e, en):
            need(e, en, (sVE, st[f"{a}q2red{NSS-1}"]))
            st[a + "q2reds"] = sVE.n
        on("vector", q2f)

        NR2 = ceil_div(NSS, 4)
        for rnd in range(NR2):
            tl = list(range(rnd * 4, min(rnd * 4 + 4, NSS)))
            def qb1t(e, en, rnd=rnd, tl=tl):
                need(e, en, (sVE, st[a + "q2reds"]))
                need(e, en, (sVE, st.get(f"{a}q2rowev{rnd-1}", 0)))
                for ii, i in enumerate(tl):
                    ins = e.matmul(psF[0:1, (i % 4) * 128:(i % 4) * 128 + 128], stat_tok[:, i:i + 1], identf[:, :],
                                   start=(ii == 0), stop=(ii == len(tl) - 1))
                sTE.inc(ins)
                st[f"{a}q2rowmm{rnd}"] = sTE.n
            def qb1v(e, en, rnd=rnd, tl=tl):
                need(e, en, (sTE, st[f"{a}q2rowmm{rnd}"]))
                sVE.inc(e.tensor_copy(rowsc[0:1, 0:len(tl) * 128], psF[0:1, 0:len(tl) * 128]))
                st[f"{a}q2rowev{rnd}"] = sVE.n
            def qb2t(e, en, rnd=rnd, tl=tl):
                need(e, en, (sVE, st[f"{a}q2rowev{rnd}"]))
                need(e, en, (sVE, st.get(f"{a}q2bcev{rnd-1}", 0)))
                for ii, i in enumerate(tl):
                    ins = e.matmul(psE[:, (i % 4) * 128:(i % 4) * 128 + 128], ones1f[0:1, :], rowsc[0:1, (i % 4) * 128:(i % 4) * 128 + 128],
                                   start=(ii == 0), stop=(ii == len(tl) - 1))
                sTE.inc(ins)
                st[f"{a}q2bcmm{rnd}"] = sTE.n
            def qb2v(e, en, rnd=rnd, tl=tl):
                need(e, en, (sTE, st[f"{a}q2bcmm{rnd}"]))
                for i in tl:
                    ins = e.tensor_copy(sT3[:, 1 + i * 128:1 + (i + 1) * 128], psE[:, (i % 4) * 128:(i % 4) * 128 + 128])
                sVE.inc(ins)
                st[f"{a}q2bcev{rnd}"] = sVE.n
            on("tensor", qb1t)
            on("vector", qb1v)
            on("tensor", qb2t)
            on("vector", qb2v)

        def q4v(e, en):
            need(e, en, (sVE, st[f"{a}q2bcev{NR2-1}"]))
            need(e, en, (sVE, sVE.n))
            sVE.inc(e.tensor_scalar(sT3[:, 1:1 + SEG], sT3[:, 1:1 + SEG], 1e-5, None, ALU.max))
            need(e, en, (sVE, sVE.n))
            sVE.inc(e.tensor_scalar(sT4[:, 1:1 + SEG], sT3[:, 1:1 + SEG], 0.5 / 127.0, None, ALU.mult))
            need(e, en, (sVE, sVE.n))
            sVE.inc(e.tensor_scalar(sT4[:, 1:1 + SEG], sT4[:, 1:1 + SEG], scalb[:, 1:2], None, ALU.mult))
            sVE.inc(e.reciprocal(sT3[:, 1:1 + SEG], sT3[:, 1:1 + SEG]))
            need(e, en, (sVE, sVE.n))
            sVE.inc(e.tensor_scalar(sT3[:, 1:1 + SEG], sT3[:, 1:1 + SEG], 127.0, None, ALU.mult))
            st[a + "q2sc"] = sVE.n
        on("vector", q4v)

    def ffn_mm2(app):
        a = f"f{app}"

        def emit_quant(pr, ks):
            pc0 = 1 + pr * PW
            xq2 = xq2A if pr % 2 == 0 else xq2B
            for k in ks:
                hb = h2u[k % 2]

                def l2g(e, en, k=k, pr=pr, pc0=pc0, hb=hb):
                    need(e, en, st[f"{a}h2st{k}"], st[f"{a}h2st{MH-1}"], (sSE, st[f"{a}habs{MH-1}"]))
                    need(e, en, (sVE, st.get(f"{a}xq2_{pr}_{k-2}", 0)))
                    if k < 2 and pr > 0:
                        need(e, en, (sVE, st.get(f"{a}xq2_{pr-1}_{MH-2+k}", 0)))
                    qGP.begin(e, en)
                    qGP.inc(e.dma_start(out=hb[:, 0:PW], in_=h2_d[k, :, pc0 - 1:pc0 - 1 + PW]))
                    st[f"{a}l2{pr}_{k}"] = qGP.end()

                def l2v(e, en, k=k, pr=pr, pc0=pc0, hb=hb, xq2=xq2):
                    need(e, en, st[f"{a}l2{pr}_{k}"], (sVE, st[a + "q2sc"]), (sTE, st[f"{a}cvu3"]))
                    if app == 0 and pr == 1:
                        need(e, en, st["wready1"])
                    need(e, en, (sTE, st.get(f"{a}ymm{pr-2}_{KD-1}", 0)))
                    need(e, en, (sVE, sVE.n))
                    sVE.inc(e.tensor_tensor(ytmp[:, 0:PW], hb[:, 0:PW], sT3[:, pc0:pc0 + PW], ALU.mult))
                    need(e, en, (sVE, sVE.n))
                    sVE.inc(e.tensor_scalar(xq2[k][:, 0:PW], ytmp[:, 0:PW], MAGIC, -MAGIC, ALU.add, ALU.add))
                    st[f"{a}xq2_{pr}_{k}"] = sVE.n
                on("gpsimd", l2g)
                on("vector", l2v)

        _QSPLIT = [5, 5, 5, 5, 4, 4, 4, 0]

        emit_quant(0, range(MH))
        for pr in range(NPAIR):
            pc0 = 1 + pr * PW
            xq2 = xq2A if pr % 2 == 0 else xq2B
            for mo in range(KD):
                r = mo % 2

                def w2s(e, en, mo=mo, r=r, pr=pr):
                    need(e, en, st["wready1"], (sTE, st[f"{a}cvu3"]))
                    if app == 0:
                        need(e, en, (sVE, st[f"{a}xq{KD-1}"]))
                    if mo >= 2:
                        need(e, en, (sTE, st[f"{a}ymm{pr}_{mo-2}"]))
                    elif pr > 0:
                        need(e, en, (sTE, st[f"{a}ymm{pr-1}_{KD-2+mo}"]))
                    qIN.begin(e, en)
                    qIN.inc(e.dma_start(
                        out=w2sb[r][:, :].rearrange("p (k q) -> p k q", q=128),
                        in_=w2q_d.rearrange("(t p q) -> t p q", p=128, q=128)[mo::KD, :, :].rearrange("k p q -> p k q")))
                    st[f"{a}w2l{pr}_{mo}"] = qIN.end()

                def ymt(e, en, mo=mo, r=r, pr=pr, xq2=xq2):
                    need(e, en, st[f"{a}w2l{pr}_{mo}"],
                         (sVE, st[f"{a}xq2_{pr}_{MH-1}"]))
                    ps = psA if mo % 2 == 0 else psB
                    if mo >= 2:
                        need(e, en, (sVE, st[f"{a}yev{pr}_{mo-2}"]))
                    elif pr > 0:
                        need(e, en, (sVE, st[f"{a}yev{pr-1}_{KD-2+mo}"]))
                    for k in range(MH):
                        ins = e.matmul(ps[:, 0:PW], w2sb[r][:, k * 128:(k + 1) * 128], xq2[k][:, 0:PW],
                                       start=(k == 0), stop=(k == MH - 1))
                    sTE.inc(ins)
                    st[f"{a}ymm{pr}_{mo}"] = sTE.n
                    if mo == KD - 1:
                        st[f"{a}ymms{pr}"] = sTE.n

                def yev(e, en, mo=mo, pr=pr, pc0=pc0):
                    ps = psA if mo % 2 == 0 else psB
                    need(e, en, (sTE, st[f"{a}ymm{pr}_{mo}"]), (sVE, st[a + "q2sc"]))
                    need(e, en, (sVE, sVE.n))
                    sVE.inc(e.tensor_tensor(ytmp[:, 0:PW], ps[:, 0:PW], sT4[:, pc0:pc0 + PW], ALU.mult))
                    need(e, en, (sVE, sVE.n))
                    sVE.inc(e.tensor_tensor(xg[mo][:, pc0:pc0 + PW], ytmp[:, 0:PW], xg[mo][:, pc0:pc0 + PW], ALU.add))
                    st[f"{a}yev{pr}_{mo}"] = sVE.n
                on("sync", w2s)
                on("tensor", ymt)
                on("vector", yev)
                if pr + 1 < NPAIR:
                    s0 = sum(_QSPLIT[:mo])
                    emit_quant(pr + 1, range(s0, s0 + _QSPLIT[mo]))

        def fin(e, en):
            for mo in range(KD):
                need(e, en, (sVE, st[f"{a}yev{NPAIR-1}_{mo}"]))
            st[a + "resid"] = max(st[f"{a}yev{pr}_{mo}"] for pr in range(NPAIR) for mo in range(KD))
        on("vector", fin)

    def ffn_sublayer(app, dep=()):
        ffn_stats(app, with_quant=True, dep=dep)
        ffn_quant_apply(app)
        ffn_mm1_conv(app)
        ffn_quant2(app)
        ffn_mm2(app)

    def tap_sb(name, tiles, cols, deps, eng="sync"):
        if name not in tap_t:
            return
        q = qOUT if eng == "sync" else qGP
        def tf(e, en):
            for d in deps:
                need(e, en, d)
            q.begin(e, en)
            for i, t in enumerate(tiles):
                q.inc(e.dma_start(out=tap_t[name][i * 128:(i + 1) * 128, 0:cols], in_=t[:, 0:cols]))
            st["tap_" + name] = q.end()
        on(eng, tf)

    ffn_stats(0, with_quant=True)
    ffn_quant_apply(0)
    qw_scale(1)
    ffn_mm1_conv(0, extra_per_blk=lambda blk: [qw_pass2_step(1, blk * 8 + j) for j in range(8)])
    st["wready1"] = [st[f"wstr1_{NWT-2}"], st[f"wstr1_{NWT-1}"]]
    ffn_quant2(0)
    ffn_mm2(0)
    tap_sb("x1", xg, G, [(sVE, st["f0resid"])])
    tap_sb("xq0", xq, GV, [(sVE, st["f0xqall"])], eng="gpsimd")
    if "w1q" in tap_t:
        def tf_w1(e, en):
            need(e, en, st["wready0"])
            qGP.begin(e, en)
            for m in range(MH):
                for k in range(KD):
                    off = (k * MH + m) * 16384
                    qGP.inc(e.dma_start(out=tap_t["w1q"][(m * KD + k) * 128:(m * KD + k + 1) * 128, :],
                                        in_=w1q_d[off:off + 16384].rearrange("(p q) -> p q", q=128)))
            st["tap_w1q"] = qGP.end()
        on("gpsimd", tf_w1)
    if "h2d" in tap_t:
        def tf_h2(e, en):
            need(e, en, st["f0h2st31"])
            qGP.begin(e, en)
            for m in range(MH):
                qGP.inc(e.dma_start(out=tap_t["h2d"][m * 128:(m + 1) * 128, :], in_=h2_d[m, :, :]))
            st["tap_h2d"] = qGP.end()
        on("gpsimd", tf_h2)

    ffn_stats("ln2", with_quant=False, dep=[(sVE, st["f0resid"])])

    def xn2v(e, en):
        need(e, en, (sSE, st["fln2rstd"]), st.get("tap_xq0", 0))
        for k in range(KD):
            need(e, en, (sVE, sVE.n))
            sVE.inc(e.tensor_sub(sT1[:, 1:1 + SEG], xg[k][:, 1:1 + SEG], sScr[:, 1:1 + SEG]))
            need(e, en, (sVE, sVE.n))
            sVE.inc(e.tensor_tensor(xn2[k][:, 0:SEG], sT1[:, 1:1 + SEG], sT2[:, 1:1 + SEG], ALU.mult))
            st[f"xn2_{k}"] = sVE.n
        st["xn2all"] = sVE.n
    on("vector", xn2v)
    tap_sb("xn2", xn2, SEG, [(sVE, st["xn2all"])], eng="gpsimd")

    _SPORD = [3, 4, 5, 6, 7, 0, 1, 2]

    def spx(e, en):
        need(e, en, st.get("tap_x1", 0), st.get("tap_xn2", 0))
        for k in _SPORD:
            need(e, en, (sVE, st[f"f0yev{NPAIR-1}_{k}"]))
            qGP.begin(e, en)
            qGP.inc(e.dma_start(out=x1_d[k * 128:(k + 1) * 128, :], in_=xg[k][:, :]))
            st[f"xsp{k}"] = qGP.end()
        st["xspill"] = [st["xsp1"], st["xsp2"]]
        st["xsp_qk"] = [st["xsp6"], st["xsp7"]]
    on("gpsimd", spx)

    def awl(e, en):
        for k in range(KD):
            qGP.begin(e, en)
            qGP.inc(e.dma_start(out=wgk1a[k][:, :], in_=wgk1_t[k * 128:(k + 1) * 128, :]))
        qGP.inc(e.dma_start(out=wgk2a[:, :], in_=wgk2_t))
        st["wgkl"] = qGP.end()
        for k in range(KD):
            qGP.begin(e, en)
            qGP.inc(e.dma_start(out=wva[k][:, :], in_=wv_t[k * 128:(k + 1) * 128, :]))
        st["wvl"] = qGP.end()
        need(e, en, st["xsp_qk"], (sVE, st["xn2all"]))
        for k in range(KD):
            qGP.begin(e, en)
            qGP.inc(e.dma_start(out=wqa[k][:, :], in_=wq_t[k * 128:(k + 1) * 128, :]))
            qGP.inc(e.dma_start(out=wka[k][:, :], in_=wk_t[k * 128:(k + 1) * 128, :]))
        st["wqkl"] = qGP.end()
    on("gpsimd", awl)

    NT2 = SEG // 512
    for ti in range(NT2):
        def ut(e, en, ti=ti):
            need(e, en, st["wgkl"], (sVE, st["xn2all"]))
            need(e, en, (sVE, st.get(f"uev{ti-1}", 0)))
            for k in range(KD):
                ins = e.matmul(psC[0:32, 0:512], wgk1a[k][:, :], xn2[k][:, ti * 512:(ti + 1) * 512],
                               start=(k == 0), stop=(k == KD - 1))
            sTE.inc(ins)
            st[f"umm{ti}"] = sTE.n

        def uv(e, en, ti=ti):
            need(e, en, (sTE, st[f"umm{ti}"]))
            sVE.inc(e.tensor_copy(u_bf[:, ti * 512:(ti + 1) * 512], psC[0:32, 0:512]))
            st[f"uev{ti}"] = sVE.n
        on("tensor", ut)
        on("vector", uv)

    def uf(e, en):
        need(e, en, (sVE, st[f"uev{NT2-1}"]))
        st["u_done"] = sVE.n
    on("vector", uf)

    for half in range(2):
        hw = SEG // 2
        hc0 = half * hw
        NTH = hw // 512 if hw >= 512 else 1
        TW = hw // NTH
        for m in range(KD):
            for ti in range(NTH):
                g = (half * KD + m) * NTH + ti

                def zt(e, en, half=half, hc0=hc0, m=m, ti=ti, g=g):
                    need(e, en, (sVE, st["u_done"]))
                    gp = g - 2
                    if gp >= 0:
                        need(e, en, (sSE, st.get(f"sgk{gp}", 0)))
                    ps = psD if g % 2 == 0 else psE
                    sTE.inc(e.matmul(ps[:, 0:TW if TW <= 512 else 512], wgk2a[:, m * 128:(m + 1) * 128],
                                     u_bf[:, hc0 + ti * TW:hc0 + (ti + 1) * TW], start=True, stop=True))
                    st[f"zmm{g}"] = sTE.n

                def zs(e, en, half=half, hc0=hc0, m=m, ti=ti, g=g):
                    need(e, en, (sTE, st[f"f0ymms{NPAIR-1}"]), (sVE, st["f0resid"]))
                    need(e, en, (sSE, st.get(f"ghalf{half-1}", 0)))
                    need(e, en, (sTE, st[f"zmm{g}"]))
                    ps = psD if g % 2 == 0 else psE
                    sSE.inc(e.activation(sgs[m][:, ti * TW:(ti + 1) * TW], ps[:, 0:TW], AF.Sigmoid, bias=bgk2_sb[:, m:m + 1]))
                    st[f"sgk{g}"] = sSE.n
                on("tensor", zt)
                on("scalar", zs)

        def zl(e, en, half=half, hc0=hc0):
            need(e, en, st["xspill"], (sVE, st["xn2all"]))
            for m in range(KD):
                need(e, en, (sSE, st[f"sgk{(half * KD + m) * NTH + NTH - 1}"]))
                sSE.inc(e.activation(g16[m][:, hc0:hc0 + hw], sgs[m][:, 0:hw], AF.Ln))
            st[f"ghalf{half}"] = sSE.n
        on("scalar", zl)
    st["g16done"] = st["ghalf1"]

    for j in range(NJ):
        for n in range(4):
            g = j * 4 + n

            def vt(e, en, j=j, n=n, g=g):
                need(e, en, st["wvl"], (sVE, st["xn2all"]))
                gp = g - 2
                if gp >= 0:
                    need(e, en, (sVE, st.get(f"vev{gp}", 0)))
                ps = psF if g % 2 == 0 else psG
                for k in range(KD):
                    ins = e.matmul(ps[:, 0:512], xn2[k][:, j * 128:(j + 1) * 128], wva[k][:, n * 512:(n + 1) * 512],
                                   start=(k == 0), stop=(k == KD - 1))
                sTE.inc(ins)
                st[f"vmm{g}"] = sTE.n

            def vv(e, en, j=j, n=n, g=g):
                need(e, en, (sTE, st[f"vmm{g}"]))
                if n == 0:
                    need(e, en, st.get(f"vst{j-1}", 0))
                ps = psF if g % 2 == 0 else psG
                sVE.inc(e.tensor_copy(vtmp[:, n * 512:(n + 1) * 512], ps[:, 0:512]))
                st[f"vev{g}"] = sVE.n
            on("tensor", vt)
            on("vector", vv)

        def vg(e, en, j=j):
            need(e, en, (sVE, st[f"vev{j * 4 + 3}"]))
            qGP.begin(e, en)
            qGP.inc(e.dma_start(out=v_d[j, :, :], in_=vtmp[:, :]))
            st[f"vst{j}"] = qGP.end()
        on("gpsimd", vg)
    def vfin(e, en):
        st["v_done"] = qGP.fence()
    on("gpsimd", vfin)

    def tz(e, en):
        need(e, en, st["v_done"], st["wready1"])
        sGC.inc(e.memset(T32[:, :], 0.0))
        st["Tzero"] = sGC.n
    on("gpsimd", tz)

    PS_O0, PS_O1, PS_A, PS_TD0, PS_TD1 = psA, psB, psC, psD, psE
    for c in range(OC):
        c0 = c * 256
        def l1v(e, en, c=c, c0=c0):
            need(e, en, (sSE, st["g16done"]), (sGC, st["Tzero"]))
            need(e, en, (sVE, st.get(f"c{c-1}brun", 0)))
            for k in range(KD):
                e.tensor_scalar(ebias[:, k:k + 1], Brun[:, k:k + 1], 1.0 / 32.0, LN16, ALU.mult, ALU.add)
                sVE.inc(e.tensor_scalar(ebias[:, KD + k:KD + k + 1], Brun[:, k:k + 1], -1.0 / 32.0, None, ALU.mult))
            st[f"c{c}ebias"] = sVE.n
            need(e, en, (sSE, st.get(f"c{c-1}exps", 0)))
            for k in range(KD):
                sVE.inc(e.tensor_tensor_scan(csc[:, k * 256:(k + 1) * 256], g16[k][:, c0:c0 + 256],
                                             zz256[:, 0:256], 0.0, ALU.add, ALU.add))
                st[f"c{c}scan{k}"] = sVE.n
            st[f"c{c}scans"] = sVE.n
        on("vector", l1v)

        def l1s(e, en, c=c):
            need(e, en, (sVE, st[f"c{c}scans"]), (sVE, st[f"c{c}ebias"]))
            need(e, en, (sVE, st.get(f"c{c-1}qk", 0)))
            for k in range(KD):
                e.activation(eqc[:, k * 256:(k + 1) * 256], csc[:, k * 256:(k + 1) * 256], AF.Exp,
                             bias=ebias[:, k:k + 1], scale=1.0 / 32.0)
                sSE.inc(e.activation(ekc[:, k * 256:(k + 1) * 256], csc[:, k * 256:(k + 1) * 256], AF.Exp,
                                     bias=ebias[:, KD + k:KD + k + 1], scale=-1.0 / 32.0))
            st[f"c{c}exps"] = sSE.n
        on("scalar", l1s)

        def l1b(e, en, c=c):
            need(e, en, (sSE, st[f"c{c}exps"]))
            for k in range(KD):
                sVE.inc(e.tensor_tensor(Brun[:, k:k + 1], Brun[:, k:k + 1], csc[:, k * 256 + 255:k * 256 + 256], ALU.add))
            st[f"c{c}brun"] = sVE.n
        on("vector", l1b)

        def lvg(e, en, c=c):
            need(e, en, st["v_done"])
            need(e, en, (sVE, st.get(f"c{c-2}oall", 0)))
            qGP.begin(e, en)
            qGP.inc(e.dma_start(out=vtk[c % 2][:, :].rearrange("p (j q) -> p j q", q=DV),
                                in_=v_d[2 * c:2 * c + 2, :, :].rearrange("j p q -> p j q")))
            st[f"c{c}vl"] = qGP.end()
        on("gpsimd", lvg)

        for qk in range(2):
            wt, key, ec = (wqa, "q", eqc) if qk == 0 else (wka, "k", ekc)
            tg = qtc if qk == 0 else ktc
            for m in range(KD):
                g = qk * KD + m

                def lq1(e, en, c=c, c0=c0, m=m, g=g, wt=wt, key=key):
                    if g == 0:
                        need(e, en, st["wqkl"], (sVE, st["xn2all"]))
                        need(e, en, (sVE, st.get(f"vev{NJ * 4 - 1}", 0)))
                    gp = g - 2
                    if gp >= 0:
                        need(e, en, (sVE, st.get(f"c{c}qkev{gp}", 0)))
                    ps = psF if g % 2 == 0 else psG
                    for k in range(KD):
                        ins = e.matmul(ps[:, 0:256], wt[k][:, m * 128:(m + 1) * 128], xn2[k][:, c0:c0 + 256],
                                       start=(k == 0), stop=(k == KD - 1))
                    sTE.inc(ins)
                    st[f"c{c}qkmm{g}"] = sTE.n

                def lq2(e, en, c=c, m=m, g=g, ec=ec, tg=tg):
                    if g == 0:
                        need(e, en, (sSE, st[f"c{c}exps"]))
                        need(e, en, (sTE, st.get(f"c{c-1}omm{H-1}", 0)), st.get(f"c{c-1}qsp", 0),
                             (sTE, st.get(f"c{c-1}tdmm{H-1}", 0)))
                    need(e, en, (sTE, st[f"c{c}qkmm{g}"]))
                    ps = psF if g % 2 == 0 else psG
                    sVE.inc(e.tensor_tensor(tg[:, m * 256:(m + 1) * 256], ps[:, 0:256], ec[:, m * 256:(m + 1) * 256], ALU.mult))
                    st[f"c{c}qkev{g}"] = sVE.n
                on("tensor", lq1)
                on("vector", lq2)

        def lqf(e, en, c=c):
            need(e, en, (sVE, st[f"c{c}qkev{2 * KD - 1}"]))
            st[f"c{c}qk"] = sVE.n
            st[f"c{c}qev{KD-1}"] = sVE.n
        on("vector", lqf)

        def lqg(e, en, c=c, c0=c0):
            need(e, en, (sVE, st[f"c{c}qev{KD-1}"]))
            qGP.begin(e, en)
            qGP.inc(e.dma_start(out=qt_d[:, :, c0:c0 + 256].rearrange("k p q -> p k q"),
                                in_=qtc[:, :].rearrange("p (k q) -> p k q", q=256)))
            st[f"c{c}qsp"] = qGP.end()
        on("gpsimd", lqg)

        for jt in range(2):
            def ltt(e, en, c=c, jt=jt):
                need(e, en, (sVE, st[f"c{c}qk"]))
                need(e, en, (sVE, st.get(f"c{c}ktokev{jt-1}", 0)))
                if jt == 0:
                    need(e, en, (sVE, st.get(f"c{c-1}ktokev1", 0)))
                for m in range(KD):
                    ins = e.transpose(psT[:, m * 128:(m + 1) * 128], ktc[:, m * 256 + jt * 128:m * 256 + jt * 128 + 128], ident[:, :])
                sTE.inc(ins)
                st[f"c{c}ktok{jt}"] = sTE.n

            def ltv(e, en, c=c, jt=jt):
                need(e, en, (sTE, st[f"c{c}ktok{jt}"]))
                need(e, en, (sTE, st.get(f"c{c-1}tdmm{H-1}", 0)))
                sVE.inc(e.tensor_copy(ktok[:, jt * 1024:(jt + 1) * 1024], psT[:, 0:1024]))
                st[f"c{c}ktokev{jt}"] = sVE.n
            on("tensor", ltt)
            on("vector", ltv)

        def lt16(e, en, c=c):
            need(e, en, (sVE, st.get(f"c{c-1}tupd", 0)), (sTE, st.get(f"c{c-1}omm{H-1}", 0)))
            need(e, en, (sGC, st["Tzero"]), st["wready1"])
            sVE.inc(e.tensor_copy(T16[:, :], T32[:, :]))
            st[f"c{c}t16"] = sVE.n
        on("vector", lt16)

        for h in range(H):
            def lat(e, en, c=c, h=h):
                need(e, en, (sVE, st[f"c{c}qk"]))
                need(e, en, (sVE, st.get(f"c{c}aev{h-1}", 0)))
                first = True
                for bi, (jt, tt_) in enumerate(((0, 0), (1, 1), (0, 1))):
                    for kk in range(2):
                        ch = 2 * h + kk
                        ins = e.matmul(PS_A[:, bi * 128:(bi + 1) * 128],
                                       ktc[:, ch * 256 + jt * 128:ch * 256 + jt * 128 + 128],
                                       qtc[:, ch * 256 + tt_ * 128:ch * 256 + tt_ * 128 + 128],
                                       start=first, stop=(bi == 2 and kk == 1))
                        first = False
                sTE.inc(ins)
                st[f"c{c}amm{h}"] = sTE.n
            on("tensor", lat)

            def lav(e, en, c=c, h=h):
                need(e, en, (sTE, st[f"c{c}amm{h}"]), (sGC, st["consts_built"]))
                need(e, en, (sTE, st.get(f"c{c}omm{h-1}", 0)))
                e.tensor_tensor(Abuf[:, 0:128], PS_A[:, 0:128], maskJT[:, :], ALU.mult)
                e.tensor_tensor(Abuf[:, 128:256], PS_A[:, 128:256], maskJT[:, :], ALU.mult)
                sVE.inc(e.tensor_copy(Abuf[:, 256:384], PS_A[:, 256:384]))
                st[f"c{c}aev{h}"] = sVE.n
            on("vector", lav)

            def lot(e, en, c=c, h=h):
                need(e, en, (sVE, st[f"c{c}aev{h}"]), st[f"c{c}vl"], (sVE, st[f"c{c}t16"]))
                need(e, en, (sVE, st.get(f"c{c}oev{h-1}", 0)))
                vb = vtk[c % 2]
                first = True
                for cv in range(4):
                    e.matmul(PS_O0[:, cv * 128:(cv + 1) * 128], vb[:, 0 * DV + h * 512 + cv * 128:0 * DV + h * 512 + (cv + 1) * 128],
                             Abuf[:, 0:128], start=first, stop=False)
                    first = False
                for kk in range(2):
                    ch = 2 * h + kk
                    for cv in range(4):
                        e.matmul(PS_O0[:, cv * 128:(cv + 1) * 128],
                                 T16[:, ch * 512 + cv * 128:ch * 512 + (cv + 1) * 128],
                                 qtc[:, ch * 256:ch * 256 + 128], start=False, stop=(kk == 1 and cv == 3))
                first = True
                for cv in range(4):
                    e.matmul(PS_O1[:, cv * 128:(cv + 1) * 128], vb[:, 1 * DV + h * 512 + cv * 128:1 * DV + h * 512 + (cv + 1) * 128],
                             Abuf[:, 128:256], start=first, stop=False)
                    first = False
                for cv in range(4):
                    e.matmul(PS_O1[:, cv * 128:(cv + 1) * 128], vb[:, 0 * DV + h * 512 + cv * 128:0 * DV + h * 512 + (cv + 1) * 128],
                             Abuf[:, 256:384], start=False, stop=False)
                last = None
                for kk in range(2):
                    ch = 2 * h + kk
                    for cv in range(4):
                        last = e.matmul(PS_O1[:, cv * 128:(cv + 1) * 128],
                                        T16[:, ch * 512 + cv * 128:ch * 512 + (cv + 1) * 128],
                                        qtc[:, ch * 256 + 128:ch * 256 + 256], start=False, stop=(kk == 1 and cv == 3))
                sTE.inc(last)
                st[f"c{c}omm{h}"] = sTE.n
                need(e, en, (sVE, st[f"c{c}ktokev1"]))
                need(e, en, (sVE, st.get(f"c{c}tup{h-1}", 0)))
                for kk in range(2):
                    ch = 2 * h + kk
                    ps = PS_TD0 if kk == 0 else PS_TD1
                    for jt in range(2):
                        ins = e.matmul(ps[:, 0:512], ktok[:, jt * 1024 + ch * 128:jt * 1024 + (ch + 1) * 128],
                                       vb[:, jt * DV + h * 512:jt * DV + (h + 1) * 512], start=(jt == 0), stop=(jt == 1))
                    sTE.inc(ins)
                st[f"c{c}tdmm{h}"] = sTE.n
            on("tensor", lot)

            def lov(e, en, c=c, h=h):
                need(e, en, (sTE, st[f"c{c}omm{h}"]))
                need(e, en, st.get(f"c{c-1}osp", 0))
                for tau in range(2):
                    ps = PS_O0 if tau == 0 else PS_O1
                    dst = och[:, h * 1024:(h + 1) * 1024].rearrange("p (cv t) -> p cv t", t=256)[:, :, tau * 128:tau * 128 + 128]
                    src = ps[:, 0:512].rearrange("p (cv t) -> p cv t", t=128)
                    sVE.inc(e.tensor_copy(dst, src))
                st[f"c{c}oev{h}"] = sVE.n
                need(e, en, (sTE, st[f"c{c}tdmm{h}"]))
                for kk in range(2):
                    ch = 2 * h + kk
                    ps = PS_TD0 if kk == 0 else PS_TD1
                    sVE.inc(e.tensor_tensor(T32[:, ch * 512:(ch + 1) * 512], ps[:, 0:512], T32[:, ch * 512:(ch + 1) * 512], ALU.add))
                st[f"c{c}tup{h}"] = sVE.n
            on("vector", lov)
        st[f"c{c}tupd"] = st[f"c{c}tup{H-1}"]

        def log_(e, en, c=c, c0=c0):
            need(e, en, (sVE, st[f"c{c}oev{H-1}"]))
            qGP.begin(e, en)
            qGP.inc(e.dma_start(out=o_d[:, :, c0:c0 + 256].rearrange("cv p t -> p cv t"),
                                in_=och[:, :].rearrange("p (cv t) -> p cv t", t=256)))
            st[f"c{c}osp"] = qGP.end()
        on("gpsimd", log_)

        def lfin(e, en, c=c):
            need(e, en, (sVE, st[f"c{c}tupd"]), (sVE, st[f"c{c}oev{H-1}"]))
            st[f"c{c}oall"] = sVE.n
        on("vector", lfin)

    def pcs(e, en):
        need(e, en, (sVE, st[f"c{OC-1}tupd"]), (sVE, st[f"c{OC-1}brun"]))
        for h in range(H):
            for kk in range(2):
                ch = 2 * h + kk
                off = (h * 257 + kk * 128) * 512
                qGP.begin(e, en)
                qGP.inc(e.dma_start(out=cc_s[off:off + 128 * 512].rearrange("(p q) -> p q", q=512),
                                    in_=T32[:, ch * 512:(ch + 1) * 512]))
            offl = (h * 257 + 256) * 512
            for kk in range(2):
                ch = 2 * h + kk
                qGP.inc(e.dma_start(out=cc_s[offl + kk * 128:offl + kk * 128 + 128].rearrange("(p o) -> p o", o=1),
                                    in_=Brun[:, ch:ch + 1]))
            qGP.inc(e.dma_start(out=cc_s[offl + 256:offl + 512].rearrange("(p o) -> p o", o=2),
                                in_=zzf2[:, 0:2]))
        st["ccs"] = qGP.end()
        need(e, en, st["ccs"])
        sCC.inc(e.collective_compute(
            "AllGather", ALU.bypass,
            replica_groups=[[0, 1, 2, 3], [4, 5, 6, 7]],
            ins=[cc_s], outs=[cc_r]))
        st["ag1"] = sCC.n
    on("gpsimd", pcs)

    def gwl(e, en):
        need(e, en, (sVE, st[f"c{OC-1}oall"]))
        need(e, en, st["ccs"])
        for k in range(KD):
            qGP.begin(e, en)
            qGP.inc(e.dma_start(out=wga[k][:, :], in_=wg_t[k * 128:(k + 1) * 128, :]))
        st["wgl"] = qGP.end()
    on("gpsimd", gwl)

    def xrl(e, en):
        need(e, en, st["ccs"])
        need(e, en, st["wqkl"])
        for k in range(KD):
            qIN.begin(e, en)
            qIN.inc(e.dma_start(out=xg[k][:, :], in_=x1_d[k * 128:(k + 1) * 128, :]))
        st["xreload"] = qIN.end()
    on("sync", xrl)

    for mg in range(MV):
        for ti in range(NT2):
            g = mg * NT2 + ti

            def gt(e, en, mg=mg, ti=ti, g=g):
                need(e, en, st["wgl"], (sVE, st["xn2all"]))
                gp = g - 2
                if gp >= 0:
                    need(e, en, (sSE, st.get(f"gse{gp}", 0)))
                ps = psF if g % 2 == 0 else psG
                for k in range(KD):
                    ins = e.matmul(ps[:, 0:512], wga[k][:, mg * 128:(mg + 1) * 128], xn2[k][:, ti * 512:(ti + 1) * 512],
                                   start=(k == 0), stop=(k == KD - 1))
                sTE.inc(ins)
                st[f"gmm{g}"] = sTE.n

            def gs(e, en, mg=mg, ti=ti, g=g):
                need(e, en, (sTE, st[f"gmm{g}"]))
                if ti == 0:
                    need(e, en, st.get(f"gst{mg-1}", 0))
                ps = psF if g % 2 == 0 else psG
                sSE.inc(e.activation(h2t[:, ti * 512:(ti + 1) * 512], ps[:, 0:512], AF.Silu))
                st[f"gse{g}"] = sSE.n
            on("tensor", gt)
            on("scalar", gs)

        def gg(e, en, mg=mg):
            need(e, en, (sSE, st[f"gse{mg * NT2 + NT2 - 1}"]))
            qGP.begin(e, en)
            qGP.inc(e.dma_start(out=gate_d[mg, :, :], in_=h2t[:, 0:SEG]))
            st[f"gst{mg}"] = qGP.end()
        on("gpsimd", gg)
    def gfin(e, en):
        st["gate_done"] = qGP.fence()
    on("gpsimd", gfin)

    def cmb_g(e, en):
        need(e, en, (sCC, st["ag1"]), st["gate_done"])
        for j in range(4):
            for ch in range(KD):
                h, kk = ch // 2, ch % 2
                off = j * H * 257 * 512 + (h * 257 + 256) * 512 + kk * 128
                qGP.begin(e, en)
                qGP.inc(e.dma_start(out=Lg[:, j * KD + ch:j * KD + ch + 1],
                                    in_=cc_r[off:off + 128].rearrange("(p o) -> p o", o=1)))
        st["Lgl"] = qGP.end()
        sGC.inc(e.memset(Sin[:, :], 0.0))
        st["sin0"] = sGC.n
    on("gpsimd", cmb_g)

    for j in range(4):
        def cj_v(e, en, j=j):
            need(e, en, st["Lgl"], st["consts_loaded"])
            need(e, en, (sSE, st.get(f"ec{j-1}", 0)))
            for ch in range(KD):
                for l in range(4):
                    if l == 0:
                        ins = e.tensor_scalar(eacc[:, ch:ch + 1], Lg[:, l * KD + ch:l * KD + ch + 1], ctl_sb[:, 4 * j + l:4 * j + l + 1], None, ALU.mult)
                    else:
                        need(e, en, (sVE, sVE.n))
                        ins = e.scalar_tensor_tensor(eacc[:, ch:ch + 1], Lg[:, l * KD + ch:l * KD + ch + 1], ctl_sb[:, 4 * j + l:4 * j + l + 1], eacc[:, ch:ch + 1], ALU.mult, ALU.add)
                    sVE.inc(ins)
            st[f"eacc{j}"] = sVE.n
        on("vector", cj_v)

        def cj_s(e, en, j=j):
            need(e, en, (sVE, st[f"eacc{j}"]))
            sSE.inc(e.activation(ecol[:, :], eacc[:, :], AF.Exp, bias=ctl_sb[:, 16 + j:17 + j], scale=1.0 / 32.0))
            st[f"ec{j}"] = sSE.n
        on("scalar", cj_s)

        for ch in range(KD):
            def cj_g(e, en, j=j, ch=ch):
                need(e, en, (sCC, st["ag1"]))
                h, kk = ch // 2, ch % 2
                prev = st.get(f"sacc{j}_{ch-1}", st.get(f"sacc{j-1}_{KD-1}", 0) if ch == 0 else 0)
                need(e, en, (sVE, prev))
                off = j * H * 257 * 512 + (h * 257 + kk * 128) * 512
                r = ch % 2
                qGP.begin(e, en)
                qGP.inc(e.dma_start(out=sjb[:, 0:512] if r == 0 else sjb[:, 0:512],
                                    in_=cc_r[off:off + 128 * 512].rearrange("(p q) -> p q", q=512)))
                st[f"sjl{j}_{ch}"] = qGP.end()
            def cj_a(e, en, j=j, ch=ch):
                need(e, en, st[f"sjl{j}_{ch}"], (sSE, st[f"ec{j}"]), (sGC, st["sin0"]))
                need(e, en, (sVE, st.get(f"sacc{j-1}_{ch}", 0)))
                sVE.inc(e.scalar_tensor_tensor(Sin[:, ch * 512:(ch + 1) * 512], sjb[:, 0:512], ecol[:, ch:ch + 1],
                                               Sin[:, ch * 512:(ch + 1) * 512], ALU.mult, ALU.add))
                st[f"sacc{j}_{ch}"] = sVE.n
            on("gpsimd", cj_g)
            on("vector", cj_a)

    def s16v(e, en):
        for ch in range(KD):
            need(e, en, (sVE, st[f"sacc3_{ch}"]))
        sVE.inc(e.tensor_copy(S16b[:, :], Sin[:, :]))
        st["s16"] = sVE.n
    on("vector", s16v)

    def wol(e, en):
        need(e, en, st["gate_done"])
        for cv in range(MV):
            qGP.begin(e, en)
            qGP.inc(e.dma_start(out=woa[cv][:, :], in_=wout_t[cv * 128:(cv + 1) * 128, :]))
        st["wol"] = qGP.end()
    on("gpsimd", wol)

    for c in range(OC):
        c0 = c * 256
        def pb_g(e, en, c=c, c0=c0):
            need(e, en, (sVE, st["s16"]), st[f"c{OC-1}osp"], st[f"c{OC-1}qsp"])
            need(e, en, (sVE, st.get(f"pb{c-1}done", 0)))
            qGP.begin(e, en)
            qGP.inc(e.dma_start(out=och[:, :].rearrange("p (cv t) -> p cv t", t=256),
                                in_=o_d[:, :, c0:c0 + 256].rearrange("cv p t -> p cv t")))
            qGP.inc(e.dma_start(out=eqc[:, :].rearrange("p (k q) -> p k q", q=256),
                                in_=qt_d[:, :, c0:c0 + 256].rearrange("k p q -> p k q")))
            qGP.inc(e.dma_start(out=gch[:, :].rearrange("p (cv t) -> p cv t", t=256),
                                in_=gate_d[:, :, c0:c0 + 256].rearrange("cv p t -> p cv t")))
            st[f"pb{c}in"] = qGP.end()
        on("gpsimd", pb_g)

        for cv in range(MV):
            def pb_t1(e, en, c=c, cv=cv):
                need(e, en, st[f"pb{c}in"], (sVE, st["s16"]))
                need(e, en, (sVE, st.get(f"pb{c}o1_{cv-2}", 0)))
                h = cv // 4
                ps = psF if cv % 2 == 0 else psG
                for kk in range(2):
                    ch = 2 * h + kk
                    ins = e.matmul(ps[:, 0:256], S16b[:, ch * 512 + (cv % 4) * 128:ch * 512 + (cv % 4) * 128 + 128],
                                   eqc[:, ch * 256:(ch + 1) * 256], start=(kk == 0), stop=(kk == 1))
                sTE.inc(ins)
                st[f"pb{c}corr{cv}"] = sTE.n

            def pb_v1(e, en, c=c, cv=cv):
                ps = psF if cv % 2 == 0 else psG
                need(e, en, (sTE, st[f"pb{c}corr{cv}"]))
                sVE.inc(e.tensor_tensor(och[:, cv * 256:(cv + 1) * 256], ps[:, 0:256], och[:, cv * 256:(cv + 1) * 256], ALU.add))
                st[f"pb{c}o1_{cv}"] = sVE.n
            on("tensor", pb_t1)
            on("vector", pb_v1)

        for cv in range(MV):
            h = cv // 4

            def pb_s1(e, en, c=c, cv=cv):
                need(e, en, (sVE, st[f"pb{c}o1_{cv}"]))
                need(e, en, (sTE, st.get(f"pb{c}msmm{cv-2}", 0)))
                sSE.inc(e.activation(o2scr[:, (cv % 2) * 256:(cv % 2) * 256 + 256],
                                     och[:, cv * 256:(cv + 1) * 256], AF.Square))
                st[f"pb{c}sq{cv}"] = sSE.n

            def pb_t2(e, en, c=c, cv=cv, h=h):
                need(e, en, (sSE, st[f"pb{c}sq{cv}"]))
                if cv % 4 == 0:
                    need(e, en, (sVE, st.get(f"pb{c}msv{h-1}", 0)))
                ins = e.matmul(psC[:, 0:256], ones_b[:, :], o2scr[:, (cv % 2) * 256:(cv % 2) * 256 + 256],
                               start=(cv % 4 == 0), stop=(cv % 4 == 3))
                sTE.inc(ins)
                st[f"pb{c}msmm{cv}"] = sTE.n
                if cv % 4 == 3:
                    st[f"pb{c}ms{h}"] = sTE.n
            on("scalar", pb_s1)
            on("tensor", pb_t2)

            if cv % 4 == 3:
                def pb_v2(e, en, c=c, h=h):
                    need(e, en, (sTE, st[f"pb{c}ms{h}"]))
                    sVE.inc(e.tensor_scalar(msb_t[:, h * 256:(h + 1) * 256], psC[:, 0:256], 1.0 / HV, float(EPS), ALU.mult, ALU.add))
                    st[f"pb{c}msv{h}"] = sVE.n
                on("vector", pb_v2)

        def pb_s2(e, en, c=c):
            ins = None
            for h in range(H):
                need(e, en, (sVE, st[f"pb{c}msv{h}"]))
                ins = e.activation(msb_t[:, h * 256:(h + 1) * 256], msb_t[:, h * 256:(h + 1) * 256], AF.Ln)
            sSE.inc(ins)
            need(e, en, (sSE, sSE.n))
            for h in range(H):
                ins = e.activation(msb_t[:, h * 256:(h + 1) * 256], msb_t[:, h * 256:(h + 1) * 256], AF.Exp, scale=-0.5)
            sSE.inc(ins)
            st[f"pb{c}rstd"] = sSE.n
        on("scalar", pb_s2)

        def pb_v3(e, en, c=c):
            need(e, en, (sSE, st[f"pb{c}rstd"]), st[f"pb{c}in"])
            for cv in range(MV):
                h = cv // 4
                need(e, en, (sVE, sVE.n))
                sVE.inc(e.scalar_tensor_tensor(och[:, cv * 256:(cv + 1) * 256], och[:, cv * 256:(cv + 1) * 256],
                                               rmsw_sb[:, cv % 4:cv % 4 + 1], msb_t[:, h * 256:(h + 1) * 256], ALU.mult, ALU.mult))
            need(e, en, (sVE, sVE.n))
            for cv in range(MV):
                need(e, en, (sVE, sVE.n))
                sVE.inc(e.tensor_tensor(och[:, cv * 256:(cv + 1) * 256], och[:, cv * 256:(cv + 1) * 256],
                                        gch[:, cv * 256:(cv + 1) * 256], ALU.mult))
            st[f"pb{c}of"] = sVE.n
        on("vector", pb_v3)

        for mo in range(KD):
            def pb_t3(e, en, c=c, mo=mo):
                need(e, en, (sVE, st[f"pb{c}of"]), st["wol"], st["xreload"])
                need(e, en, (sVE, st.get(f"pb{c}xr{mo-2}", 0)))
                ps = psD if mo % 2 == 0 else psE
                for cv in range(MV):
                    ins = e.matmul(ps[:, 0:256], woa[cv][:, mo * 128:(mo + 1) * 128], och[:, cv * 256:(cv + 1) * 256],
                                   start=(cv == 0), stop=(cv == MV - 1))
                sTE.inc(ins)
                st[f"pb{c}wo{mo}"] = sTE.n

            def pb_v4(e, en, c=c, c0=c0, mo=mo):
                ps = psD if mo % 2 == 0 else psE
                need(e, en, (sTE, st[f"pb{c}wo{mo}"]))
                sVE.inc(e.tensor_tensor(xg[mo][:, 1 + c0:1 + c0 + 256], ps[:, 0:256], xg[mo][:, 1 + c0:1 + c0 + 256], ALU.add))
                st[f"pb{c}xr{mo}"] = sVE.n
            on("tensor", pb_t3)
            on("vector", pb_v4)

        def pb_fin(e, en, c=c):
            need(e, en, (sVE, st[f"pb{c}xr{KD-1}"]))
            st[f"pb{c}done"] = sVE.n
        on("vector", pb_fin)
    st["x2done"] = st[f"pb{OC-1}done"]
    tap_sb("x2", xg, G, [(sVE, st["x2done"])])

    def c2g(e, en):
        need(e, en, (sVE, st["pb0done"]), (sVE, st["x2done"]))
        for k in range(KD):
            qGP.begin(e, en)
            qGP.inc(e.dma_start(out=cc2_s[k * 128:(k + 1) * 128].rearrange("(p o) -> p o", o=1), in_=xg[k][:, 1:2]))
            qGP.inc(e.dma_start(out=cc2_s[D + k * 128:D + (k + 1) * 128].rearrange("(p o) -> p o", o=1), in_=xg[k][:, SEG:SEG + 1]))
        st["cc2s"] = qGP.end()
        need(e, en, st["cc2s"])
        sCC.inc(e.collective_compute(
            "AllGather", ALU.bypass,
            replica_groups=[[0, 1, 2, 3], [4, 5, 6, 7]],
            ins=[cc2_s], outs=[cc2_r]))
        st["ag2"] = sCC.n
        need(e, en, (sCC, st["ag2"]))
        for k in range(KD):
            qGP.begin(e, en)
            qGP.inc(e.dma_start(out=rows8[:, k * 8:(k + 1) * 8].rearrange("p (j f) -> p j f", f=2),
                                in_=cc2_r.rearrange("(j f p) -> p j f", f=2, p=D)[k * 128:(k + 1) * 128, :, :],
                                allow_slow_non_contiguous=True))
        st["rows8l"] = qGP.end()
    on("gpsimd", c2g)

    def c2v(e, en):
        need(e, en, st["rows8l"], st["consts_loaded"], st.get("tap_x2", 0))
        for k in range(KD):
            for j in range(4):
                if j == 0:
                    e.tensor_scalar(xg[k][:, 0:1], rows8[:, k * 8 + j * 2 + 1:k * 8 + j * 2 + 2], ctl_sb[:, 20 + j:21 + j], None, ALU.mult)
                    ins = e.tensor_scalar(xg[k][:, G - 3:G - 2], rows8[:, k * 8 + j * 2:k * 8 + j * 2 + 1], ctl_sb[:, 24 + j:25 + j], None, ALU.mult)
                else:
                    e.scalar_tensor_tensor(xg[k][:, 0:1], rows8[:, k * 8 + j * 2 + 1:k * 8 + j * 2 + 2], ctl_sb[:, 20 + j:21 + j], xg[k][:, 0:1], ALU.mult, ALU.add)
                    ins = e.scalar_tensor_tensor(xg[k][:, G - 3:G - 2], rows8[:, k * 8 + j * 2:k * 8 + j * 2 + 1], ctl_sb[:, 24 + j:25 + j], xg[k][:, G - 3:G - 2], ALU.mult, ALU.add)
                sVE.inc(ins)
                need(e, en, (sVE, sVE.n))
        st["halo_fixed_v"] = sVE.n
    on("vector", c2v)
    st["halo_fixed"] = st["halo_fixed_v"]

    ffn_sublayer(1, dep=[(sVE, st["x2done"]), (sVE, st["halo_fixed"])])

    def out_s(e, en):
        need(e, en, (sVE, st["f1resid"]))
        for k in range(KD):
            qOUT.begin(e, en)
            qOUT.inc(e.dma_start(out=y_out[k * 128:(k + 1) * 128, :], in_=xg[k][:, 1:1 + SEG]))
        st["yout"] = qOUT.end()
        qOUT.drain(e, en)
        qGP.drain(e, en)
        qIN.drain(e, en)
    on("sync", out_s)

    if merge:
        for ename in ("sync", "gpsimd", "tensor", "vector", "scalar"):
            def run_all(e, fs=tuple(equeue[ename]), en=ename):
                for f in fs:
                    f(e, en)
            getattr(block, ename)(run_all)
    block_cm.__exit__(None, None, None)
    return nc, st


def _prep_core_inputs(core, SEG, x_full, w):
    b, s = core // 4, core % 4
    N = 4 * SEG
    t0 = s * SEG
    G = SEG + 4
    xg = np.zeros((G, 1024), np.float32)
    lo, hi = t0 - 1, t0 + SEG + 1
    slo, shi = max(lo, 0), min(hi, N)
    xg[slo - lo:shi - lo, :] = x_full[b, slo:shi, :]
    d = {
        "x": np.ascontiguousarray(xg.T),
        "wq_t": np.ascontiguousarray(w["Wq"].T),
        "wk_t": np.ascontiguousarray(w["Wk"].T),
        "wv_t": np.ascontiguousarray(w["Wv"].T),
        "wg_t": np.ascontiguousarray(w["Wg"].T),
        "wgk1_t": np.ascontiguousarray(w["Wgk1"].T),
        "wgk2_t": np.ascontiguousarray(w["Wgk2"].T),
        "bgk2": np.ascontiguousarray(w["bgk2"]),
        "wout_t": np.ascontiguousarray(w["Wout"].T),
        "w1_t": np.ascontiguousarray(w["w1"].T),
        "w2_t": np.ascontiguousarray(w["w2"].T),
        "conv_w": np.ascontiguousarray(w["conv_w"][:, 0, :].T.reshape(3, -1) if w["conv_w"].ndim == 3 else w["conv_w"]),
        "conv_b": np.ascontiguousarray(w["conv_b"]),
        "rms_w": np.ascontiguousarray(w["rms_w"]),
    }
    cw = w["conv_w"].reshape(4096, 3)
    d["conv_w"] = np.ascontiguousarray(cw.T)
    ctl = np.zeros(32, np.float32)
    g = s
    for j in range(4):
        for l in range(4):
            ctl[4 * j + l] = 1.0 if (j <= l < g) else 0.0
    for j in range(4):
        ctl[16 + j] = 0.0 if j < g else NEG
    for j in range(4):
        ctl[20 + j] = 1.0 if j == g - 1 else 0.0
        ctl[24 + j] = 1.0 if j == g + 1 else 0.0
    d["ctl"] = np.tile(ctl[None, :], (128, 1)).astype(np.float32)
    return d


def kernel(L0_attn_x, L0_attn_Wq, L0_attn_Wk, L0_attn_Wv, L0_attn_Wg,
           L0_attn_Wgk1, L0_attn_Wgk2, L0_attn_bgk2, L0_attn_Wout, L0_attn_rms_w,
           L0_ffn_w1, L0_ffn_conv_w, L0_ffn_conv_b, L0_ffn_w2):
    SEG = 2048
    x_full = np.asarray(L0_attn_x, np.float32)
    w = {
        "Wq": np.asarray(L0_attn_Wq, np.float32), "Wk": np.asarray(L0_attn_Wk, np.float32),
        "Wv": np.asarray(L0_attn_Wv, np.float32), "Wg": np.asarray(L0_attn_Wg, np.float32),
        "Wgk1": np.asarray(L0_attn_Wgk1, np.float32), "Wgk2": np.asarray(L0_attn_Wgk2, np.float32),
        "bgk2": np.asarray(L0_attn_bgk2, np.float32), "Wout": np.asarray(L0_attn_Wout, np.float32),
        "rms_w": np.asarray(L0_attn_rms_w, np.float32),
        "w1": np.asarray(L0_ffn_w1, np.float32), "conv_w": np.asarray(L0_ffn_conv_w, np.float32),
        "conv_b": np.asarray(L0_ffn_conv_b, np.float32), "w2": np.asarray(L0_ffn_w2, np.float32),
    }
    nc = build_program(SEG=SEG)
    in_maps = [_prep_core_inputs(c, SEG, x_full, w) for c in range(8)]
    res = run_bass_kernel_spmd(nc, in_maps, core_ids=list(range(8)))
    out = np.empty((2, 8192, 1024), np.float32)
    for c in range(8):
        b, s = c // 4, c % 4
        out[b, s * SEG:(s + 1) * SEG, :] = res.results[c]["y"].T
    return out


# version 13
# speedup vs baseline: 1.0187x; 1.0041x over previous
import numpy as np
import concourse.bass as bass
import concourse.mybir as mybir
from concourse.bass_utils import run_bass_kernel_spmd

F32 = mybir.dt.float32
BF16 = mybir.dt.bfloat16
F16 = mybir.dt.float16
I8 = mybir.dt.int8
AF = mybir.ActivationFunctionType
ALU = mybir.AluOpType
AXX = mybir.AxisListType.X

D, DH, DK, DV, H, HQK, HV = 1024, 4096, 1024, 2048, 4, 256, 512
KD, MH, MV = D // 128, DH // 128, DV // 128
LN16 = float(np.log(1.0 / 16.0))
EPS = 1e-5
NEG = -1.0e20
MAGIC = 12582912.0
SZ = {F32: 4, BF16: 2, F16: 2, I8: 1}


def ceil_div(a, b):
    return (a + b - 1) // b


def rnd64(x):
    return (x + 63) // 64 * 64


def build_program(SEG=2048, taps=()):
    _, st1 = _build(SEG, taps, None, False)
    nc, _ = _build(SEG, taps, st1, True)
    return nc


def _build(SEG=2048, taps=(), st_seed=None, merge=False):
    OC = SEG // 256
    NJ = SEG // 128
    G = SEG + 4
    GV = SEG + 2
    NT = ceil_div(GV, 512)
    TT = [(i * 512, min(512, GV - i * 512)) for i in range(NT)]
    NTT = max(1, SEG // 512)
    TK = [(1 + i * 512, min(512, SEG - i * 512)) for i in range(NTT)]
    NS = ceil_div(GV, 128)
    NSS = SEG // 128
    NPAIR = max(1, SEG // 512)
    PW = SEG // NPAIR

    nc = bass.Bass("TRN2", target_bir_lowering=False)

    def din(name, shape):
        return nc.dram_tensor(name, list(shape), F32, kind="ExternalInput").ap()

    x_in = din("x", [D, G])
    wq_t, wk_t = din("wq_t", [D, DK]), din("wk_t", [D, DK])
    wv_t, wg_t = din("wv_t", [D, DV]), din("wg_t", [D, DV])
    wgk1_t, wgk2_t = din("wgk1_t", [D, 32]), din("wgk2_t", [32, DK])
    bgk2_i = din("bgk2", [DK])
    wout_t = din("wout_t", [DV, D])
    w1_t, w2_t = din("w1_t", [D, DH]), din("w2_t", [DH, D])
    conv_w_i, conv_b_i = din("conv_w", [3, DH]), din("conv_b", [DH])
    rms_w_i = din("rms_w", [HV])
    ctl_i = din("ctl", [128, 32])

    y_out = nc.dram_tensor("y", [D, SEG], F32, kind="ExternalOutput").ap()
    tap_t = {}
    for name, shape in taps:
        tap_t[name] = nc.dram_tensor("tap_" + name, list(shape), F32, kind="ExternalOutput").ap()

    w1q_d = nc.dram_tensor("w1q_d", [MH * KD * 128 * 128], BF16).ap()
    w2q_d = nc.dram_tensor("w2q_d", [KD * MH * 128 * 128], BF16).ap()
    h2_d = nc.dram_tensor("h2_d", [MH, 128, SEG], BF16).ap()
    o_d = nc.dram_tensor("o_d", [MV, 128, SEG], BF16).ap()
    qt_d = nc.dram_tensor("qt_d", [KD, 128, SEG], BF16).ap()
    v_d = nc.dram_tensor("v_d", [NJ, 128, DV], BF16).ap()
    gate_d = nc.dram_tensor("gate_d", [MV, 128, SEG], BF16).ap()
    x1_d = nc.dram_tensor("x1_d", [D, G], F32).ap()
    cc_s = nc.dram_tensor("cc_s", [H * 257 * 512], F32).ap()
    cc_r = nc.dram_tensor("cc_r", [4 * H * 257 * 512], F32).ap()
    cc2_s = nc.dram_tensor("cc2_s", [2 * D], F32).ap()
    cc2_r = nc.dram_tensor("cc2_r", [4 * 2 * D], F32).ap()

    psA = nc.alloc_psum_tensor("psA", [128, 512], F32)
    psB = nc.alloc_psum_tensor("psB", [128, 512], F32)
    psC = nc.alloc_psum_tensor("psC", [128, 512], F32)
    psD = nc.alloc_psum_tensor("psD", [128, 512], F32)
    psE = nc.alloc_psum_tensor("psE", [128, 512], F32)
    psF = nc.alloc_psum_tensor("psF", [128, 512], F32)
    psG = nc.alloc_psum_tensor("psG", [128, 512], F32)
    psT = nc.alloc_psum_tensor("psT", [128, 1024], BF16)

    SB = nc.alloc_sbuf_tensor
    ident = SB("ident", [128, 128], BF16)
    identf = SB("identf", [128, 128], F32)
    maskJT = SB("maskJT", [128, 128], BF16)
    ones_b = SB("ones_b", [128, 128], BF16)
    onesf = SB("onesf", [128, 128], F32)
    ones1f = SB("ones1f", [1, 128], F32)
    ctl_sb = SB("ctl_sb", [128, 32], F32)
    convb_sb = SB("convb_sb", [128, MH], F32)
    convw_sb = SB("convw_sb", [128, 3 * MH], F32)
    convw_s = SB("convw_s", [128, 3 * MH], F32)
    rmsw_sb = SB("rmsw_sb", [128, 4], F32)
    bgk2_sb = SB("bgk2_sb", [128, KD], F32)
    wsum = SB("wsum", [128, 70], F32)
    scal = SB("scal", [1, 16], F32)
    scalb = SB("scalb", [128, 16], F32)
    Brun = SB("Brun", [128, KD], F32)
    ebias = SB("ebias", [128, 2 * KD], F32)
    stat_tok = SB("stat_tok", [128, 40], F32)
    rows8 = SB("rows8", [128, 8 * KD], F32)
    Lg = SB("Lg", [128, 4 * KD], F32)
    eacc = SB("eacc", [128, KD], F32)
    ecol = SB("ecol", [128, KD], F32)
    sjb = SB("sjb", [128, 512], F32)
    wst = [SB(f"wst{i}", [128, 1024], BF16) for i in range(2)]
    xbfr = SB("xbfr", [128, 1024], BF16)
    x2br = SB("x2br", [128, 1024], BF16)
    wgk1a = [SB(f"wgk1a{k}", [128, 32], BF16) for k in range(KD)]
    wgk2a = SB("wgk2a", [32, DK], BF16)
    zz256 = SB("zz256", [128, 256], F16)
    Abuf = SB("Abuf", [128, 3 * 128], BF16)
    hsel = SB("hsel", [128, 16], F32)
    ytmp = SB("ytmp", [128, 512], F32)
    epsc = SB("epsc", [128, 1], F32)
    zzf2 = SB("zzf2", [128, 2], F32)
    rowsc = SB("rowsc", [1, 512], F32)
    rowsc2 = SB("rowsc2", [1, 512], F32)
    o2scr = SB("o2scr", [128, 512], BF16)
    wvev = SB("wvev", [128, 512], BF16)

    base0 = rnd64(nc.sbuf_base)
    top = nc.sbuf_top

    def at(name, shape, dtype, off):
        per = int(np.prod(shape[1:])) * SZ[dtype]
        assert base0 + off + per <= top, f"{name}: off {off}+{per} exceeds {top - base0}"
        return nc.alloc_sbuf_tensor_at(name, list(shape), dtype, offset=base0 + off)

    RX = 0
    GB = rnd64(G * 4)
    xg = [at(f"xg{k}", [128, G], F32, RX + k * GB) for k in range(KD)]
    SGB = rnd64(SEG * 2)
    g16 = [at(f"g16_{k}", [128, SEG], F16, RX + k * SGB) for k in range(KD)]
    wqa = [at(f"wqa{k}", [128, DK], BF16, RX + 8 * SGB + k * 2048) for k in range(KD)]
    wka = [at(f"wka{k}", [128, DK], BF16, RX + 8 * SGB + 16384 + k * 2048) for k in range(KD)]
    RXE = max(8 * GB, 8 * SGB + 32768)

    RA = RXE
    GVB = rnd64(GV * 4)
    GV2B = rnd64(GV * 2)
    SEGB2 = rnd64(SEG * 2)
    sT1 = at("sT1", [128, GV], F32, RA + 0)
    sT2 = at("sT2", [128, GV], F32, RA + GVB)
    sScr = at("sScr", [128, GV], F32, RA + 2 * GVB)
    maxP = at("maxP", [128, NS * 128], BF16, RA + 3 * GVB)
    minP = at("minP", [128, NS * 128], BF16, RA + 3 * GVB + rnd64(NS * 256))
    hB = [at(f"hB{m}", [128, GV], BF16, RA + m * GV2B) for m in range(8)]
    OF1 = max(8 * GV2B, MH * 1024)
    h2t = at("h2t", [128, SEG], BF16, RA + OF1)
    habs = at("habs", [128, SEG], BF16, RA + OF1 + SEGB2)
    hmax = at("hmax", [128, SEG], BF16, RA + OF1 + 2 * SEGB2)
    OF2 = OF1 + 3 * SEGB2
    diag = [at(f"diag{i}", [128, 128], BF16, RA + OF2 + i * 256) for i in range(24)]
    OF3 = OF2 + 16384
    sT3 = at("sT3", [128, GV], F32, RA + OF3)
    sT4 = at("sT4", [128, GV], F32, RA + OF3 + GVB)
    RAE = RA + OF3 + 2 * GVB
    assert rnd64(PW * 2) == 1024
    xq2A = [at(f"xq2A_{k}", [128, PW], BF16, RA + k * 1024) for k in range(MH)]
    eqc = at("eqc", [128, KD * 256], BF16, RA + 0)
    ekc = at("ekc", [128, KD * 256], BF16, RA + 4224)
    qtc = at("qtc", [128, KD * 256], BF16, RA + 8448)
    ktc = at("ktc", [128, KD * 256], BF16, RA + 12672)
    ktok = at("ktok", [128, 2 * DK], BF16, RA + 16896)
    csc = at("csc", [128, KD * 256], F32, RA + 21120)
    och = at("och", [128, MV * 256], BF16, RA + 29312)
    vtk = [at(f"vtk{i}", [128, 2 * DV], BF16, RA + 37504 + i * 8192) for i in range(2)]
    T32 = at("T32", [128, KD * 512], F32, RA + 53888)
    T16 = at("T16", [128, KD * 512], BF16, RA + 70272)
    ALE = 70272 + 8192
    u_bf = at("u_bf", [32, SEG], BF16, RA + 0)
    sgs = [at(f"sgs_{k}", [128, SEG // 2], F32, RA + 4224 + k * rnd64(SEG * 2)) for k in range(KD)]
    wva = [at(f"wva{k}", [128, DV], BF16, RA + 4224 + 8 * rnd64(SEG * 2) + k * 4096) for k in range(KD)]
    vtmp = at("vtmp", [128, DV], BF16, RA + 4224 + 8 * rnd64(SEG * 2) + 32768)
    APRE = 4224 + 8 * rnd64(SEG * 2) + 32768 + 4096
    Sin = at("Sin", [128, KD * 512], F32, RA + 0)
    msb_t = at("msb_t", [128, H * 256], F32, RA + 4224)
    wga = [at(f"wga{k}", [128, DV], BF16, RA + 37504 + k * 4096) for k in range(KD)]
    woa = [at(f"woa{k}", [128, D], BF16, RA + 37504 + k * 2048) for k in range(MV)]
    gch = at("gch", [128, MV * 256], BF16, RA + 70272)
    S16b = at("S16b", [128, KD * 512], BF16, RA + 16896)
    RAE2 = ALE

    RB = RA + max(RAE - RA, RAE2, APRE, MH * rnd64(PW * 2))
    xq = [at(f"xq{k}", [128, GV], BF16, RB + k * GV2B) for k in range(KD)]
    w2sb = [at(f"w2sb{i}", [128, 4096], BF16, RB + i * 8192) for i in range(2)]
    _xq2b_off = lambda k: (RA + OF1 + k * 1024) if k < 28 else (RB + 16448 + (k - 28) * 1024)
    xq2B = [at(f"xq2B_{k}", [128, PW], BF16, _xq2b_off(k)) for k in range(MH)]
    h2u = [at(f"h2u{i}", [128, PW], BF16, RB + 20544 + i * 1024) for i in range(2)]
    xn2 = [at(f"xn2_{k}", [128, SEG], BF16, RB + k * SGB) for k in range(KD)]
    FZ = RB + 8 * GV2B
    NBW1 = 4 if 2 * GV2B >= 8192 else 2
    _wq32_off = [RA + OF2 + 6144, RA + OF2 + 10240, RB + 6 * GV2B, RB + 6 * GV2B + 4096]
    wq32d = [at(f"wq32d{i}", [128, 1024], F32, _wq32_off[i]) for i in range(NBW1)]
    _statE = rnd64(3 * GVB + 2 * rnd64(NS * 256))
    w2q32d = [at(f"w2q32d{i}", [128, 1024], F32, RA + max(OF1, _statE) + i * 4096) for i in range(2)]
    wqbd = at("wqbd", [128, 2048], BF16, FZ)
    absb = at("absb", [128, 1024], BF16, FZ + 4096)
    RBE = FZ + 6144
    assert base0 + RBE <= top, f"total arena {RBE} over {top - base0}"

    class Sem:
        def __init__(self, name, dk=1):
            self.h = nc.alloc_semaphore(name)
            self.n = 0
            self.name = name
            self.dk = dk

        def inc(self, inst, k=None):
            k = self.dk if k is None else k
            inst.then_inc(self.h, k)
            self.n += k
            return inst

    sTE, sVE, sSE, sGC, sCC = (Sem(x) for x in ("sTE", "sVE", "sSE", "sGC", "sCC"))

    marks = {}

    def need(e, en, *deps):
        for d in deps:
            if not d:
                continue
            if isinstance(d, list):
                need(e, en, *d)
                continue
            sem, v = d
            if v is None or v <= 0:
                continue
            key = (en, sem.name)
            if marks.get(key, 0) < v:
                own = {"tensor": sTE, "vector": sVE, "scalar": sSE}.get(en)
                if own is not None and sem is own and v > sem.n:
                    import sys as _s
                    print(f"SELF-FUTURE WAIT: {en} waits {sem.name}>={v} at {sem.n}", file=_s.stderr)
                e.wait_ge(sem.h, v)
                marks[key] = v

    class DQ:

        def __init__(self, name):
            self.s = [Sem(name + "0", 16), Sem(name + "1", 16)]
            self.k = 0
            self.last_end = [0, 0]

        def begin(self, e, en):
            p = self.k % 2
            need(e, en, (self.s[p], self.last_end[p]))

        def inc(self, inst):
            self.s[self.k % 2].inc(inst)
            return inst

        def end(self):
            p = self.k % 2
            tok = (self.s[p], self.s[p].n)
            self.last_end[p] = self.s[p].n
            self.k += 1
            return tok

        def drain(self, e, en):
            need(e, en, (self.s[0], self.s[0].n), (self.s[1], self.s[1].n))

        def fence(self):
            return [(self.s[0], self.s[0].n), (self.s[1], self.s[1].n)]

    qIN, qGP, qOUT = DQ("qIN"), DQ("qGP"), DQ("qOUT")

    block_cm = nc.Block()
    block = block_cm.__enter__()
    equeue = {k: [] for k in ("sync", "gpsimd", "tensor", "vector", "scalar")}

    def on(ename, f):
        if merge:
            equeue[ename].append(f)
        else:
            getattr(block, ename)(lambda e, f=f, ename=ename: f(e, ename))

    def _tokval(v):
        if isinstance(v, tuple):
            return ("t", v[0].name, v[1])
        if isinstance(v, list):
            return tuple(_tokval(x) for x in v)
        return v

    if st_seed:
        sem_by_name = {s.name: s for s in (sTE, sVE, sSE, sGC, sCC)}
        for q in (qIN, qGP, qOUT):
            for s in q.s:
                sem_by_name[s.name] = s

        def _remap(v):
            if isinstance(v, tuple):
                return (sem_by_name[v[0].name], v[1])
            if isinstance(v, list):
                return [_remap(x) for x in v]
            return v

        class _ChkDict(dict):
            def __setitem__(self, k, v):
                if k in self and _tokval(self[k]) != _tokval(v):
                    raise AssertionError(f"two-pass mismatch at {k}: {_tokval(self[k])} -> {_tokval(v)}")
                dict.__setitem__(self, k, _remap(v) if True else v)
        st = _ChkDict({k: _remap(v) for k, v in st_seed.items()})
    else:
        st = {}

    def ph0_sync(e, en):
        for k in range(KD):
            qIN.begin(e, en)
            qIN.inc(e.dma_start(out=xg[k][:, :], in_=x_in[k * 128:(k + 1) * 128, :]))
        st["x_loaded"] = qIN.end()
        qIN.begin(e, en)
        qIN.inc(e.dma_start(out=ctl_sb[:, :], in_=ctl_i))
        for m in range(MH):
            qIN.inc(e.dma_start(out=convb_sb[:, m:m + 1], in_=conv_b_i[m * 128:(m + 1) * 128].rearrange("(p o) -> p o", o=1)))
        for tp in range(3):
            for m in range(MH):
                qIN.inc(e.dma_start(out=convw_sb[:, tp * MH + m:tp * MH + m + 1], in_=conv_w_i[tp, m * 128:(m + 1) * 128].rearrange("(p o) -> p o", o=1)))
        for c in range(4):
            qIN.inc(e.dma_start(out=rmsw_sb[:, c:c + 1], in_=rms_w_i[c * 128:(c + 1) * 128].rearrange("(p o) -> p o", o=1)))
        for k in range(KD):
            qIN.inc(e.dma_start(out=bgk2_sb[:, k:k + 1], in_=bgk2_i[k * 128:(k + 1) * 128].rearrange("(p o) -> p o", o=1)))
        st["consts_loaded"] = qIN.end()

    def ph0_gp(e, en):
        sGC.inc(e.memset(ident[:, :], 0.0))
        sGC.inc(e.memset(identf[:, :], 0.0))
        sGC.inc(e.memset(maskJT[:, :], 0.0))
        need(e, en, (sGC, sGC.n))
        sGC.inc(e.affine_select(out=ident[:, :], in_=ident[:, :], compare_op=ALU.not_equal,
                                fill=1.0, base=0, pattern=[[-1, 128]], channel_multiplier=1))
        sGC.inc(e.affine_select(out=identf[:, :], in_=identf[:, :], compare_op=ALU.not_equal,
                                fill=1.0, base=0, pattern=[[-1, 128]], channel_multiplier=1))
        sGC.inc(e.affine_select(out=maskJT[:, :], in_=maskJT[:, :], compare_op=ALU.is_gt,
                                fill=1.0, base=0, pattern=[[-1, 128]], channel_multiplier=1))
        e.memset(ones_b[:, :], 1.0)
        e.memset(onesf[:, :], 1.0)
        e.memset(ones1f[:, :], 1.0)
        e.memset(zz256[:, :], 0.0)
        e.memset(epsc[:, :], float(EPS))
        e.memset(zzf2[:, :], 0.0)
        e.memset(scal[:, :], 0.0)
        e.memset(wsum[:, :], 0.0)
        e.memset(stat_tok[:, :], 0.0)
        e.memset(rowsc[:, :], 0.0)
        e.memset(rowsc2[:, :], 0.0)
        sGC.inc(e.memset(Brun[:, :], 0.0))
        st["consts_built"] = sGC.n

    on("sync", ph0_sync)
    on("gpsimd", ph0_gp)

    NWT = 32

    def w_src_ap(widx, i):
        if widx == 0:
            return w1_t[(i // 4) * 128:(i // 4 + 1) * 128, (i % 4) * 1024:(i % 4 + 1) * 1024]
        return w2_t[i * 128:(i + 1) * 128, :]

    def w_out_ap(widx, i):
        out_d = w1q_d if widx == 0 else w2q_d
        off = ((i // 4) * 32 + (i % 4) * 8) * 16384 if widx == 0 else i * 8 * 16384
        return out_d[off:off + 8 * 16384].rearrange("(m p q) -> p m q", p=128, q=128)

    def qw_pass1_step(widx, i):
        buf = wq32d if widx == 0 else w2q32d
        nb = NBW1 if widx == 0 else 2
        q, eng = (qIN, "sync") if widx == 0 else (qGP, "gpsimd")

        def f1(e, en, widx=widx, i=i, q=q, buf=buf, nb=nb):
            need(e, en, (sSE, st.get(f"wabs{widx}_{i-nb}", 0)))
            if widx == 1 and i < 2:
                need(e, en, (sSE, st[f"wabs0_{NWT-2+i}"]))
            q.begin(e, en)
            q.inc(e.dma_start(out=buf[i % nb][:, :], in_=w_src_ap(widx, i)))
            st[f"wl{widx}_{i}"] = q.end()

        def f1v(e, en, widx=widx, i=i, buf=buf, nb=nb):
            need(e, en, st[f"wl{widx}_{i}"], (sGC, st["consts_built"]))
            need(e, en, (sSE, sSE.n))
            sSE.inc(e.activation(absb[:, :], buf[i % nb][:, :], AF.Abs,
                                 accum_out=wsum[:, widx * 32 + i:widx * 32 + i + 1]))
            st[f"wabs{widx}_{i}"] = sSE.n
        on(eng, f1)
        on("scalar", f1v)

    def qw_scale(widx):
        def p1f(e, en):
            need(e, en, (sSE, st[f"wabs{widx}_{NWT-1}"]))
            need(e, en, (sVE, sVE.n))
            sVE.inc(e.tensor_reduce(wsum[:, 64 + widx:65 + widx], wsum[:, widx * 32:widx * 32 + NWT], AXX, ALU.add))
            st[f"wcol{widx}"] = sVE.n
        on("vector", p1f)

        def p2t(e, en):
            need(e, en, (sVE, st[f"wcol{widx}"]), (sGC, st["consts_built"]))
            sTE.inc(e.matmul(psF[0:1, 0:128], wsum[:, 64 + widx:65 + widx], onesf[:, 0:128], start=True, stop=True))
            st[f"wtot{widx}"] = sTE.n
        on("tensor", p2t)

        def p3v(e, en):
            need(e, en, (sTE, st[f"wtot{widx}"]), (sGC, st["consts_built"]))
            numel = float(DH * D)
            sVE.inc(e.tensor_scalar(scal[0:1, widx:widx + 1], psF[0:1, 0:1], 1.0 / numel, 1e-5, ALU.mult, ALU.max))
            need(e, en, (sVE, sVE.n))
            sVE.inc(e.reciprocal(scal[0:1, 2 + widx:3 + widx], scal[0:1, widx:widx + 1]))
            st[f"wrec{widx}"] = sVE.n
        on("vector", p3v)

        def p4t(e, en):
            need(e, en, (sVE, st[f"wrec{widx}"]))
            sTE.inc(e.matmul(psF[0:128, 0:16], ones1f[0:1, :], scal[0:1, 0:16], start=True, stop=True))
            st[f"wbc{widx}"] = sTE.n
        on("tensor", p4t)

        def p4v(e, en):
            need(e, en, (sTE, st[f"wbc{widx}"]))
            sVE.inc(e.tensor_copy(scalb[:, 0:16], psF[0:128, 0:16]))
            st[f"wscal{widx}"] = sVE.n
        on("vector", p4v)

    def qw_pass2_step(widx, i):
        buf = wq32d
        nb = NBW1 if widx == 0 else 2
        q = qIN if widx == 0 else qOUT

        def p5s(e, en, widx=widx, i=i, buf=buf, q=q, nb=nb):
            need(e, en, (sVE, st[f"wcol{widx}"]))
            if i >= nb:
                need(e, en, (sVE, st[f"wqd{widx}_{i-nb}"]))
            elif widx == 1:
                need(e, en, (sVE, st[f"wqd0_{NWT-1}"]))
            q.begin(e, en)
            q.inc(e.dma_start(out=buf[i % nb][:, :], in_=w_src_ap(widx, i)))
            st[f"wl2{widx}_{i}"] = q.end()

        def p5v(e, en, widx=widx, i=i, buf=buf, nb=nb):
            need(e, en, st[f"wl2{widx}_{i}"], (sVE, st[f"wscal{widx}"]))
            need(e, en, (sVE, sVE.n))
            sVE.inc(e.tensor_scalar(buf[i % nb][:, :], buf[i % nb][:, :], scalb[:, 2 + widx:3 + widx], 1.49, ALU.mult, ALU.min))
            need(e, en, (sVE, sVE.n))
            sVE.inc(e.tensor_scalar(buf[i % nb][:, :], buf[i % nb][:, :], -1.49, MAGIC, ALU.max, ALU.add))
            need(e, en, (sVE, sVE.n))
            if i >= 2:
                need(e, en, st[f"wstr{widx}_{i-2}"])
            elif widx == 1:
                need(e, en, st[f"wstr0_{NWT-2+i}"])
            sVE.inc(e.tensor_scalar(wqbd[:, (i % 2) * 1024:(i % 2 + 1) * 1024], buf[i % nb][:, :], -MAGIC, None, ALU.add))
            st[f"wqd{widx}_{i}"] = sVE.n

        def p5g(e, en, widx=widx, i=i):
            need(e, en, (sVE, st[f"wqd{widx}_{i}"]))
            qGP.begin(e, en)
            qGP.inc(e.dma_start(
                out=w_out_ap(widx, i),
                in_=wqbd[:, (i % 2) * 1024:(i % 2 + 1) * 1024].rearrange("p (m q) -> p m q", q=128)))
            st[f"wstr{widx}_{i}"] = qGP.end()
        on("sync", p5s)
        on("vector", p5v)
        on("gpsimd", p5g)

    for i in range(NWT):
        qw_pass1_step(0, i)
    qw_scale(0)
    for i in range(NWT):
        qw_pass2_step(0, i)
        qw_pass1_step(1, i)
    st["wready0"] = [st[f"wstr0_{NWT-2}"], st[f"wstr0_{NWT-1}"]]


    def convscale(e, en):
        need(e, en, st["consts_loaded"], (sVE, st["wscal0"]))
        sVE.inc(e.tensor_scalar(convw_s[:, :], convw_sb[:, :], scalb[:, 0:1], None, ALU.mult))
        st["convw_scaled"] = sVE.n
    on("vector", convscale)

    def ffn_stats(app, with_quant=True, dep=()):
        a = f"f{app}"

        def sv(e, en):
            need(e, en, st["x_loaded"], *dep)
            if with_quant:
                if NS * 128 > GV:
                    e.memset(maxP[:, GV:NS * 128], 0.0)
                    e.memset(minP[:, GV:NS * 128], 0.0)
                sVE.inc(e.tensor_copy(maxP[:, 0:GV], xg[0][:, 0:GV]))
                sVE.inc(e.tensor_copy(minP[:, 0:GV], xg[0][:, 0:GV]))
                for k in range(1, KD):
                    need(e, en, (sVE, sVE.n))
                    sVE.inc(e.tensor_tensor(maxP[:, 0:GV], xg[k][:, 0:GV], maxP[:, 0:GV], ALU.max))
                    sVE.inc(e.tensor_tensor(minP[:, 0:GV], xg[k][:, 0:GV], minP[:, 0:GV], ALU.min))
                st[a + "trees"] = sVE.n
        on("vector", sv)

        for ti, (c0, w) in enumerate(TT):
            for k in range(KD):
                r = k % 2

                def s1v(e, en, ti=ti, c0=c0, w=w, k=k, r=r):
                    need(e, en, st["x_loaded"], *dep)
                    g = ti * KD + k - 2
                    if g >= 0:
                        pt, pk = divmod(g, KD)
                        need(e, en, (sTE, st.get(f"{a}smm{pt}_{pk}", 0)))
                    sVE.inc(e.tensor_copy(xbfr[:, r * 512:r * 512 + w], xg[k][:, c0:c0 + w]))
                    st[f"{a}sxb{ti}_{k}"] = sVE.n

                def s1se(e, en, ti=ti, c0=c0, w=w, k=k, r=r):
                    need(e, en, st["x_loaded"], *dep)
                    g = ti * KD + k - 2
                    if g >= 0:
                        pt, pk = divmod(g, KD)
                        need(e, en, (sTE, st.get(f"{a}smm{pt}_{pk}", 0)))
                    sSE.inc(e.activation(x2br[:, r * 512:r * 512 + w], xg[k][:, c0:c0 + w], AF.Square))
                    st[f"{a}sx2{ti}_{k}"] = sSE.n

                def s1t(e, en, ti=ti, c0=c0, w=w, k=k, r=r):
                    if k == 0 and ti >= 1:
                        need(e, en, (sVE, st[f"{a}var{ti-1}"]))
                    need(e, en, (sVE, st[f"{a}sxb{ti}_{k}"]), (sSE, st[f"{a}sx2{ti}_{k}"]))
                    e.matmul(psA[:, 0:w], ones_b[:, :], xbfr[:, r * 512:r * 512 + w],
                             start=(k == 0), stop=(k == KD - 1))
                    mm = e.matmul(psB[:, 0:w], ones_b[:, :], x2br[:, r * 512:r * 512 + w],
                                  start=(k == 0), stop=(k == KD - 1))
                    sTE.inc(mm)
                    st[f"{a}smm{ti}_{k}"] = sTE.n
                on("vector", s1v)
                on("scalar", s1se)
                on("tensor", s1t)
            st[f"{a}stats{ti}"] = st[f"{a}smm{ti}_{KD-1}"]

            def s2v(e, en, ti=ti, c0=c0, w=w):
                need(e, en, (sTE, st[f"{a}stats{ti}"]), *dep)
                sVE.inc(e.tensor_scalar(sScr[:, c0:c0 + w], psA[:, 0:w], 1.0 / D, None, ALU.mult))
                need(e, en, (sVE, sVE.n))
                sVE.inc(e.tensor_mul(sT1[:, c0:c0 + w], sScr[:, c0:c0 + w], sScr[:, c0:c0 + w]))
                need(e, en, (sVE, sVE.n))
                sVE.inc(e.scalar_tensor_tensor(sT2[:, c0:c0 + w], psB[:, 0:w], 1.0 / D, sT1[:, c0:c0 + w], ALU.mult, ALU.subtract))
                st[f"{a}var{ti}"] = sVE.n
            on("vector", s2v)

        def s3se(e, en):
            need(e, en, (sVE, st[f"{a}var{NT-1}"]))
            need(e, en, (sGC, st["consts_built"]))
            sSE.inc(e.activation(sT2[:, 0:GV], sT2[:, 0:GV], AF.Ln, bias=epsc[:, 0:1]))
            need(e, en, (sSE, sSE.n))
            sSE.inc(e.activation(sT2[:, 0:GV], sT2[:, 0:GV], AF.Exp, scale=-0.5))
            st[a + "rstd"] = sSE.n
        on("scalar", s3se)

        if not with_quant:
            return

        for i in range(NS):
            def s4t(e, en, i=i):
                need(e, en, (sVE, st[a + "trees"]))
                need(e, en, (sVE, st.get(f"{a}tred{i-1}", 0)))
                e.transpose(psT[:, 0:128], maxP[:, i * 128:(i + 1) * 128], ident[:, :])
                sTE.inc(e.transpose(psT[:, 128:256], minP[:, i * 128:(i + 1) * 128], ident[:, :]))
                st[f"{a}tp{i}"] = sTE.n

            def s4v(e, en, i=i):
                need(e, en, (sTE, st[f"{a}tp{i}"]))
                e.tensor_reduce(stat_tok[:, i:i + 1], psT[:, 0:128], AXX, ALU.max)
                sVE.inc(e.tensor_reduce(stat_tok[:, 20 + i:21 + i], psT[:, 128:256], AXX, ALU.min))
                st[f"{a}tred{i}"] = sVE.n
            on("tensor", s4t)
            on("vector", s4v)

        def s4f(e, en):
            need(e, en, (sVE, st[f"{a}tred{NS-1}"]))
            st[a + "treds"] = sVE.n
        on("vector", s4f)

        NR = ceil_div(NS, 4)
        for rnd in range(NR):
            tl = list(range(rnd * 4, min(rnd * 4 + 4, NS)))
            def b1t(e, en, rnd=rnd, tl=tl):
                need(e, en, (sVE, st[a + "treds"]))
                need(e, en, (sVE, st.get(f"{a}rowev{rnd-1}", 0)))
                for ii, i in enumerate(tl):
                    ins = e.matmul(psF[0:1, (i % 4) * 128:(i % 4) * 128 + 128], stat_tok[:, i:i + 1], identf[:, :],
                                   start=(ii == 0), stop=(ii == len(tl) - 1))
                for ii, i in enumerate(tl):
                    ins = e.matmul(psG[0:1, (i % 4) * 128:(i % 4) * 128 + 128], stat_tok[:, 20 + i:21 + i], identf[:, :],
                                   start=(ii == 0), stop=(ii == len(tl) - 1))
                sTE.inc(ins)
                st[f"{a}rowmm{rnd}"] = sTE.n
            def b1v(e, en, rnd=rnd, tl=tl):
                need(e, en, (sTE, st[f"{a}rowmm{rnd}"]))
                e.tensor_copy(rowsc[0:1, 0:len(tl) * 128], psF[0:1, 0:len(tl) * 128])
                sVE.inc(e.tensor_copy(rowsc2[0:1, 0:len(tl) * 128], psG[0:1, 0:len(tl) * 128]))
                st[f"{a}rowev{rnd}"] = sVE.n
            def b2t(e, en, rnd=rnd, tl=tl):
                need(e, en, (sVE, st[f"{a}rowev{rnd}"]))
                need(e, en, (sVE, st.get(f"{a}bcev{rnd-1}", 0)))
                for ii, i in enumerate(tl):
                    e.matmul(psE[:, (i % 4) * 128:(i % 4) * 128 + 128], ones1f[0:1, :], rowsc[0:1, (i % 4) * 128:(i % 4) * 128 + 128],
                             start=(ii == 0), stop=(ii == len(tl) - 1))
                    ins = e.matmul(psD[:, (i % 4) * 128:(i % 4) * 128 + 128], ones1f[0:1, :], rowsc2[0:1, (i % 4) * 128:(i % 4) * 128 + 128],
                                   start=(ii == 0), stop=(ii == len(tl) - 1))
                sTE.inc(ins)
                st[f"{a}bcmm{rnd}"] = sTE.n
            def b2v(e, en, rnd=rnd, tl=tl):
                need(e, en, (sTE, st[f"{a}bcmm{rnd}"]))
                for i in tl:
                    w = min(128, GV - i * 128)
                    if w <= 0:
                        continue
                    e.tensor_copy(sT3[:, i * 128:i * 128 + w], psE[:, (i % 4) * 128:(i % 4) * 128 + w])
                    ins = e.tensor_copy(sT4[:, i * 128:i * 128 + w], psD[:, (i % 4) * 128:(i % 4) * 128 + w])
                sVE.inc(ins)
                st[f"{a}bcev{rnd}"] = sVE.n
            on("tensor", b1t)
            on("vector", b1v)
            on("tensor", b2t)
            on("vector", b2v)
        def bfin(e, en):
            need(e, en, (sVE, st[f"{a}bcev{NR-1}"]))
            st[a + "bcasts"] = sVE.n
        on("vector", bfin)

        def s7v(e, en):
            need(e, en, (sVE, st[a + "bcasts"]), (sSE, st[a + "rstd"]))
            sVE.inc(e.tensor_sub(sT3[:, 0:GV], sT3[:, 0:GV], sScr[:, 0:GV]))
            sVE.inc(e.tensor_sub(sT4[:, 0:GV], sScr[:, 0:GV], sT4[:, 0:GV]))
            need(e, en, (sVE, sVE.n))
            sVE.inc(e.tensor_tensor(sT3[:, 0:GV], sT3[:, 0:GV], sT4[:, 0:GV], ALU.max))
            need(e, en, (sVE, sVE.n))
            sVE.inc(e.tensor_tensor(sT3[:, 0:GV], sT3[:, 0:GV], sT2[:, 0:GV], ALU.mult))
            need(e, en, (sVE, sVE.n))
            sVE.inc(e.tensor_scalar(sT3[:, 0:GV], sT3[:, 0:GV], 1e-5, None, ALU.max))
            need(e, en, (sVE, sVE.n))
            sVE.inc(e.tensor_scalar(sT4[:, 0:GV], sT3[:, 0:GV], 1.0 / 127.0, None, ALU.mult))
            sVE.inc(e.reciprocal(sT3[:, 0:GV], sT3[:, 0:GV]))
            need(e, en, (sVE, sVE.n))
            sVE.inc(e.tensor_scalar(sT3[:, 0:GV], sT3[:, 0:GV], 127.0, None, ALU.mult))
            need(e, en, (sVE, sVE.n))
            sVE.inc(e.tensor_tensor(sT3[:, 0:GV], sT3[:, 0:GV], sT2[:, 0:GV], ALU.mult))
            need(e, en, (sVE, sVE.n))
            sVE.inc(e.scalar_tensor_tensor(sT1[:, 0:GV], sScr[:, 0:GV], -1.0, sT3[:, 0:GV], ALU.mult, ALU.mult))
            st[a + "quantsc"] = sVE.n
        on("vector", s7v)

    def ffn_quant_apply(app):
        a = f"f{app}"

        def qv(e, en):
            need(e, en, (sVE, st[a + "quantsc"]))
            if app == 1:
                need(e, en, st.get("tap_xq0", 0), st.get("tap_xn2", 0))
            for k in range(KD):
                need(e, en, (sVE, sVE.n))
                sVE.inc(e.tensor_tensor(sScr[:, 0:GV], xg[k][:, 0:GV], sT3[:, 0:GV], ALU.mult))
                need(e, en, (sVE, sVE.n))
                sVE.inc(e.tensor_tensor(sScr[:, 0:GV], sScr[:, 0:GV], sT1[:, 0:GV], ALU.add))
                need(e, en, (sVE, sVE.n))
                sVE.inc(e.tensor_scalar(sScr[:, 0:GV], sScr[:, 0:GV], 127.0, -128.0, ALU.min, ALU.max))
                need(e, en, (sVE, sVE.n))
                sVE.inc(e.tensor_scalar(sScr[:, 0:GV], sScr[:, 0:GV], MAGIC, -MAGIC, ALU.add, ALU.add))
                need(e, en, (sVE, sVE.n))
                sVE.inc(e.tensor_tensor(xq[k][:, 0:GV], sScr[:, 0:GV], sT4[:, 0:GV], ALU.mult))
                st[f"{a}xq{k}"] = sVE.n
            st[a + "xqall"] = sVE.n
        on("vector", qv)

    def ffn_mm1_conv(app, extra_per_blk=None):
        a = f"f{app}"
        for blk in range(4):
            if extra_per_blk:
                extra_per_blk(blk)
            def dv(e, en, blk=blk):
                need(e, en, (sVE, st["convw_scaled"]), (sGC, st["consts_built"]))
                key = f"{a}cvu{blk-1}" if blk > 0 else (f"f{app-1}cvu3" if app > 0 else None)
                if key:
                    need(e, en, (sTE, st[key]))
                for t in range(3):
                    for mloc in range(8):
                        m = blk * 8 + mloc
                        sVE.inc(e.tensor_scalar(diag[t * 8 + mloc][:, :], ident[:, :], convw_s[:, t * MH + m:t * MH + m + 1], None, ALU.mult))
                st[f"{a}diag{blk}"] = sVE.n
            on("vector", dv)

            for mloc in range(8):
                m = blk * 8 + mloc
                r = m % 2

                def ws(e, en, m=m, r=r):
                    need(e, en, st["wready0"])
                    need(e, en, (sTE, st.get(f"{a}mmw{m-2}", 0)))
                    qIN.begin(e, en)
                    qIN.inc(e.dma_start(
                        out=wst[r][:, :].rearrange("p (k q) -> p k q", q=128),
                        in_=w1q_d.rearrange("(t p q) -> t p q", p=128, q=128)[m::MH, :, :].rearrange("k p q -> p k q")))
                    st[f"{a}wl{m}"] = qIN.end()

                on("sync", ws)

                def mt0(e, en, m=m, r=r):
                    need(e, en, st[f"{a}wl{m}"], (sVE, st[a + "xqall"]))
                on("tensor", mt0)
                for ti, (c0, w) in enumerate(TT):
                    def mt(e, en, m=m, r=r, ti=ti, c0=c0, w=w):
                        ps = psA if ti % 2 == 0 else psB
                        need(e, en, (sVE, st.get(f"{a}hev{m}_{ti-2}", 0)))
                        lp = ti + 2 * ((NT - 1 - ti) // 2)
                        need(e, en, (sVE, st.get(f"{a}hev{m-1}_{lp}", 0)))
                        for k in range(KD):
                            ins = e.matmul(ps[:, 0:w], wst[r][:, k * 128:(k + 1) * 128], xq[k][:, c0:c0 + w],
                                           start=(k == 0), stop=(k == KD - 1))
                        sTE.inc(ins)
                        st[f"{a}hmm{m}_{ti}"] = sTE.n
                        if ti == NT - 1:
                            st[f"{a}mmw{m}"] = sTE.n

                    def ev(e, en, m=m, mloc=mloc, ti=ti, c0=c0, w=w):
                        need(e, en, (sTE, st.get(f"{a}cmm{m-8}_{len(TK)-1}", 0)))
                        ps = psA if ti % 2 == 0 else psB
                        need(e, en, (sTE, st[f"{a}hmm{m}_{ti}"]))
                        sVE.inc(e.tensor_copy(hB[mloc][:, c0:c0 + w], ps[:, 0:w]))
                        st[f"{a}hev{m}_{ti}"] = sVE.n
                        if ti == NT - 1:
                            st[f"{a}hband{m}"] = sVE.n
                    on("tensor", mt)
                    on("vector", ev)

            for mloc in range(8):
                m = blk * 8 + mloc
                for ti, (c0, w) in enumerate(TK):
                    g = m * len(TK) + ti

                    def ct(e, en, m=m, mloc=mloc, blk=blk, ti=ti, c0=c0, w=w, g=g):
                        need(e, en, (sVE, st[f"{a}hband{m}"]), (sVE, st[f"{a}diag{blk}"]))
                        gp = g - 2
                        if gp >= 0:
                            pm, pt = divmod(gp, len(TK))
                            need(e, en, (sSE, st.get(f"{a}cse{pm}_{pt}", 0)))
                        ps = psC if g % 2 == 0 else psD
                        e.matmul(ps[:, 0:w], diag[0 * 8 + mloc][:, :], hB[mloc][:, c0 - 1:c0 - 1 + w], start=True, stop=False)
                        e.matmul(ps[:, 0:w], diag[1 * 8 + mloc][:, :], hB[mloc][:, c0:c0 + w], start=False, stop=False)
                        ins = e.matmul(ps[:, 0:w], diag[2 * 8 + mloc][:, :], hB[mloc][:, c0 + 1:c0 + 1 + w], start=False, stop=True)
                        sTE.inc(ins)
                        st[f"{a}cmm{m}_{ti}"] = sTE.n
                        if mloc == 7 and ti == len(TK) - 1:
                            st[f"{a}cvu{blk}"] = sTE.n

                    def cs(e, en, m=m, ti=ti, c0=c0, w=w, g=g):
                        ps = psC if g % 2 == 0 else psD
                        need(e, en, (sTE, st[f"{a}cmm{m}_{ti}"]))
                        if ti == 0:
                            need(e, en, st.get(f"{a}h2st{m-1}", 0))
                        sSE.inc(e.activation(h2t[:, c0 - 1:c0 - 1 + w], ps[:, 0:w], AF.Silu, bias=convb_sb[:, m:m + 1]))
                        st[f"{a}cse{m}_{ti}"] = sSE.n
                    on("tensor", ct)
                    on("scalar", cs)

                def csa(e, en, m=m):
                    need(e, en, (sSE, st[f"{a}cse{m}_{len(TK)-1}"]))
                    need(e, en, (sVE, st.get(f"{a}hmax{m-1}", 0)))
                    sSE.inc(e.activation(habs[:, 0:SEG], h2t[:, 0:SEG], AF.Abs))
                    st[f"{a}habs{m}"] = sSE.n

                def cv(e, en, m=m):
                    need(e, en, (sSE, st[f"{a}habs{m}"]))
                    if m == 0:
                        sVE.inc(e.tensor_copy(hmax[:, 0:SEG], habs[:, 0:SEG]))
                    else:
                        sVE.inc(e.tensor_tensor(hmax[:, 0:SEG], habs[:, 0:SEG], hmax[:, 0:SEG], ALU.max))
                    st[f"{a}hmax{m}"] = sVE.n

                def cg(e, en, m=m):
                    need(e, en, (sSE, st[f"{a}habs{m}"]))
                    qGP.begin(e, en)
                    qGP.inc(e.dma_start(out=h2_d[m, :, :], in_=h2t[:, 0:SEG]))
                    st[f"{a}h2st{m}"] = qGP.end()
                on("scalar", csa)
                on("vector", cv)
                on("gpsimd", cg)

    def ffn_quant2(app):
        a = f"f{app}"

        for i in range(NSS):
            def q2t(e, en, i=i):
                need(e, en, (sVE, st[f"{a}hmax{MH-1}"]))
                need(e, en, (sVE, st.get(f"{a}q2red{i-1}", 0)))
                sTE.inc(e.transpose(psT[:, 0:128], hmax[:, i * 128:(i + 1) * 128], ident[:, :]))
                st[f"{a}q2tp{i}"] = sTE.n

            def q2v(e, en, i=i):
                need(e, en, (sTE, st[f"{a}q2tp{i}"]))
                sVE.inc(e.tensor_reduce(stat_tok[:, i:i + 1], psT[:, 0:128], AXX, ALU.max))
                st[f"{a}q2red{i}"] = sVE.n
            on("tensor", q2t)
            on("vector", q2v)

        def q2f(e, en):
            need(e, en, (sVE, st[f"{a}q2red{NSS-1}"]))
            st[a + "q2reds"] = sVE.n
        on("vector", q2f)

        NR2 = ceil_div(NSS, 4)
        for rnd in range(NR2):
            tl = list(range(rnd * 4, min(rnd * 4 + 4, NSS)))
            def qb1t(e, en, rnd=rnd, tl=tl):
                need(e, en, (sVE, st[a + "q2reds"]))
                need(e, en, (sVE, st.get(f"{a}q2rowev{rnd-1}", 0)))
                for ii, i in enumerate(tl):
                    ins = e.matmul(psF[0:1, (i % 4) * 128:(i % 4) * 128 + 128], stat_tok[:, i:i + 1], identf[:, :],
                                   start=(ii == 0), stop=(ii == len(tl) - 1))
                sTE.inc(ins)
                st[f"{a}q2rowmm{rnd}"] = sTE.n
            def qb1v(e, en, rnd=rnd, tl=tl):
                need(e, en, (sTE, st[f"{a}q2rowmm{rnd}"]))
                sVE.inc(e.tensor_copy(rowsc[0:1, 0:len(tl) * 128], psF[0:1, 0:len(tl) * 128]))
                st[f"{a}q2rowev{rnd}"] = sVE.n
            def qb2t(e, en, rnd=rnd, tl=tl):
                need(e, en, (sVE, st[f"{a}q2rowev{rnd}"]))
                need(e, en, (sVE, st.get(f"{a}q2bcev{rnd-1}", 0)))
                for ii, i in enumerate(tl):
                    ins = e.matmul(psE[:, (i % 4) * 128:(i % 4) * 128 + 128], ones1f[0:1, :], rowsc[0:1, (i % 4) * 128:(i % 4) * 128 + 128],
                                   start=(ii == 0), stop=(ii == len(tl) - 1))
                sTE.inc(ins)
                st[f"{a}q2bcmm{rnd}"] = sTE.n
            def qb2v(e, en, rnd=rnd, tl=tl):
                need(e, en, (sTE, st[f"{a}q2bcmm{rnd}"]))
                for i in tl:
                    ins = e.tensor_copy(sT3[:, 1 + i * 128:1 + (i + 1) * 128], psE[:, (i % 4) * 128:(i % 4) * 128 + 128])
                sVE.inc(ins)
                st[f"{a}q2bcev{rnd}"] = sVE.n
            on("tensor", qb1t)
            on("vector", qb1v)
            on("tensor", qb2t)
            on("vector", qb2v)

        def q4v(e, en):
            need(e, en, (sVE, st[f"{a}q2bcev{NR2-1}"]))
            need(e, en, (sVE, sVE.n))
            sVE.inc(e.tensor_scalar(sT3[:, 1:1 + SEG], sT3[:, 1:1 + SEG], 1e-5, None, ALU.max))
            need(e, en, (sVE, sVE.n))
            sVE.inc(e.tensor_scalar(sT4[:, 1:1 + SEG], sT3[:, 1:1 + SEG], 0.5 / 127.0, None, ALU.mult))
            need(e, en, (sVE, sVE.n))
            sVE.inc(e.tensor_scalar(sT4[:, 1:1 + SEG], sT4[:, 1:1 + SEG], scalb[:, 1:2], None, ALU.mult))
            sVE.inc(e.reciprocal(sT3[:, 1:1 + SEG], sT3[:, 1:1 + SEG]))
            need(e, en, (sVE, sVE.n))
            sVE.inc(e.tensor_scalar(sT3[:, 1:1 + SEG], sT3[:, 1:1 + SEG], 127.0, None, ALU.mult))
            st[a + "q2sc"] = sVE.n
        on("vector", q4v)

    def ffn_mm2(app):
        a = f"f{app}"

        def emit_quant(pr, ks):
            pc0 = 1 + pr * PW
            xq2 = xq2A if pr % 2 == 0 else xq2B
            for k in ks:
                hb = h2u[k % 2]

                def l2g(e, en, k=k, pr=pr, pc0=pc0, hb=hb):
                    need(e, en, st[f"{a}h2st{k}"], st[f"{a}h2st{MH-1}"], (sSE, st[f"{a}habs{MH-1}"]))
                    need(e, en, (sVE, st.get(f"{a}xq2_{pr}_{k-2}", 0)))
                    if k < 2 and pr > 0:
                        need(e, en, (sVE, st.get(f"{a}xq2_{pr-1}_{MH-2+k}", 0)))
                    qGP.begin(e, en)
                    qGP.inc(e.dma_start(out=hb[:, 0:PW], in_=h2_d[k, :, pc0 - 1:pc0 - 1 + PW]))
                    st[f"{a}l2{pr}_{k}"] = qGP.end()

                def l2v(e, en, k=k, pr=pr, pc0=pc0, hb=hb, xq2=xq2):
                    need(e, en, st[f"{a}l2{pr}_{k}"], (sVE, st[a + "q2sc"]), (sTE, st[f"{a}cvu3"]))
                    if app == 0 and pr == 1:
                        need(e, en, st["wready1"])
                    need(e, en, (sTE, st.get(f"{a}ymm{pr-2}_{KD-1}", 0)))
                    need(e, en, (sVE, sVE.n))
                    sVE.inc(e.tensor_tensor(ytmp[:, 0:PW], hb[:, 0:PW], sT3[:, pc0:pc0 + PW], ALU.mult))
                    need(e, en, (sVE, sVE.n))
                    sVE.inc(e.tensor_scalar(xq2[k][:, 0:PW], ytmp[:, 0:PW], MAGIC, -MAGIC, ALU.add, ALU.add))
                    st[f"{a}xq2_{pr}_{k}"] = sVE.n
                on("gpsimd", l2g)
                on("vector", l2v)

        _QSPLIT = [5, 5, 5, 5, 4, 4, 4, 0]

        emit_quant(0, range(MH))
        for pr in range(NPAIR):
            pc0 = 1 + pr * PW
            xq2 = xq2A if pr % 2 == 0 else xq2B
            for mo in range(KD):
                r = mo % 2

                def w2s(e, en, mo=mo, r=r, pr=pr):
                    need(e, en, st["wready1"], (sTE, st[f"{a}cvu3"]))
                    if app == 0:
                        need(e, en, (sVE, st[f"{a}xq{KD-1}"]))
                    if mo >= 2:
                        need(e, en, (sTE, st[f"{a}ymm{pr}_{mo-2}"]))
                    elif pr > 0:
                        need(e, en, (sTE, st[f"{a}ymm{pr-1}_{KD-2+mo}"]))
                    qIN.begin(e, en)
                    qIN.inc(e.dma_start(
                        out=w2sb[r][:, :].rearrange("p (k q) -> p k q", q=128),
                        in_=w2q_d.rearrange("(t p q) -> t p q", p=128, q=128)[mo::KD, :, :].rearrange("k p q -> p k q")))
                    st[f"{a}w2l{pr}_{mo}"] = qIN.end()

                def ymt(e, en, mo=mo, r=r, pr=pr, xq2=xq2):
                    need(e, en, st[f"{a}w2l{pr}_{mo}"],
                         (sVE, st[f"{a}xq2_{pr}_{MH-1}"]))
                    ps = psA if mo % 2 == 0 else psB
                    if mo >= 2:
                        need(e, en, (sVE, st[f"{a}yev{pr}_{mo-2}"]))
                    elif pr > 0:
                        need(e, en, (sVE, st[f"{a}yev{pr-1}_{KD-2+mo}"]))
                    for k in range(MH):
                        ins = e.matmul(ps[:, 0:PW], w2sb[r][:, k * 128:(k + 1) * 128], xq2[k][:, 0:PW],
                                       start=(k == 0), stop=(k == MH - 1))
                    sTE.inc(ins)
                    st[f"{a}ymm{pr}_{mo}"] = sTE.n
                    if mo == KD - 1:
                        st[f"{a}ymms{pr}"] = sTE.n

                def yev(e, en, mo=mo, pr=pr, pc0=pc0):
                    ps = psA if mo % 2 == 0 else psB
                    need(e, en, (sTE, st[f"{a}ymm{pr}_{mo}"]), (sVE, st[a + "q2sc"]))
                    need(e, en, (sVE, sVE.n))
                    sVE.inc(e.tensor_tensor(ytmp[:, 0:PW], ps[:, 0:PW], sT4[:, pc0:pc0 + PW], ALU.mult))
                    need(e, en, (sVE, sVE.n))
                    sVE.inc(e.tensor_tensor(xg[mo][:, pc0:pc0 + PW], ytmp[:, 0:PW], xg[mo][:, pc0:pc0 + PW], ALU.add))
                    st[f"{a}yev{pr}_{mo}"] = sVE.n
                on("sync", w2s)
                on("tensor", ymt)
                on("vector", yev)
                if pr + 1 < NPAIR:
                    s0 = sum(_QSPLIT[:mo])
                    emit_quant(pr + 1, range(s0, s0 + _QSPLIT[mo]))

        def fin(e, en):
            for mo in range(KD):
                need(e, en, (sVE, st[f"{a}yev{NPAIR-1}_{mo}"]))
            st[a + "resid"] = max(st[f"{a}yev{pr}_{mo}"] for pr in range(NPAIR) for mo in range(KD))
        on("vector", fin)

    def ffn_sublayer(app, dep=()):
        ffn_stats(app, with_quant=True, dep=dep)
        ffn_quant_apply(app)
        ffn_mm1_conv(app)
        ffn_quant2(app)
        ffn_mm2(app)

    def tap_sb(name, tiles, cols, deps, eng="sync"):
        if name not in tap_t:
            return
        q = qOUT if eng == "sync" else qGP
        def tf(e, en):
            for d in deps:
                need(e, en, d)
            q.begin(e, en)
            for i, t in enumerate(tiles):
                q.inc(e.dma_start(out=tap_t[name][i * 128:(i + 1) * 128, 0:cols], in_=t[:, 0:cols]))
            st["tap_" + name] = q.end()
        on(eng, tf)

    ffn_stats(0, with_quant=True)
    ffn_quant_apply(0)
    qw_scale(1)
    ffn_mm1_conv(0, extra_per_blk=lambda blk: [qw_pass2_step(1, blk * 8 + j) for j in range(8)])
    st["wready1"] = [st[f"wstr1_{NWT-2}"], st[f"wstr1_{NWT-1}"]]
    ffn_quant2(0)
    ffn_mm2(0)
    tap_sb("x1", xg, G, [(sVE, st["f0resid"])])
    tap_sb("xq0", xq, GV, [(sVE, st["f0xqall"])], eng="gpsimd")
    if "w1q" in tap_t:
        def tf_w1(e, en):
            need(e, en, st["wready0"])
            qGP.begin(e, en)
            for m in range(MH):
                for k in range(KD):
                    off = (k * MH + m) * 16384
                    qGP.inc(e.dma_start(out=tap_t["w1q"][(m * KD + k) * 128:(m * KD + k + 1) * 128, :],
                                        in_=w1q_d[off:off + 16384].rearrange("(p q) -> p q", q=128)))
            st["tap_w1q"] = qGP.end()
        on("gpsimd", tf_w1)
    if "h2d" in tap_t:
        def tf_h2(e, en):
            need(e, en, st["f0h2st31"])
            qGP.begin(e, en)
            for m in range(MH):
                qGP.inc(e.dma_start(out=tap_t["h2d"][m * 128:(m + 1) * 128, :], in_=h2_d[m, :, :]))
            st["tap_h2d"] = qGP.end()
        on("gpsimd", tf_h2)

    ffn_stats("ln2", with_quant=False, dep=[(sVE, st["f0resid"])])

    def xn2v(e, en):
        need(e, en, (sSE, st["fln2rstd"]), st.get("tap_xq0", 0))
        for k in range(KD):
            need(e, en, (sVE, sVE.n))
            sVE.inc(e.tensor_sub(sT1[:, 1:1 + SEG], xg[k][:, 1:1 + SEG], sScr[:, 1:1 + SEG]))
            need(e, en, (sVE, sVE.n))
            sVE.inc(e.tensor_tensor(xn2[k][:, 0:SEG], sT1[:, 1:1 + SEG], sT2[:, 1:1 + SEG], ALU.mult))
            st[f"xn2_{k}"] = sVE.n
        st["xn2all"] = sVE.n
    on("vector", xn2v)
    tap_sb("xn2", xn2, SEG, [(sVE, st["xn2all"])], eng="gpsimd")

    _SPORD = [3, 4, 5, 6, 7, 0, 1, 2]

    def spx(e, en):
        need(e, en, st.get("tap_x1", 0), st.get("tap_xn2", 0))
        for k in _SPORD:
            need(e, en, (sVE, st[f"f0yev{NPAIR-1}_{k}"]))
            qGP.begin(e, en)
            qGP.inc(e.dma_start(out=x1_d[k * 128:(k + 1) * 128, :], in_=xg[k][:, :]))
            st[f"xsp{k}"] = qGP.end()
        st["xspill"] = [st["xsp1"], st["xsp2"]]
        st["xsp_qk"] = [st["xsp6"], st["xsp7"]]
    on("gpsimd", spx)

    def awl(e, en):
        for k in range(KD):
            qGP.begin(e, en)
            qGP.inc(e.dma_start(out=wgk1a[k][:, :], in_=wgk1_t[k * 128:(k + 1) * 128, :]))
        qGP.inc(e.dma_start(out=wgk2a[:, :], in_=wgk2_t))
        st["wgkl"] = qGP.end()
        for k in range(KD):
            qGP.begin(e, en)
            qGP.inc(e.dma_start(out=wva[k][:, :], in_=wv_t[k * 128:(k + 1) * 128, :]))
        st["wvl"] = qGP.end()
        need(e, en, st["xsp_qk"], (sVE, st["xn2all"]))
        for k in range(KD):
            qGP.begin(e, en)
            qGP.inc(e.dma_start(out=wqa[k][:, :], in_=wq_t[k * 128:(k + 1) * 128, :]))
            qGP.inc(e.dma_start(out=wka[k][:, :], in_=wk_t[k * 128:(k + 1) * 128, :]))
        st["wqkl"] = qGP.end()
    on("gpsimd", awl)

    NT2 = SEG // 512
    for ti in range(NT2):
        def ut(e, en, ti=ti):
            need(e, en, st["wgkl"], (sVE, st["xn2all"]))
            need(e, en, (sVE, st.get(f"uev{ti-1}", 0)))
            for k in range(KD):
                ins = e.matmul(psC[0:32, 0:512], wgk1a[k][:, :], xn2[k][:, ti * 512:(ti + 1) * 512],
                               start=(k == 0), stop=(k == KD - 1))
            sTE.inc(ins)
            st[f"umm{ti}"] = sTE.n

        def uv(e, en, ti=ti):
            need(e, en, (sTE, st[f"umm{ti}"]))
            sVE.inc(e.tensor_copy(u_bf[:, ti * 512:(ti + 1) * 512], psC[0:32, 0:512]))
            st[f"uev{ti}"] = sVE.n
        on("tensor", ut)
        on("vector", uv)

    def uf(e, en):
        need(e, en, (sVE, st[f"uev{NT2-1}"]))
        st["u_done"] = sVE.n
    on("vector", uf)

    for half in range(2):
        hw = SEG // 2
        hc0 = half * hw
        NTH = hw // 512 if hw >= 512 else 1
        TW = hw // NTH
        for m in range(KD):
            for ti in range(NTH):
                g = (half * KD + m) * NTH + ti

                def zt(e, en, half=half, hc0=hc0, m=m, ti=ti, g=g):
                    need(e, en, (sVE, st["u_done"]))
                    gp = g - 2
                    if gp >= 0:
                        need(e, en, (sSE, st.get(f"sgk{gp}", 0)))
                    ps = psD if g % 2 == 0 else psE
                    sTE.inc(e.matmul(ps[:, 0:TW if TW <= 512 else 512], wgk2a[:, m * 128:(m + 1) * 128],
                                     u_bf[:, hc0 + ti * TW:hc0 + (ti + 1) * TW], start=True, stop=True))
                    st[f"zmm{g}"] = sTE.n

                def zs(e, en, half=half, hc0=hc0, m=m, ti=ti, g=g):
                    need(e, en, (sTE, st[f"f0ymms{NPAIR-1}"]), (sVE, st["f0resid"]))
                    need(e, en, (sSE, st.get(f"ghalf{half-1}", 0)))
                    need(e, en, (sTE, st[f"zmm{g}"]))
                    ps = psD if g % 2 == 0 else psE
                    sSE.inc(e.activation(sgs[m][:, ti * TW:(ti + 1) * TW], ps[:, 0:TW], AF.Sigmoid, bias=bgk2_sb[:, m:m + 1]))
                    st[f"sgk{g}"] = sSE.n
                on("tensor", zt)
                on("scalar", zs)

        def zl(e, en, half=half, hc0=hc0):
            need(e, en, st["xspill"], (sVE, st["xn2all"]))
            for m in range(KD):
                need(e, en, (sSE, st[f"sgk{(half * KD + m) * NTH + NTH - 1}"]))
                sSE.inc(e.activation(g16[m][:, hc0:hc0 + hw], sgs[m][:, 0:hw], AF.Ln))
            st[f"ghalf{half}"] = sSE.n
        on("scalar", zl)
    st["g16done"] = st["ghalf1"]

    for j in range(NJ):
        for n in range(4):
            g = j * 4 + n

            def vt(e, en, j=j, n=n, g=g):
                need(e, en, st["wvl"], (sVE, st["xn2all"]))
                gp = g - 2
                if gp >= 0:
                    need(e, en, (sVE, st.get(f"vev{gp}", 0)))
                ps = psF if g % 2 == 0 else psG
                for k in range(KD):
                    ins = e.matmul(ps[:, 0:512], xn2[k][:, j * 128:(j + 1) * 128], wva[k][:, n * 512:(n + 1) * 512],
                                   start=(k == 0), stop=(k == KD - 1))
                sTE.inc(ins)
                st[f"vmm{g}"] = sTE.n

            def vv(e, en, j=j, n=n, g=g):
                need(e, en, (sTE, st[f"vmm{g}"]))
                if n == 0:
                    need(e, en, st.get(f"vst{j-1}", 0))
                ps = psF if g % 2 == 0 else psG
                sVE.inc(e.tensor_copy(vtmp[:, n * 512:(n + 1) * 512], ps[:, 0:512]))
                st[f"vev{g}"] = sVE.n
            on("tensor", vt)
            on("vector", vv)

        def vg(e, en, j=j):
            need(e, en, (sVE, st[f"vev{j * 4 + 3}"]))
            qGP.begin(e, en)
            qGP.inc(e.dma_start(out=v_d[j, :, :], in_=vtmp[:, :]))
            st[f"vst{j}"] = qGP.end()
        on("gpsimd", vg)
    def vfin(e, en):
        st["v_done"] = qGP.fence()
    on("gpsimd", vfin)

    def tz(e, en):
        need(e, en, st["v_done"], st["wready1"])
        sGC.inc(e.memset(T32[:, :], 0.0))
        st["Tzero"] = sGC.n
    on("gpsimd", tz)

    PS_O0, PS_O1, PS_A, PS_TD0, PS_TD1 = psA, psB, psC, psD, psE
    for c in range(OC):
        c0 = c * 256
        def l1v(e, en, c=c, c0=c0):
            need(e, en, (sSE, st["g16done"]), (sGC, st["Tzero"]))
            need(e, en, (sVE, st.get(f"c{c-1}brun", 0)))
            for k in range(KD):
                e.tensor_scalar(ebias[:, k:k + 1], Brun[:, k:k + 1], 1.0 / 32.0, LN16, ALU.mult, ALU.add)
                sVE.inc(e.tensor_scalar(ebias[:, KD + k:KD + k + 1], Brun[:, k:k + 1], -1.0 / 32.0, None, ALU.mult))
            st[f"c{c}ebias"] = sVE.n
            need(e, en, (sSE, st.get(f"c{c-1}exps", 0)))
            for k in range(KD):
                sVE.inc(e.tensor_tensor_scan(csc[:, k * 256:(k + 1) * 256], g16[k][:, c0:c0 + 256],
                                             zz256[:, 0:256], 0.0, ALU.add, ALU.add))
                st[f"c{c}scan{k}"] = sVE.n
            st[f"c{c}scans"] = sVE.n
        on("vector", l1v)

        def l1s(e, en, c=c):
            need(e, en, (sVE, st[f"c{c}scans"]), (sVE, st[f"c{c}ebias"]))
            need(e, en, (sVE, st.get(f"c{c-1}qk", 0)))
            for k in range(KD):
                e.activation(eqc[:, k * 256:(k + 1) * 256], csc[:, k * 256:(k + 1) * 256], AF.Exp,
                             bias=ebias[:, k:k + 1], scale=1.0 / 32.0)
                sSE.inc(e.activation(ekc[:, k * 256:(k + 1) * 256], csc[:, k * 256:(k + 1) * 256], AF.Exp,
                                     bias=ebias[:, KD + k:KD + k + 1], scale=-1.0 / 32.0))
            st[f"c{c}exps"] = sSE.n
        on("scalar", l1s)

        def l1b(e, en, c=c):
            need(e, en, (sSE, st[f"c{c}exps"]))
            for k in range(KD):
                sVE.inc(e.tensor_tensor(Brun[:, k:k + 1], Brun[:, k:k + 1], csc[:, k * 256 + 255:k * 256 + 256], ALU.add))
            st[f"c{c}brun"] = sVE.n
        on("vector", l1b)

        def lvg(e, en, c=c):
            need(e, en, st["v_done"])
            need(e, en, (sVE, st.get(f"c{c-2}oall", 0)))
            qGP.begin(e, en)
            qGP.inc(e.dma_start(out=vtk[c % 2][:, :].rearrange("p (j q) -> p j q", q=DV),
                                in_=v_d[2 * c:2 * c + 2, :, :].rearrange("j p q -> p j q")))
            st[f"c{c}vl"] = qGP.end()
        on("gpsimd", lvg)

        for qk in range(2):
            wt, key, ec = (wqa, "q", eqc) if qk == 0 else (wka, "k", ekc)
            tg = qtc if qk == 0 else ktc
            for m in range(KD):
                g = qk * KD + m

                def lq1(e, en, c=c, c0=c0, m=m, g=g, wt=wt, key=key):
                    if g == 0:
                        need(e, en, st["wqkl"], (sVE, st["xn2all"]))
                        need(e, en, (sVE, st.get(f"vev{NJ * 4 - 1}", 0)))
                    gp = g - 2
                    if gp >= 0:
                        need(e, en, (sVE, st.get(f"c{c}qkev{gp}", 0)))
                    ps = psF if g % 2 == 0 else psG
                    for k in range(KD):
                        ins = e.matmul(ps[:, 0:256], wt[k][:, m * 128:(m + 1) * 128], xn2[k][:, c0:c0 + 256],
                                       start=(k == 0), stop=(k == KD - 1))
                    sTE.inc(ins)
                    st[f"c{c}qkmm{g}"] = sTE.n

                def lq2(e, en, c=c, m=m, g=g, ec=ec, tg=tg):
                    if g == 0:
                        need(e, en, (sSE, st[f"c{c}exps"]))
                        need(e, en, (sTE, st.get(f"c{c-1}omm{H-1}", 0)), st.get(f"c{c-1}qsp", 0),
                             (sTE, st.get(f"c{c-1}tdmm{H-1}", 0)))
                    need(e, en, (sTE, st[f"c{c}qkmm{g}"]))
                    ps = psF if g % 2 == 0 else psG
                    sVE.inc(e.tensor_tensor(tg[:, m * 256:(m + 1) * 256], ps[:, 0:256], ec[:, m * 256:(m + 1) * 256], ALU.mult))
                    st[f"c{c}qkev{g}"] = sVE.n
                on("tensor", lq1)
                on("vector", lq2)

        def lqf(e, en, c=c):
            need(e, en, (sVE, st[f"c{c}qkev{2 * KD - 1}"]))
            st[f"c{c}qk"] = sVE.n
            st[f"c{c}qev{KD-1}"] = sVE.n
        on("vector", lqf)

        def lqg(e, en, c=c, c0=c0):
            need(e, en, (sVE, st[f"c{c}qev{KD-1}"]))
            qGP.begin(e, en)
            qGP.inc(e.dma_start(out=qt_d[:, :, c0:c0 + 256].rearrange("k p q -> p k q"),
                                in_=qtc[:, :].rearrange("p (k q) -> p k q", q=256)))
            st[f"c{c}qsp"] = qGP.end()
        on("gpsimd", lqg)

        for jt in range(2):
            def ltt(e, en, c=c, jt=jt):
                need(e, en, (sVE, st[f"c{c}qk"]))
                need(e, en, (sVE, st.get(f"c{c}ktokev{jt-1}", 0)))
                if jt == 0:
                    need(e, en, (sVE, st.get(f"c{c-1}ktokev1", 0)))
                for m in range(KD):
                    ins = e.transpose(psT[:, m * 128:(m + 1) * 128], ktc[:, m * 256 + jt * 128:m * 256 + jt * 128 + 128], ident[:, :])
                sTE.inc(ins)
                st[f"c{c}ktok{jt}"] = sTE.n

            def ltv(e, en, c=c, jt=jt):
                need(e, en, (sTE, st[f"c{c}ktok{jt}"]))
                need(e, en, (sTE, st.get(f"c{c-1}tdmm{H-1}", 0)))
                sVE.inc(e.tensor_copy(ktok[:, jt * 1024:(jt + 1) * 1024], psT[:, 0:1024]))
                st[f"c{c}ktokev{jt}"] = sVE.n
            on("tensor", ltt)
            on("vector", ltv)

        def lt16(e, en, c=c):
            need(e, en, (sVE, st.get(f"c{c-1}tupd", 0)), (sTE, st.get(f"c{c-1}omm{H-1}", 0)))
            need(e, en, (sGC, st["Tzero"]), st["wready1"])
            sVE.inc(e.tensor_copy(T16[:, :], T32[:, :]))
            st[f"c{c}t16"] = sVE.n
        on("vector", lt16)

        for h in range(H):
            def lat(e, en, c=c, h=h):
                need(e, en, (sVE, st[f"c{c}qk"]))
                need(e, en, (sVE, st.get(f"c{c}aev{h-1}", 0)))
                first = True
                for bi, (jt, tt_) in enumerate(((0, 0), (1, 1), (0, 1))):
                    for kk in range(2):
                        ch = 2 * h + kk
                        ins = e.matmul(PS_A[:, bi * 128:(bi + 1) * 128],
                                       ktc[:, ch * 256 + jt * 128:ch * 256 + jt * 128 + 128],
                                       qtc[:, ch * 256 + tt_ * 128:ch * 256 + tt_ * 128 + 128],
                                       start=first, stop=(bi == 2 and kk == 1))
                        first = False
                sTE.inc(ins)
                st[f"c{c}amm{h}"] = sTE.n
            on("tensor", lat)

            def lav(e, en, c=c, h=h):
                need(e, en, (sTE, st[f"c{c}amm{h}"]), (sGC, st["consts_built"]))
                need(e, en, (sTE, st.get(f"c{c}omm{h-1}", 0)))
                e.tensor_tensor(Abuf[:, 0:128], PS_A[:, 0:128], maskJT[:, :], ALU.mult)
                e.tensor_tensor(Abuf[:, 128:256], PS_A[:, 128:256], maskJT[:, :], ALU.mult)
                sVE.inc(e.tensor_copy(Abuf[:, 256:384], PS_A[:, 256:384]))
                st[f"c{c}aev{h}"] = sVE.n
            on("vector", lav)

            def lot(e, en, c=c, h=h):
                need(e, en, (sVE, st[f"c{c}aev{h}"]), st[f"c{c}vl"], (sVE, st[f"c{c}t16"]))
                need(e, en, (sVE, st.get(f"c{c}oev{h-1}", 0)))
                vb = vtk[c % 2]
                first = True
                for cv in range(4):
                    e.matmul(PS_O0[:, cv * 128:(cv + 1) * 128], vb[:, 0 * DV + h * 512 + cv * 128:0 * DV + h * 512 + (cv + 1) * 128],
                             Abuf[:, 0:128], start=first, stop=False)
                    first = False
                for kk in range(2):
                    ch = 2 * h + kk
                    for cv in range(4):
                        e.matmul(PS_O0[:, cv * 128:(cv + 1) * 128],
                                 T16[:, ch * 512 + cv * 128:ch * 512 + (cv + 1) * 128],
                                 qtc[:, ch * 256:ch * 256 + 128], start=False, stop=(kk == 1 and cv == 3))
                first = True
                for cv in range(4):
                    e.matmul(PS_O1[:, cv * 128:(cv + 1) * 128], vb[:, 1 * DV + h * 512 + cv * 128:1 * DV + h * 512 + (cv + 1) * 128],
                             Abuf[:, 128:256], start=first, stop=False)
                    first = False
                for cv in range(4):
                    e.matmul(PS_O1[:, cv * 128:(cv + 1) * 128], vb[:, 0 * DV + h * 512 + cv * 128:0 * DV + h * 512 + (cv + 1) * 128],
                             Abuf[:, 256:384], start=False, stop=False)
                last = None
                for kk in range(2):
                    ch = 2 * h + kk
                    for cv in range(4):
                        last = e.matmul(PS_O1[:, cv * 128:(cv + 1) * 128],
                                        T16[:, ch * 512 + cv * 128:ch * 512 + (cv + 1) * 128],
                                        qtc[:, ch * 256 + 128:ch * 256 + 256], start=False, stop=(kk == 1 and cv == 3))
                sTE.inc(last)
                st[f"c{c}omm{h}"] = sTE.n
                need(e, en, (sVE, st[f"c{c}ktokev1"]))
                need(e, en, (sVE, st.get(f"c{c}tup{h-1}", 0)))
                for kk in range(2):
                    ch = 2 * h + kk
                    ps = PS_TD0 if kk == 0 else PS_TD1
                    for jt in range(2):
                        ins = e.matmul(ps[:, 0:512], ktok[:, jt * 1024 + ch * 128:jt * 1024 + (ch + 1) * 128],
                                       vb[:, jt * DV + h * 512:jt * DV + (h + 1) * 512], start=(jt == 0), stop=(jt == 1))
                    sTE.inc(ins)
                st[f"c{c}tdmm{h}"] = sTE.n
            on("tensor", lot)

            def lov(e, en, c=c, h=h):
                need(e, en, (sTE, st[f"c{c}omm{h}"]))
                need(e, en, st.get(f"c{c-1}osp", 0))
                for tau in range(2):
                    ps = PS_O0 if tau == 0 else PS_O1
                    dst = och[:, h * 1024:(h + 1) * 1024].rearrange("p (cv t) -> p cv t", t=256)[:, :, tau * 128:tau * 128 + 128]
                    src = ps[:, 0:512].rearrange("p (cv t) -> p cv t", t=128)
                    sVE.inc(e.tensor_copy(dst, src))
                st[f"c{c}oev{h}"] = sVE.n
                need(e, en, (sTE, st[f"c{c}tdmm{h}"]))
                for kk in range(2):
                    ch = 2 * h + kk
                    ps = PS_TD0 if kk == 0 else PS_TD1
                    sVE.inc(e.tensor_tensor(T32[:, ch * 512:(ch + 1) * 512], ps[:, 0:512], T32[:, ch * 512:(ch + 1) * 512], ALU.add))
                st[f"c{c}tup{h}"] = sVE.n
            on("vector", lov)
        st[f"c{c}tupd"] = st[f"c{c}tup{H-1}"]

        def log_(e, en, c=c, c0=c0):
            need(e, en, (sVE, st[f"c{c}oev{H-1}"]))
            qGP.begin(e, en)
            qGP.inc(e.dma_start(out=o_d[:, :, c0:c0 + 256].rearrange("cv p t -> p cv t"),
                                in_=och[:, :].rearrange("p (cv t) -> p cv t", t=256)))
            st[f"c{c}osp"] = qGP.end()
        on("gpsimd", log_)

        def lfin(e, en, c=c):
            need(e, en, (sVE, st[f"c{c}tupd"]), (sVE, st[f"c{c}oev{H-1}"]))
            st[f"c{c}oall"] = sVE.n
        on("vector", lfin)

    def pcs(e, en):
        need(e, en, (sVE, st[f"c{OC-1}tupd"]), (sVE, st[f"c{OC-1}brun"]))
        for h in range(H):
            for kk in range(2):
                ch = 2 * h + kk
                off = (h * 257 + kk * 128) * 512
                qGP.begin(e, en)
                qGP.inc(e.dma_start(out=cc_s[off:off + 128 * 512].rearrange("(p q) -> p q", q=512),
                                    in_=T32[:, ch * 512:(ch + 1) * 512]))
            offl = (h * 257 + 256) * 512
            for kk in range(2):
                ch = 2 * h + kk
                qGP.inc(e.dma_start(out=cc_s[offl + kk * 128:offl + kk * 128 + 128].rearrange("(p o) -> p o", o=1),
                                    in_=Brun[:, ch:ch + 1]))
            qGP.inc(e.dma_start(out=cc_s[offl + 256:offl + 512].rearrange("(p o) -> p o", o=2),
                                in_=zzf2[:, 0:2]))
        st["ccs"] = qGP.end()
        need(e, en, st["ccs"])
        sCC.inc(e.collective_compute(
            "AllGather", ALU.bypass,
            replica_groups=[[0, 1, 2, 3], [4, 5, 6, 7]],
            ins=[cc_s], outs=[cc_r]))
        st["ag1"] = sCC.n
    on("gpsimd", pcs)

    def gwl(e, en):
        need(e, en, (sVE, st[f"c{OC-1}oall"]))
        need(e, en, st["ccs"])
        for k in range(KD):
            qGP.begin(e, en)
            qGP.inc(e.dma_start(out=wga[k][:, :], in_=wg_t[k * 128:(k + 1) * 128, :]))
        st["wgl"] = qGP.end()
    on("gpsimd", gwl)

    def xrl(e, en):
        need(e, en, st["ccs"])
        need(e, en, st["wqkl"])
        for k in range(KD):
            qIN.begin(e, en)
            qIN.inc(e.dma_start(out=xg[k][:, :], in_=x1_d[k * 128:(k + 1) * 128, :]))
        st["xreload"] = qIN.end()
    on("sync", xrl)

    for mg in range(MV):
        for ti in range(NT2):
            g = mg * NT2 + ti

            def gt(e, en, mg=mg, ti=ti, g=g):
                need(e, en, st["wgl"], (sVE, st["xn2all"]))
                gp = g - 2
                if gp >= 0:
                    need(e, en, (sSE, st.get(f"gse{gp}", 0)))
                ps = psF if g % 2 == 0 else psG
                for k in range(KD):
                    ins = e.matmul(ps[:, 0:512], wga[k][:, mg * 128:(mg + 1) * 128], xn2[k][:, ti * 512:(ti + 1) * 512],
                                   start=(k == 0), stop=(k == KD - 1))
                sTE.inc(ins)
                st[f"gmm{g}"] = sTE.n

            def gs(e, en, mg=mg, ti=ti, g=g):
                need(e, en, (sTE, st[f"gmm{g}"]))
                if ti == 0:
                    need(e, en, st.get(f"gst{mg-1}", 0))
                ps = psF if g % 2 == 0 else psG
                sSE.inc(e.activation(h2t[:, ti * 512:(ti + 1) * 512], ps[:, 0:512], AF.Silu))
                st[f"gse{g}"] = sSE.n
            on("tensor", gt)
            on("scalar", gs)

        def gg(e, en, mg=mg):
            need(e, en, (sSE, st[f"gse{mg * NT2 + NT2 - 1}"]))
            qGP.begin(e, en)
            qGP.inc(e.dma_start(out=gate_d[mg, :, :], in_=h2t[:, 0:SEG]))
            st[f"gst{mg}"] = qGP.end()
        on("gpsimd", gg)
    def gfin(e, en):
        st["gate_done"] = qGP.fence()
    on("gpsimd", gfin)

    def cmb_g(e, en):
        need(e, en, (sCC, st["ag1"]), st["gate_done"])
        for j in range(4):
            for ch in range(KD):
                h, kk = ch // 2, ch % 2
                off = j * H * 257 * 512 + (h * 257 + 256) * 512 + kk * 128
                qGP.begin(e, en)
                qGP.inc(e.dma_start(out=Lg[:, j * KD + ch:j * KD + ch + 1],
                                    in_=cc_r[off:off + 128].rearrange("(p o) -> p o", o=1)))
        st["Lgl"] = qGP.end()
        sGC.inc(e.memset(Sin[:, :], 0.0))
        st["sin0"] = sGC.n
    on("gpsimd", cmb_g)

    for j in range(4):
        def cj_v(e, en, j=j):
            need(e, en, st["Lgl"], st["consts_loaded"])
            need(e, en, (sSE, st.get(f"ec{j-1}", 0)))
            for ch in range(KD):
                for l in range(4):
                    if l == 0:
                        ins = e.tensor_scalar(eacc[:, ch:ch + 1], Lg[:, l * KD + ch:l * KD + ch + 1], ctl_sb[:, 4 * j + l:4 * j + l + 1], None, ALU.mult)
                    else:
                        need(e, en, (sVE, sVE.n))
                        ins = e.scalar_tensor_tensor(eacc[:, ch:ch + 1], Lg[:, l * KD + ch:l * KD + ch + 1], ctl_sb[:, 4 * j + l:4 * j + l + 1], eacc[:, ch:ch + 1], ALU.mult, ALU.add)
                    sVE.inc(ins)
            st[f"eacc{j}"] = sVE.n
        on("vector", cj_v)

        def cj_s(e, en, j=j):
            need(e, en, (sVE, st[f"eacc{j}"]))
            sSE.inc(e.activation(ecol[:, :], eacc[:, :], AF.Exp, bias=ctl_sb[:, 16 + j:17 + j], scale=1.0 / 32.0))
            st[f"ec{j}"] = sSE.n
        on("scalar", cj_s)

        for ch in range(KD):
            def cj_g(e, en, j=j, ch=ch):
                need(e, en, (sCC, st["ag1"]))
                h, kk = ch // 2, ch % 2
                prev = st.get(f"sacc{j}_{ch-1}", st.get(f"sacc{j-1}_{KD-1}", 0) if ch == 0 else 0)
                need(e, en, (sVE, prev))
                off = j * H * 257 * 512 + (h * 257 + kk * 128) * 512
                r = ch % 2
                qGP.begin(e, en)
                qGP.inc(e.dma_start(out=sjb[:, 0:512] if r == 0 else sjb[:, 0:512],
                                    in_=cc_r[off:off + 128 * 512].rearrange("(p q) -> p q", q=512)))
                st[f"sjl{j}_{ch}"] = qGP.end()
            def cj_a(e, en, j=j, ch=ch):
                need(e, en, st[f"sjl{j}_{ch}"], (sSE, st[f"ec{j}"]), (sGC, st["sin0"]))
                need(e, en, (sVE, st.get(f"sacc{j-1}_{ch}", 0)))
                sVE.inc(e.scalar_tensor_tensor(Sin[:, ch * 512:(ch + 1) * 512], sjb[:, 0:512], ecol[:, ch:ch + 1],
                                               Sin[:, ch * 512:(ch + 1) * 512], ALU.mult, ALU.add))
                st[f"sacc{j}_{ch}"] = sVE.n
            on("gpsimd", cj_g)
            on("vector", cj_a)

    def s16v(e, en):
        for ch in range(KD):
            need(e, en, (sVE, st[f"sacc3_{ch}"]))
        sVE.inc(e.tensor_copy(S16b[:, :], Sin[:, :]))
        st["s16"] = sVE.n
    on("vector", s16v)

    def wol(e, en):
        need(e, en, st["gate_done"])
        for cv in range(MV):
            qGP.begin(e, en)
            qGP.inc(e.dma_start(out=woa[cv][:, :], in_=wout_t[cv * 128:(cv + 1) * 128, :]))
        st["wol"] = qGP.end()
    on("gpsimd", wol)

    for c in range(OC):
        c0 = c * 256
        def pb_g(e, en, c=c, c0=c0):
            need(e, en, (sVE, st["s16"]), st[f"c{OC-1}osp"], st[f"c{OC-1}qsp"])
            need(e, en, (sVE, st.get(f"pb{c-1}done", 0)))
            qGP.begin(e, en)
            qGP.inc(e.dma_start(out=och[:, :].rearrange("p (cv t) -> p cv t", t=256),
                                in_=o_d[:, :, c0:c0 + 256].rearrange("cv p t -> p cv t")))
            qGP.inc(e.dma_start(out=eqc[:, :].rearrange("p (k q) -> p k q", q=256),
                                in_=qt_d[:, :, c0:c0 + 256].rearrange("k p q -> p k q")))
            qGP.inc(e.dma_start(out=gch[:, :].rearrange("p (cv t) -> p cv t", t=256),
                                in_=gate_d[:, :, c0:c0 + 256].rearrange("cv p t -> p cv t")))
            st[f"pb{c}in"] = qGP.end()
        on("gpsimd", pb_g)

        for cv in range(MV):
            def pb_t1(e, en, c=c, cv=cv):
                need(e, en, st[f"pb{c}in"], (sVE, st["s16"]))
                need(e, en, (sVE, st.get(f"pb{c}o1_{cv-2}", 0)))
                h = cv // 4
                ps = psF if cv % 2 == 0 else psG
                for kk in range(2):
                    ch = 2 * h + kk
                    ins = e.matmul(ps[:, 0:256], S16b[:, ch * 512 + (cv % 4) * 128:ch * 512 + (cv % 4) * 128 + 128],
                                   eqc[:, ch * 256:(ch + 1) * 256], start=(kk == 0), stop=(kk == 1))
                sTE.inc(ins)
                st[f"pb{c}corr{cv}"] = sTE.n

            def pb_v1(e, en, c=c, cv=cv):
                ps = psF if cv % 2 == 0 else psG
                need(e, en, (sTE, st[f"pb{c}corr{cv}"]))
                sVE.inc(e.tensor_tensor(och[:, cv * 256:(cv + 1) * 256], ps[:, 0:256], och[:, cv * 256:(cv + 1) * 256], ALU.add))
                st[f"pb{c}o1_{cv}"] = sVE.n
            on("tensor", pb_t1)
            on("vector", pb_v1)

        for cv in range(MV):
            h = cv // 4

            def pb_s1(e, en, c=c, cv=cv):
                need(e, en, (sVE, st[f"pb{c}o1_{cv}"]))
                need(e, en, (sTE, st.get(f"pb{c}msmm{cv-2}", 0)))
                sSE.inc(e.activation(o2scr[:, (cv % 2) * 256:(cv % 2) * 256 + 256],
                                     och[:, cv * 256:(cv + 1) * 256], AF.Square))
                st[f"pb{c}sq{cv}"] = sSE.n

            def pb_t2(e, en, c=c, cv=cv, h=h):
                need(e, en, (sSE, st[f"pb{c}sq{cv}"]))
                if cv % 4 == 0:
                    need(e, en, (sVE, st.get(f"pb{c}msv{h-1}", 0)))
                ins = e.matmul(psC[:, 0:256], ones_b[:, :], o2scr[:, (cv % 2) * 256:(cv % 2) * 256 + 256],
                               start=(cv % 4 == 0), stop=(cv % 4 == 3))
                sTE.inc(ins)
                st[f"pb{c}msmm{cv}"] = sTE.n
                if cv % 4 == 3:
                    st[f"pb{c}ms{h}"] = sTE.n
            on("scalar", pb_s1)
            on("tensor", pb_t2)

            if cv % 4 == 3:
                def pb_v2(e, en, c=c, h=h):
                    need(e, en, (sTE, st[f"pb{c}ms{h}"]))
                    sVE.inc(e.tensor_scalar(msb_t[:, h * 256:(h + 1) * 256], psC[:, 0:256], 1.0 / HV, float(EPS), ALU.mult, ALU.add))
                    st[f"pb{c}msv{h}"] = sVE.n
                on("vector", pb_v2)

        def pb_s2(e, en, c=c):
            ins = None
            for h in range(H):
                need(e, en, (sVE, st[f"pb{c}msv{h}"]))
                ins = e.activation(msb_t[:, h * 256:(h + 1) * 256], msb_t[:, h * 256:(h + 1) * 256], AF.Ln)
            sSE.inc(ins)
            need(e, en, (sSE, sSE.n))
            for h in range(H):
                ins = e.activation(msb_t[:, h * 256:(h + 1) * 256], msb_t[:, h * 256:(h + 1) * 256], AF.Exp, scale=-0.5)
            sSE.inc(ins)
            st[f"pb{c}rstd"] = sSE.n
        on("scalar", pb_s2)

        def pb_v3(e, en, c=c):
            need(e, en, (sSE, st[f"pb{c}rstd"]), st[f"pb{c}in"])
            for cv in range(MV):
                h = cv // 4
                need(e, en, (sVE, sVE.n))
                sVE.inc(e.scalar_tensor_tensor(och[:, cv * 256:(cv + 1) * 256], och[:, cv * 256:(cv + 1) * 256],
                                               rmsw_sb[:, cv % 4:cv % 4 + 1], msb_t[:, h * 256:(h + 1) * 256], ALU.mult, ALU.mult))
            need(e, en, (sVE, sVE.n))
            for cv in range(MV):
                need(e, en, (sVE, sVE.n))
                sVE.inc(e.tensor_tensor(och[:, cv * 256:(cv + 1) * 256], och[:, cv * 256:(cv + 1) * 256],
                                        gch[:, cv * 256:(cv + 1) * 256], ALU.mult))
            st[f"pb{c}of"] = sVE.n
        on("vector", pb_v3)

        for mo in range(KD):
            def pb_t3(e, en, c=c, mo=mo):
                need(e, en, (sVE, st[f"pb{c}of"]), st["wol"], st["xreload"])
                need(e, en, (sVE, st.get(f"pb{c}xr{mo-2}", 0)))
                ps = psD if mo % 2 == 0 else psE
                for cv in range(MV):
                    ins = e.matmul(ps[:, 0:256], woa[cv][:, mo * 128:(mo + 1) * 128], och[:, cv * 256:(cv + 1) * 256],
                                   start=(cv == 0), stop=(cv == MV - 1))
                sTE.inc(ins)
                st[f"pb{c}wo{mo}"] = sTE.n

            def pb_v4(e, en, c=c, c0=c0, mo=mo):
                ps = psD if mo % 2 == 0 else psE
                need(e, en, (sTE, st[f"pb{c}wo{mo}"]))
                sVE.inc(e.tensor_tensor(xg[mo][:, 1 + c0:1 + c0 + 256], ps[:, 0:256], xg[mo][:, 1 + c0:1 + c0 + 256], ALU.add))
                st[f"pb{c}xr{mo}"] = sVE.n
            on("tensor", pb_t3)
            on("vector", pb_v4)

        def pb_fin(e, en, c=c):
            need(e, en, (sVE, st[f"pb{c}xr{KD-1}"]))
            st[f"pb{c}done"] = sVE.n
        on("vector", pb_fin)
    st["x2done"] = st[f"pb{OC-1}done"]
    tap_sb("x2", xg, G, [(sVE, st["x2done"])])

    def c2g(e, en):
        need(e, en, (sVE, st["pb0done"]), (sVE, st["x2done"]))
        for k in range(KD):
            qGP.begin(e, en)
            qGP.inc(e.dma_start(out=cc2_s[k * 128:(k + 1) * 128].rearrange("(p o) -> p o", o=1), in_=xg[k][:, 1:2]))
            qGP.inc(e.dma_start(out=cc2_s[D + k * 128:D + (k + 1) * 128].rearrange("(p o) -> p o", o=1), in_=xg[k][:, SEG:SEG + 1]))
        st["cc2s"] = qGP.end()
        need(e, en, st["cc2s"])
        sCC.inc(e.collective_compute(
            "AllGather", ALU.bypass,
            replica_groups=[[0, 1, 2, 3], [4, 5, 6, 7]],
            ins=[cc2_s], outs=[cc2_r]))
        st["ag2"] = sCC.n
        need(e, en, (sCC, st["ag2"]))
        for k in range(KD):
            qGP.begin(e, en)
            qGP.inc(e.dma_start(out=rows8[:, k * 8:(k + 1) * 8].rearrange("p (j f) -> p j f", f=2),
                                in_=cc2_r.rearrange("(j f p) -> p j f", f=2, p=D)[k * 128:(k + 1) * 128, :, :],
                                allow_slow_non_contiguous=True))
        st["rows8l"] = qGP.end()
    on("gpsimd", c2g)

    def c2v(e, en):
        need(e, en, st["rows8l"], st["consts_loaded"], st.get("tap_x2", 0))
        for k in range(KD):
            for j in range(4):
                if j == 0:
                    e.tensor_scalar(xg[k][:, 0:1], rows8[:, k * 8 + j * 2 + 1:k * 8 + j * 2 + 2], ctl_sb[:, 20 + j:21 + j], None, ALU.mult)
                    ins = e.tensor_scalar(xg[k][:, G - 3:G - 2], rows8[:, k * 8 + j * 2:k * 8 + j * 2 + 1], ctl_sb[:, 24 + j:25 + j], None, ALU.mult)
                else:
                    e.scalar_tensor_tensor(xg[k][:, 0:1], rows8[:, k * 8 + j * 2 + 1:k * 8 + j * 2 + 2], ctl_sb[:, 20 + j:21 + j], xg[k][:, 0:1], ALU.mult, ALU.add)
                    ins = e.scalar_tensor_tensor(xg[k][:, G - 3:G - 2], rows8[:, k * 8 + j * 2:k * 8 + j * 2 + 1], ctl_sb[:, 24 + j:25 + j], xg[k][:, G - 3:G - 2], ALU.mult, ALU.add)
                sVE.inc(ins)
                need(e, en, (sVE, sVE.n))
        st["halo_fixed_v"] = sVE.n
    on("vector", c2v)
    st["halo_fixed"] = st["halo_fixed_v"]

    ffn_sublayer(1, dep=[(sVE, st["x2done"]), (sVE, st["halo_fixed"])])

    def out_s(e, en):
        need(e, en, (sVE, st["f1resid"]))
        for k in range(KD):
            qOUT.begin(e, en)
            qOUT.inc(e.dma_start(out=y_out[k * 128:(k + 1) * 128, :], in_=xg[k][:, 1:1 + SEG]))
        st["yout"] = qOUT.end()
        qOUT.drain(e, en)
        qGP.drain(e, en)
        qIN.drain(e, en)
    on("sync", out_s)

    if merge:
        for ename in ("sync", "gpsimd", "tensor", "vector", "scalar"):
            def run_all(e, fs=tuple(equeue[ename]), en=ename):
                for f in fs:
                    f(e, en)
            getattr(block, ename)(run_all)
    block_cm.__exit__(None, None, None)
    return nc, st


def _prep_core_inputs(core, SEG, x_full, w):
    b, s = core // 4, core % 4
    N = 4 * SEG
    t0 = s * SEG
    G = SEG + 4
    xg = np.zeros((G, 1024), np.float32)
    lo, hi = t0 - 1, t0 + SEG + 1
    slo, shi = max(lo, 0), min(hi, N)
    xg[slo - lo:shi - lo, :] = x_full[b, slo:shi, :]
    d = {
        "x": np.ascontiguousarray(xg.T),
        "wq_t": np.ascontiguousarray(w["Wq"].T),
        "wk_t": np.ascontiguousarray(w["Wk"].T),
        "wv_t": np.ascontiguousarray(w["Wv"].T),
        "wg_t": np.ascontiguousarray(w["Wg"].T),
        "wgk1_t": np.ascontiguousarray(w["Wgk1"].T),
        "wgk2_t": np.ascontiguousarray(w["Wgk2"].T),
        "bgk2": np.ascontiguousarray(w["bgk2"]),
        "wout_t": np.ascontiguousarray(w["Wout"].T),
        "w1_t": np.ascontiguousarray(w["w1"].T),
        "w2_t": np.ascontiguousarray(w["w2"].T),
        "conv_w": np.ascontiguousarray(w["conv_w"][:, 0, :].T.reshape(3, -1) if w["conv_w"].ndim == 3 else w["conv_w"]),
        "conv_b": np.ascontiguousarray(w["conv_b"]),
        "rms_w": np.ascontiguousarray(w["rms_w"]),
    }
    cw = w["conv_w"].reshape(4096, 3)
    d["conv_w"] = np.ascontiguousarray(cw.T)
    ctl = np.zeros(32, np.float32)
    g = s
    for j in range(4):
        for l in range(4):
            ctl[4 * j + l] = 1.0 if (j <= l < g) else 0.0
    for j in range(4):
        ctl[16 + j] = 0.0 if j < g else NEG
    for j in range(4):
        ctl[20 + j] = 1.0 if j == g - 1 else 0.0
        ctl[24 + j] = 1.0 if j == g + 1 else 0.0
    d["ctl"] = np.tile(ctl[None, :], (128, 1)).astype(np.float32)
    return d


def kernel(L0_attn_x, L0_attn_Wq, L0_attn_Wk, L0_attn_Wv, L0_attn_Wg,
           L0_attn_Wgk1, L0_attn_Wgk2, L0_attn_bgk2, L0_attn_Wout, L0_attn_rms_w,
           L0_ffn_w1, L0_ffn_conv_w, L0_ffn_conv_b, L0_ffn_w2):
    SEG = 2048
    x_full = np.asarray(L0_attn_x, np.float32)
    w = {
        "Wq": np.asarray(L0_attn_Wq, np.float32), "Wk": np.asarray(L0_attn_Wk, np.float32),
        "Wv": np.asarray(L0_attn_Wv, np.float32), "Wg": np.asarray(L0_attn_Wg, np.float32),
        "Wgk1": np.asarray(L0_attn_Wgk1, np.float32), "Wgk2": np.asarray(L0_attn_Wgk2, np.float32),
        "bgk2": np.asarray(L0_attn_bgk2, np.float32), "Wout": np.asarray(L0_attn_Wout, np.float32),
        "rms_w": np.asarray(L0_attn_rms_w, np.float32),
        "w1": np.asarray(L0_ffn_w1, np.float32), "conv_w": np.asarray(L0_ffn_conv_w, np.float32),
        "conv_b": np.asarray(L0_ffn_conv_b, np.float32), "w2": np.asarray(L0_ffn_w2, np.float32),
    }
    nc = build_program(SEG=SEG)
    in_maps = [_prep_core_inputs(c, SEG, x_full, w) for c in range(8)]
    res = run_bass_kernel_spmd(nc, in_maps, core_ids=list(range(8)))
    out = np.empty((2, 8192, 1024), np.float32)
    for c in range(8):
        b, s = c // 4, c % 4
        out[b, s * SEG:(s + 1) * SEG, :] = res.results[c]["y"].T
    return out


# version 14
# speedup vs baseline: 1.0519x; 1.0325x over previous
import numpy as np
import concourse.bass as bass
import concourse.mybir as mybir
from concourse.bass_utils import run_bass_kernel_spmd

F32 = mybir.dt.float32
BF16 = mybir.dt.bfloat16
F16 = mybir.dt.float16
I8 = mybir.dt.int8
AF = mybir.ActivationFunctionType
ALU = mybir.AluOpType
AXX = mybir.AxisListType.X

D, DH, DK, DV, H, HQK, HV = 1024, 4096, 1024, 2048, 4, 256, 512
KD, MH, MV = D // 128, DH // 128, DV // 128
LN16 = float(np.log(1.0 / 16.0))
EPS = 1e-5
NEG = -1.0e20
MAGIC = 12582912.0
SZ = {F32: 4, BF16: 2, F16: 2, I8: 1}


def ceil_div(a, b):
    return (a + b - 1) // b


def rnd64(x):
    return (x + 63) // 64 * 64


def build_program(SEG=2048, taps=()):
    _, st1 = _build(SEG, taps, None, False)
    nc, _ = _build(SEG, taps, st1, True)
    return nc


def _build(SEG=2048, taps=(), st_seed=None, merge=False):
    OC = SEG // 256
    NJ = SEG // 128
    G = SEG + 4
    GV = SEG + 2
    NT = ceil_div(GV, 512)
    TT = [(i * 512, min(512, GV - i * 512)) for i in range(NT)]
    NTT = max(1, SEG // 512)
    TK = [(1 + i * 512, min(512, SEG - i * 512)) for i in range(NTT)]
    NS = ceil_div(GV, 128)
    NSS = SEG // 128
    NPAIR = max(1, SEG // 512)
    PW = SEG // NPAIR

    nc = bass.Bass("TRN2", target_bir_lowering=False)

    def din(name, shape):
        return nc.dram_tensor(name, list(shape), F32, kind="ExternalInput").ap()

    x_in = din("x", [D, G])
    wq_t, wk_t = din("wq_t", [D, DK]), din("wk_t", [D, DK])
    wv_t, wg_t = din("wv_t", [D, DV]), din("wg_t", [D, DV])
    wgk1_t, wgk2_t = din("wgk1_t", [D, 32]), din("wgk2_t", [32, DK])
    bgk2_i = din("bgk2", [DK])
    wout_t = din("wout_t", [DV, D])
    w1_t, w2_t = din("w1_t", [D, DH]), din("w2_t", [DH, D])
    conv_w_i, conv_b_i = din("conv_w", [3, DH]), din("conv_b", [DH])
    rms_w_i = din("rms_w", [HV])
    ctl_i = din("ctl", [128, 32])

    y_out = nc.dram_tensor("y", [D, SEG], F32, kind="ExternalOutput").ap()
    tap_t = {}
    for name, shape in taps:
        tap_t[name] = nc.dram_tensor("tap_" + name, list(shape), F32, kind="ExternalOutput").ap()

    w1q_d = nc.dram_tensor("w1q_d", [MH * KD * 128 * 128], BF16).ap()
    w2q_d = nc.dram_tensor("w2q_d", [KD * MH * 128 * 128], BF16).ap()
    h2_d = nc.dram_tensor("h2_d", [MH, 128, SEG], BF16).ap()
    o_d = nc.dram_tensor("o_d", [MV, 128, SEG], BF16).ap()
    qt_d = nc.dram_tensor("qt_d", [KD, 128, SEG], BF16).ap()
    v_d = nc.dram_tensor("v_d", [NJ, 128, DV], BF16).ap()
    gate_d = nc.dram_tensor("gate_d", [MV, 128, SEG], BF16).ap()
    x1_d = nc.dram_tensor("x1_d", [D, G], F32).ap()
    cc_s = nc.dram_tensor("cc_s", [H * 257 * 512], F32).ap()
    cc_r = nc.dram_tensor("cc_r", [4 * H * 257 * 512], F32).ap()
    cc2_s = nc.dram_tensor("cc2_s", [2 * D], F32).ap()
    cc2_r = nc.dram_tensor("cc2_r", [4 * 2 * D], F32).ap()

    psA = nc.alloc_psum_tensor("psA", [128, 512], F32)
    psB = nc.alloc_psum_tensor("psB", [128, 512], F32)
    psC = nc.alloc_psum_tensor("psC", [128, 512], F32)
    psD = nc.alloc_psum_tensor("psD", [128, 512], F32)
    psE = nc.alloc_psum_tensor("psE", [128, 512], F32)
    psF = nc.alloc_psum_tensor("psF", [128, 512], F32)
    psG = nc.alloc_psum_tensor("psG", [128, 512], F32)
    psT = nc.alloc_psum_tensor("psT", [128, 1024], BF16)

    SB = nc.alloc_sbuf_tensor
    ident = SB("ident", [128, 128], BF16)
    identf = SB("identf", [128, 128], F32)
    maskJT = SB("maskJT", [128, 128], BF16)
    ones_b = SB("ones_b", [128, 128], BF16)
    onesf = SB("onesf", [128, 128], F32)
    ones1f = SB("ones1f", [1, 128], F32)
    ctl_sb = SB("ctl_sb", [128, 32], F32)
    convb_sb = SB("convb_sb", [128, MH], F32)
    convw_sb = SB("convw_sb", [128, 3 * MH], F32)
    convw_s = SB("convw_s", [128, 3 * MH], F32)
    rmsw_sb = SB("rmsw_sb", [128, 4], F32)
    bgk2_sb = SB("bgk2_sb", [128, KD], F32)
    wsum = SB("wsum", [128, 70], F32)
    scal = SB("scal", [1, 16], F32)
    scalb = SB("scalb", [128, 16], F32)
    Brun = SB("Brun", [128, KD], F32)
    ebias = SB("ebias", [128, 2 * KD], F32)
    stat_tok = SB("stat_tok", [128, 40], F32)
    rows8 = SB("rows8", [128, 8 * KD], F32)
    Lg = SB("Lg", [128, 4 * KD], F32)
    eacc = SB("eacc", [128, KD], F32)
    ecol = SB("ecol", [128, KD], F32)
    sjb = SB("sjb", [128, 512], F32)
    wst = [SB(f"wst{i}", [128, 1024], BF16) for i in range(2)]
    xbfr = SB("xbfr", [128, 1024], BF16)
    x2br = SB("x2br", [128, 1024], BF16)
    wgk1a = [SB(f"wgk1a{k}", [128, 32], BF16) for k in range(KD)]
    wgk2a = SB("wgk2a", [32, DK], BF16)
    zz256 = SB("zz256", [128, 256], F16)
    Abuf = SB("Abuf", [128, 3 * 128], BF16)
    hsel = SB("hsel", [128, 16], F32)
    ytmp = SB("ytmp", [128, 512], F32)
    epsc = SB("epsc", [128, 1], F32)
    zzf2 = SB("zzf2", [128, 2], F32)
    rowsc = SB("rowsc", [1, 512], F32)
    rowsc2 = SB("rowsc2", [1, 512], F32)
    o2scr = SB("o2scr", [128, 512], BF16)
    wvev = SB("wvev", [128, 512], BF16)

    base0 = rnd64(nc.sbuf_base)
    top = nc.sbuf_top

    def at(name, shape, dtype, off):
        per = int(np.prod(shape[1:])) * SZ[dtype]
        assert base0 + off + per <= top, f"{name}: off {off}+{per} exceeds {top - base0}"
        return nc.alloc_sbuf_tensor_at(name, list(shape), dtype, offset=base0 + off)

    RX = 0
    GB = rnd64(G * 4)
    xg = [at(f"xg{k}", [128, G], F32, RX + k * GB) for k in range(KD)]
    SGB = rnd64(SEG * 2)
    g16 = [at(f"g16_{k}", [128, SEG], F16, RX + k * SGB) for k in range(KD)]
    wqa = [at(f"wqa{k}", [128, DK], BF16, RX + 8 * SGB + k * 2048) for k in range(KD)]
    wka = [at(f"wka{k}", [128, DK], BF16, RX + 8 * SGB + 16384 + k * 2048) for k in range(KD)]
    RXE = max(8 * GB, 8 * SGB + 32768)

    RA = RXE
    GVB = rnd64(GV * 4)
    GV2B = rnd64(GV * 2)
    SEGB2 = rnd64(SEG * 2)
    sT1 = at("sT1", [128, GV], F32, RA + 0)
    sT2 = at("sT2", [128, GV], F32, RA + GVB)
    sScr = at("sScr", [128, GV], F32, RA + 2 * GVB)
    maxP = at("maxP", [128, NS * 128], BF16, RA + 3 * GVB)
    minP = at("minP", [128, NS * 128], BF16, RA + 3 * GVB + rnd64(NS * 256))
    hB = [at(f"hB{m}", [128, GV], BF16, RA + m * GV2B) for m in range(8)]
    OF1 = max(8 * GV2B, MH * 1024)
    h2t = at("h2t", [128, SEG], BF16, RA + OF1)
    habs = at("habs", [128, SEG], BF16, RA + OF1 + SEGB2)
    hmax = at("hmax", [128, SEG], BF16, RA + OF1 + 2 * SEGB2)
    OF2 = OF1 + 3 * SEGB2
    diag = [at(f"diag{i}", [128, 128], BF16, RA + OF2 + i * 256) for i in range(24)]
    OF3 = OF2 + 16384
    sT3 = at("sT3", [128, GV], F32, RA + OF3)
    sT4 = at("sT4", [128, GV], F32, RA + OF3 + GVB)
    RAE = RA + OF3 + 2 * GVB
    assert rnd64(PW * 2) == 1024
    xq2A = [at(f"xq2A_{k}", [128, PW], BF16, RA + k * 1024) for k in range(MH)]
    eqc = at("eqc", [128, KD * 256], BF16, RA + 0)
    ekc = at("ekc", [128, KD * 256], BF16, RA + 4224)
    qtc = at("qtc", [128, KD * 256], BF16, RA + 8448)
    ktc = at("ktc", [128, KD * 256], BF16, RA + 12672)
    ktok = at("ktok", [128, 2 * DK], BF16, RA + 16896)
    csc = at("csc", [128, KD * 256], F32, RA + 21120)
    och = at("och", [128, MV * 256], BF16, RA + 29312)
    vtk = [at(f"vtk{i}", [128, 2 * DV], BF16, RA + 37504 + i * 8192) for i in range(2)]
    T32 = at("T32", [128, KD * 512], F32, RA + 53888)
    T16 = at("T16", [128, KD * 512], BF16, RA + 70272)
    ALE = 70272 + 8192
    u_bf = at("u_bf", [32, SEG], BF16, RA + 0)
    sgs = [at(f"sgs_{k}", [128, SEG // 2], F32, RA + 4224 + k * rnd64(SEG * 2)) for k in range(KD)]
    wva = [at(f"wva{k}", [128, DV], BF16, RA + 4224 + 8 * rnd64(SEG * 2) + k * 4096) for k in range(KD)]
    vtmp = at("vtmp", [128, DV], BF16, RA + 4224 + 8 * rnd64(SEG * 2) + 32768)
    APRE = 4224 + 8 * rnd64(SEG * 2) + 32768 + 4096
    Sin = at("Sin", [128, KD * 512], F32, RA + 0)
    msb_t = at("msb_t", [128, H * 256], F32, RA + 4224)
    wga = [at(f"wga{k}", [128, DV], BF16, RA + 37504 + k * 4096) for k in range(KD)]
    woa = [at(f"woa{k}", [128, D], BF16, RA + 37504 + k * 2048) for k in range(MV)]
    gch = at("gch", [128, MV * 256], BF16, RA + 70272)
    S16b = at("S16b", [128, KD * 512], BF16, RA + 16896)
    RAE2 = ALE

    RB = RA + max(RAE - RA, RAE2, APRE, MH * rnd64(PW * 2))
    xq = [at(f"xq{k}", [128, GV], BF16, RB + k * GV2B) for k in range(KD)]
    w2sb = [at(f"w2sb{i}", [128, 4096], BF16, RB + i * 8192) for i in range(2)]
    _xq2b_off = lambda k: (RA + OF1 + k * 1024) if k < 28 else (RB + 16448 + (k - 28) * 1024)
    xq2B = [at(f"xq2B_{k}", [128, PW], BF16, _xq2b_off(k)) for k in range(MH)]
    NH2U = 4 if 8 * GV2B >= 24640 else 2
    h2u = [at(f"h2u{i}", [128, PW], BF16, RB + 20544 + i * 1024) for i in range(NH2U)]
    xn2 = [at(f"xn2_{k}", [128, SEG], BF16, RB + k * SGB) for k in range(KD)]
    FZ = RB + 8 * GV2B
    wq32d = [at(f"wq32d{i}", [128, 1024], F32, RA + OF2 + 6144 + i * 4096) for i in range(2)]
    _statE = rnd64(3 * GVB + 2 * rnd64(NS * 256))
    w2q32d = [at(f"w2q32d{i}", [128, 1024], F32, RA + max(OF1, _statE) + i * 4096) for i in range(2)]
    wqbd = at("wqbd", [128, 2048], BF16, FZ)
    absb = at("absb", [128, 1024], BF16, FZ + 4096)
    RBE = FZ + 6144
    assert base0 + RBE <= top, f"total arena {RBE} over {top - base0}"

    class Sem:
        def __init__(self, name, dk=1):
            self.h = nc.alloc_semaphore(name)
            self.n = 0
            self.name = name
            self.dk = dk

        def inc(self, inst, k=None):
            k = self.dk if k is None else k
            inst.then_inc(self.h, k)
            self.n += k
            return inst

    sTE, sVE, sSE, sGC, sCC = (Sem(x) for x in ("sTE", "sVE", "sSE", "sGC", "sCC"))

    marks = {}

    def need(e, en, *deps):
        for d in deps:
            if not d:
                continue
            if isinstance(d, list):
                need(e, en, *d)
                continue
            sem, v = d
            if v is None or v <= 0:
                continue
            key = (en, sem.name)
            if marks.get(key, 0) < v:
                own = {"tensor": sTE, "vector": sVE, "scalar": sSE}.get(en)
                if own is not None and sem is own and v > sem.n:
                    import sys as _s
                    print(f"SELF-FUTURE WAIT: {en} waits {sem.name}>={v} at {sem.n}", file=_s.stderr)
                e.wait_ge(sem.h, v)
                marks[key] = v

    class DQ:

        def __init__(self, name):
            self.s = [Sem(name + "0", 16), Sem(name + "1", 16)]
            self.k = 0
            self.last_end = [0, 0]

        def begin(self, e, en):
            p = self.k % 2
            need(e, en, (self.s[p], self.last_end[p]))

        def inc(self, inst):
            self.s[self.k % 2].inc(inst)
            return inst

        def end(self):
            p = self.k % 2
            tok = (self.s[p], self.s[p].n)
            self.last_end[p] = self.s[p].n
            self.k += 1
            return tok

        def drain(self, e, en):
            need(e, en, (self.s[0], self.s[0].n), (self.s[1], self.s[1].n))

        def fence(self):
            return [(self.s[0], self.s[0].n), (self.s[1], self.s[1].n)]

    qIN, qGP, qOUT = DQ("qIN"), DQ("qGP"), DQ("qOUT")

    block_cm = nc.Block()
    block = block_cm.__enter__()
    equeue = {k: [] for k in ("sync", "gpsimd", "tensor", "vector", "scalar")}

    def on(ename, f):
        if merge:
            equeue[ename].append(f)
        else:
            getattr(block, ename)(lambda e, f=f, ename=ename: f(e, ename))

    def _tokval(v):
        if isinstance(v, tuple):
            return ("t", v[0].name, v[1])
        if isinstance(v, list):
            return tuple(_tokval(x) for x in v)
        return v

    if st_seed:
        sem_by_name = {s.name: s for s in (sTE, sVE, sSE, sGC, sCC)}
        for q in (qIN, qGP, qOUT):
            for s in q.s:
                sem_by_name[s.name] = s

        def _remap(v):
            if isinstance(v, tuple):
                return (sem_by_name[v[0].name], v[1])
            if isinstance(v, list):
                return [_remap(x) for x in v]
            return v

        class _ChkDict(dict):
            def __setitem__(self, k, v):
                if k in self and _tokval(self[k]) != _tokval(v):
                    raise AssertionError(f"two-pass mismatch at {k}: {_tokval(self[k])} -> {_tokval(v)}")
                dict.__setitem__(self, k, _remap(v) if True else v)
        st = _ChkDict({k: _remap(v) for k, v in st_seed.items()})
    else:
        st = {}

    def ph0_sync(e, en):
        for k in range(KD):
            qIN.begin(e, en)
            qIN.inc(e.dma_start(out=xg[k][:, :], in_=x_in[k * 128:(k + 1) * 128, :]))
        st["x_loaded"] = qIN.end()
        qIN.begin(e, en)
        qIN.inc(e.dma_start(out=ctl_sb[:, :], in_=ctl_i))
        for m in range(MH):
            qIN.inc(e.dma_start(out=convb_sb[:, m:m + 1], in_=conv_b_i[m * 128:(m + 1) * 128].rearrange("(p o) -> p o", o=1)))
        for tp in range(3):
            for m in range(MH):
                qIN.inc(e.dma_start(out=convw_sb[:, tp * MH + m:tp * MH + m + 1], in_=conv_w_i[tp, m * 128:(m + 1) * 128].rearrange("(p o) -> p o", o=1)))
        for c in range(4):
            qIN.inc(e.dma_start(out=rmsw_sb[:, c:c + 1], in_=rms_w_i[c * 128:(c + 1) * 128].rearrange("(p o) -> p o", o=1)))
        for k in range(KD):
            qIN.inc(e.dma_start(out=bgk2_sb[:, k:k + 1], in_=bgk2_i[k * 128:(k + 1) * 128].rearrange("(p o) -> p o", o=1)))
        st["consts_loaded"] = qIN.end()

    def ph0_gp(e, en):
        sGC.inc(e.memset(ident[:, :], 0.0))
        sGC.inc(e.memset(identf[:, :], 0.0))
        sGC.inc(e.memset(maskJT[:, :], 0.0))
        need(e, en, (sGC, sGC.n))
        sGC.inc(e.affine_select(out=ident[:, :], in_=ident[:, :], compare_op=ALU.not_equal,
                                fill=1.0, base=0, pattern=[[-1, 128]], channel_multiplier=1))
        sGC.inc(e.affine_select(out=identf[:, :], in_=identf[:, :], compare_op=ALU.not_equal,
                                fill=1.0, base=0, pattern=[[-1, 128]], channel_multiplier=1))
        sGC.inc(e.affine_select(out=maskJT[:, :], in_=maskJT[:, :], compare_op=ALU.is_gt,
                                fill=1.0, base=0, pattern=[[-1, 128]], channel_multiplier=1))
        e.memset(ones_b[:, :], 1.0)
        e.memset(onesf[:, :], 1.0)
        e.memset(ones1f[:, :], 1.0)
        e.memset(zz256[:, :], 0.0)
        e.memset(epsc[:, :], float(EPS))
        e.memset(zzf2[:, :], 0.0)
        e.memset(scal[:, :], 0.0)
        e.memset(wsum[:, :], 0.0)
        e.memset(stat_tok[:, :], 0.0)
        e.memset(rowsc[:, :], 0.0)
        e.memset(rowsc2[:, :], 0.0)
        sGC.inc(e.memset(Brun[:, :], 0.0))
        st["consts_built"] = sGC.n

    on("sync", ph0_sync)
    on("gpsimd", ph0_gp)

    NWT = 32

    def w_src_ap(widx, i):
        if widx == 0:
            return w1_t[(i // 4) * 128:(i // 4 + 1) * 128, (i % 4) * 1024:(i % 4 + 1) * 1024]
        return w2_t[i * 128:(i + 1) * 128, :]

    def w_out_ap(widx, i):
        out_d = w1q_d if widx == 0 else w2q_d
        off = ((i // 4) * 32 + (i % 4) * 8) * 16384 if widx == 0 else i * 8 * 16384
        return out_d[off:off + 8 * 16384].rearrange("(m p q) -> p m q", p=128, q=128)

    def qw_pass1_step(widx, i):
        buf = wq32d if widx == 0 else w2q32d
        q, eng = (qIN, "sync") if widx == 0 else (qGP, "gpsimd")

        def f1(e, en, widx=widx, i=i, q=q, buf=buf):
            need(e, en, (sSE, st.get(f"wabs{widx}_{i-2}", 0)))
            if widx == 1 and i < 2:
                need(e, en, (sSE, st[f"wabs0_{NWT-2+i}"]))
            q.begin(e, en)
            q.inc(e.dma_start(out=buf[i % 2][:, :], in_=w_src_ap(widx, i)))
            st[f"wl{widx}_{i}"] = q.end()

        def f1v(e, en, widx=widx, i=i, buf=buf):
            need(e, en, st[f"wl{widx}_{i}"], (sGC, st["consts_built"]))
            need(e, en, (sSE, sSE.n))
            sSE.inc(e.activation(absb[:, :], buf[i % 2][:, :], AF.Abs,
                                 accum_out=wsum[:, widx * 32 + i:widx * 32 + i + 1]))
            st[f"wabs{widx}_{i}"] = sSE.n
        on(eng, f1)
        on("scalar", f1v)

    def qw_scale(widx):
        def p1f(e, en):
            need(e, en, (sSE, st[f"wabs{widx}_{NWT-1}"]))
            need(e, en, (sVE, sVE.n))
            sVE.inc(e.tensor_reduce(wsum[:, 64 + widx:65 + widx], wsum[:, widx * 32:widx * 32 + NWT], AXX, ALU.add))
            st[f"wcol{widx}"] = sVE.n
        on("vector", p1f)

        def p2t(e, en):
            need(e, en, (sVE, st[f"wcol{widx}"]), (sGC, st["consts_built"]))
            sTE.inc(e.matmul(psF[0:1, 0:128], wsum[:, 64 + widx:65 + widx], onesf[:, 0:128], start=True, stop=True))
            st[f"wtot{widx}"] = sTE.n
        on("tensor", p2t)

        def p3v(e, en):
            need(e, en, (sTE, st[f"wtot{widx}"]), (sGC, st["consts_built"]))
            numel = float(DH * D)
            sVE.inc(e.tensor_scalar(scal[0:1, widx:widx + 1], psF[0:1, 0:1], 1.0 / numel, 1e-5, ALU.mult, ALU.max))
            need(e, en, (sVE, sVE.n))
            sVE.inc(e.reciprocal(scal[0:1, 2 + widx:3 + widx], scal[0:1, widx:widx + 1]))
            st[f"wrec{widx}"] = sVE.n
        on("vector", p3v)

        def p4t(e, en):
            need(e, en, (sVE, st[f"wrec{widx}"]))
            sTE.inc(e.matmul(psF[0:128, 0:16], ones1f[0:1, :], scal[0:1, 0:16], start=True, stop=True))
            st[f"wbc{widx}"] = sTE.n
        on("tensor", p4t)

        def p4v(e, en):
            need(e, en, (sTE, st[f"wbc{widx}"]))
            sVE.inc(e.tensor_copy(scalb[:, 0:16], psF[0:128, 0:16]))
            st[f"wscal{widx}"] = sVE.n
        on("vector", p4v)

    def qw_pass2_step(widx, i):
        buf = wq32d
        q = qIN if widx == 0 else qOUT

        def p5s(e, en, widx=widx, i=i, buf=buf, q=q):
            need(e, en, (sVE, st[f"wcol{widx}"]))
            if i >= 2:
                need(e, en, (sVE, st[f"wqd{widx}_{i-2}"]))
            elif widx == 1:
                need(e, en, (sVE, st[f"wqd0_{NWT-2+i}"]))
            q.begin(e, en)
            q.inc(e.dma_start(out=buf[i % 2][:, :], in_=w_src_ap(widx, i)))
            st[f"wl2{widx}_{i}"] = q.end()

        def p5v(e, en, widx=widx, i=i, buf=buf):
            need(e, en, st[f"wl2{widx}_{i}"], (sVE, st[f"wscal{widx}"]))
            need(e, en, (sVE, sVE.n))
            sVE.inc(e.tensor_scalar(buf[i % 2][:, :], buf[i % 2][:, :], scalb[:, 2 + widx:3 + widx], 1.49, ALU.mult, ALU.min))
            need(e, en, (sVE, sVE.n))
            sVE.inc(e.tensor_scalar(buf[i % 2][:, :], buf[i % 2][:, :], -1.49, MAGIC, ALU.max, ALU.add))
            need(e, en, (sVE, sVE.n))
            if i >= 2:
                need(e, en, st[f"wstr{widx}_{i-2}"])
            elif widx == 1:
                need(e, en, st[f"wstr0_{NWT-2+i}"])
            sVE.inc(e.tensor_scalar(wqbd[:, (i % 2) * 1024:(i % 2 + 1) * 1024], buf[i % 2][:, :], -MAGIC, None, ALU.add))
            st[f"wqd{widx}_{i}"] = sVE.n

        def p5g(e, en, widx=widx, i=i):
            need(e, en, (sVE, st[f"wqd{widx}_{i}"]))
            qGP.begin(e, en)
            qGP.inc(e.dma_start(
                out=w_out_ap(widx, i),
                in_=wqbd[:, (i % 2) * 1024:(i % 2 + 1) * 1024].rearrange("p (m q) -> p m q", q=128)))
            st[f"wstr{widx}_{i}"] = qGP.end()
        on("sync", p5s)
        on("vector", p5v)
        on("gpsimd", p5g)

    for i in range(NWT):
        qw_pass1_step(0, i)
    qw_scale(0)
    for i in range(NWT):
        qw_pass2_step(0, i)
        qw_pass1_step(1, i)
    st["wready0"] = [st[f"wstr0_{NWT-2}"], st[f"wstr0_{NWT-1}"]]


    def convscale(e, en):
        need(e, en, st["consts_loaded"], (sVE, st["wscal0"]))
        sVE.inc(e.tensor_scalar(convw_s[:, :], convw_sb[:, :], scalb[:, 0:1], None, ALU.mult))
        st["convw_scaled"] = sVE.n
    on("vector", convscale)

    def ffn_stats(app, with_quant=True, dep=()):
        a = f"f{app}"

        def sv(e, en):
            need(e, en, st["x_loaded"], *dep)
            if with_quant:
                if NS * 128 > GV:
                    e.memset(maxP[:, GV:NS * 128], 0.0)
                    e.memset(minP[:, GV:NS * 128], 0.0)
                sVE.inc(e.tensor_copy(maxP[:, 0:GV], xg[0][:, 0:GV]))
                sVE.inc(e.tensor_copy(minP[:, 0:GV], xg[0][:, 0:GV]))
                for k in range(1, KD):
                    need(e, en, (sVE, sVE.n))
                    sVE.inc(e.tensor_tensor(maxP[:, 0:GV], xg[k][:, 0:GV], maxP[:, 0:GV], ALU.max))
                    sVE.inc(e.tensor_tensor(minP[:, 0:GV], xg[k][:, 0:GV], minP[:, 0:GV], ALU.min))
                st[a + "trees"] = sVE.n
        on("vector", sv)

        for ti, (c0, w) in enumerate(TT):
            for k in range(KD):
                r = k % 2

                def s1v(e, en, ti=ti, c0=c0, w=w, k=k, r=r):
                    need(e, en, st["x_loaded"], *dep)
                    g = ti * KD + k - 2
                    if g >= 0:
                        pt, pk = divmod(g, KD)
                        need(e, en, (sTE, st.get(f"{a}smm{pt}_{pk}", 0)))
                    sVE.inc(e.tensor_copy(xbfr[:, r * 512:r * 512 + w], xg[k][:, c0:c0 + w]))
                    st[f"{a}sxb{ti}_{k}"] = sVE.n

                def s1se(e, en, ti=ti, c0=c0, w=w, k=k, r=r):
                    need(e, en, st["x_loaded"], *dep)
                    g = ti * KD + k - 2
                    if g >= 0:
                        pt, pk = divmod(g, KD)
                        need(e, en, (sTE, st.get(f"{a}smm{pt}_{pk}", 0)))
                    sSE.inc(e.activation(x2br[:, r * 512:r * 512 + w], xg[k][:, c0:c0 + w], AF.Square))
                    st[f"{a}sx2{ti}_{k}"] = sSE.n

                def s1t(e, en, ti=ti, c0=c0, w=w, k=k, r=r):
                    if k == 0 and ti >= 1:
                        need(e, en, (sVE, st[f"{a}var{ti-1}"]))
                    need(e, en, (sVE, st[f"{a}sxb{ti}_{k}"]), (sSE, st[f"{a}sx2{ti}_{k}"]))
                    e.matmul(psA[:, 0:w], ones_b[:, :], xbfr[:, r * 512:r * 512 + w],
                             start=(k == 0), stop=(k == KD - 1))
                    mm = e.matmul(psB[:, 0:w], ones_b[:, :], x2br[:, r * 512:r * 512 + w],
                                  start=(k == 0), stop=(k == KD - 1))
                    sTE.inc(mm)
                    st[f"{a}smm{ti}_{k}"] = sTE.n
                on("vector", s1v)
                on("scalar", s1se)
                on("tensor", s1t)
            st[f"{a}stats{ti}"] = st[f"{a}smm{ti}_{KD-1}"]

            def s2v(e, en, ti=ti, c0=c0, w=w):
                need(e, en, (sTE, st[f"{a}stats{ti}"]), *dep)
                sVE.inc(e.tensor_scalar(sScr[:, c0:c0 + w], psA[:, 0:w], 1.0 / D, None, ALU.mult))
                need(e, en, (sVE, sVE.n))
                sVE.inc(e.tensor_mul(sT1[:, c0:c0 + w], sScr[:, c0:c0 + w], sScr[:, c0:c0 + w]))
                need(e, en, (sVE, sVE.n))
                sVE.inc(e.scalar_tensor_tensor(sT2[:, c0:c0 + w], psB[:, 0:w], 1.0 / D, sT1[:, c0:c0 + w], ALU.mult, ALU.subtract))
                st[f"{a}var{ti}"] = sVE.n
            on("vector", s2v)

        def s3se(e, en):
            need(e, en, (sVE, st[f"{a}var{NT-1}"]))
            need(e, en, (sGC, st["consts_built"]))
            sSE.inc(e.activation(sT2[:, 0:GV], sT2[:, 0:GV], AF.Ln, bias=epsc[:, 0:1]))
            need(e, en, (sSE, sSE.n))
            sSE.inc(e.activation(sT2[:, 0:GV], sT2[:, 0:GV], AF.Exp, scale=-0.5))
            st[a + "rstd"] = sSE.n
        on("scalar", s3se)

        if not with_quant:
            return

        for i in range(NS):
            def s4t(e, en, i=i):
                need(e, en, (sVE, st[a + "trees"]))
                need(e, en, (sVE, st.get(f"{a}tred{i-1}", 0)))
                e.transpose(psT[:, 0:128], maxP[:, i * 128:(i + 1) * 128], ident[:, :])
                sTE.inc(e.transpose(psT[:, 128:256], minP[:, i * 128:(i + 1) * 128], ident[:, :]))
                st[f"{a}tp{i}"] = sTE.n

            def s4v(e, en, i=i):
                need(e, en, (sTE, st[f"{a}tp{i}"]))
                e.tensor_reduce(stat_tok[:, i:i + 1], psT[:, 0:128], AXX, ALU.max)
                sVE.inc(e.tensor_reduce(stat_tok[:, 20 + i:21 + i], psT[:, 128:256], AXX, ALU.min))
                st[f"{a}tred{i}"] = sVE.n
            on("tensor", s4t)
            on("vector", s4v)

        def s4f(e, en):
            need(e, en, (sVE, st[f"{a}tred{NS-1}"]))
            st[a + "treds"] = sVE.n
        on("vector", s4f)

        NR = ceil_div(NS, 4)
        for rnd in range(NR):
            tl = list(range(rnd * 4, min(rnd * 4 + 4, NS)))
            def b1t(e, en, rnd=rnd, tl=tl):
                need(e, en, (sVE, st[a + "treds"]))
                need(e, en, (sVE, st.get(f"{a}rowev{rnd-1}", 0)))
                for ii, i in enumerate(tl):
                    ins = e.matmul(psF[0:1, (i % 4) * 128:(i % 4) * 128 + 128], stat_tok[:, i:i + 1], identf[:, :],
                                   start=(ii == 0), stop=(ii == len(tl) - 1))
                for ii, i in enumerate(tl):
                    ins = e.matmul(psG[0:1, (i % 4) * 128:(i % 4) * 128 + 128], stat_tok[:, 20 + i:21 + i], identf[:, :],
                                   start=(ii == 0), stop=(ii == len(tl) - 1))
                sTE.inc(ins)
                st[f"{a}rowmm{rnd}"] = sTE.n
            def b1v(e, en, rnd=rnd, tl=tl):
                need(e, en, (sTE, st[f"{a}rowmm{rnd}"]))
                e.tensor_copy(rowsc[0:1, 0:len(tl) * 128], psF[0:1, 0:len(tl) * 128])
                sVE.inc(e.tensor_copy(rowsc2[0:1, 0:len(tl) * 128], psG[0:1, 0:len(tl) * 128]))
                st[f"{a}rowev{rnd}"] = sVE.n
            def b2t(e, en, rnd=rnd, tl=tl):
                need(e, en, (sVE, st[f"{a}rowev{rnd}"]))
                need(e, en, (sVE, st.get(f"{a}bcev{rnd-1}", 0)))
                for ii, i in enumerate(tl):
                    e.matmul(psE[:, (i % 4) * 128:(i % 4) * 128 + 128], ones1f[0:1, :], rowsc[0:1, (i % 4) * 128:(i % 4) * 128 + 128],
                             start=(ii == 0), stop=(ii == len(tl) - 1))
                    ins = e.matmul(psD[:, (i % 4) * 128:(i % 4) * 128 + 128], ones1f[0:1, :], rowsc2[0:1, (i % 4) * 128:(i % 4) * 128 + 128],
                                   start=(ii == 0), stop=(ii == len(tl) - 1))
                sTE.inc(ins)
                st[f"{a}bcmm{rnd}"] = sTE.n
            def b2v(e, en, rnd=rnd, tl=tl):
                need(e, en, (sTE, st[f"{a}bcmm{rnd}"]))
                for i in tl:
                    w = min(128, GV - i * 128)
                    if w <= 0:
                        continue
                    e.tensor_copy(sT3[:, i * 128:i * 128 + w], psE[:, (i % 4) * 128:(i % 4) * 128 + w])
                    ins = e.tensor_copy(sT4[:, i * 128:i * 128 + w], psD[:, (i % 4) * 128:(i % 4) * 128 + w])
                sVE.inc(ins)
                st[f"{a}bcev{rnd}"] = sVE.n
            on("tensor", b1t)
            on("vector", b1v)
            on("tensor", b2t)
            on("vector", b2v)
        def bfin(e, en):
            need(e, en, (sVE, st[f"{a}bcev{NR-1}"]))
            st[a + "bcasts"] = sVE.n
        on("vector", bfin)

        def s7v(e, en):
            need(e, en, (sVE, st[a + "bcasts"]), (sSE, st[a + "rstd"]))
            sVE.inc(e.tensor_sub(sT3[:, 0:GV], sT3[:, 0:GV], sScr[:, 0:GV]))
            sVE.inc(e.tensor_sub(sT4[:, 0:GV], sScr[:, 0:GV], sT4[:, 0:GV]))
            need(e, en, (sVE, sVE.n))
            sVE.inc(e.tensor_tensor(sT3[:, 0:GV], sT3[:, 0:GV], sT4[:, 0:GV], ALU.max))
            need(e, en, (sVE, sVE.n))
            sVE.inc(e.tensor_tensor(sT3[:, 0:GV], sT3[:, 0:GV], sT2[:, 0:GV], ALU.mult))
            need(e, en, (sVE, sVE.n))
            sVE.inc(e.tensor_scalar(sT3[:, 0:GV], sT3[:, 0:GV], 1e-5, None, ALU.max))
            need(e, en, (sVE, sVE.n))
            sVE.inc(e.tensor_scalar(sT4[:, 0:GV], sT3[:, 0:GV], 1.0 / 127.0, None, ALU.mult))
            sVE.inc(e.reciprocal(sT3[:, 0:GV], sT3[:, 0:GV]))
            need(e, en, (sVE, sVE.n))
            sVE.inc(e.tensor_scalar(sT3[:, 0:GV], sT3[:, 0:GV], 127.0, None, ALU.mult))
            need(e, en, (sVE, sVE.n))
            sVE.inc(e.tensor_tensor(sT3[:, 0:GV], sT3[:, 0:GV], sT2[:, 0:GV], ALU.mult))
            need(e, en, (sVE, sVE.n))
            sVE.inc(e.scalar_tensor_tensor(sT1[:, 0:GV], sScr[:, 0:GV], -1.0, sT3[:, 0:GV], ALU.mult, ALU.mult))
            st[a + "quantsc"] = sVE.n
        on("vector", s7v)

    def ffn_quant_apply(app):
        a = f"f{app}"

        def qv(e, en):
            need(e, en, (sVE, st[a + "quantsc"]))
            if app == 1:
                need(e, en, st.get("tap_xq0", 0), st.get("tap_xn2", 0))
            for k in range(KD):
                need(e, en, (sVE, sVE.n))
                sVE.inc(e.tensor_tensor(sScr[:, 0:GV], xg[k][:, 0:GV], sT3[:, 0:GV], ALU.mult))
                need(e, en, (sVE, sVE.n))
                sVE.inc(e.tensor_tensor(sScr[:, 0:GV], sScr[:, 0:GV], sT1[:, 0:GV], ALU.add))
                need(e, en, (sVE, sVE.n))
                sVE.inc(e.tensor_scalar(sScr[:, 0:GV], sScr[:, 0:GV], 127.0, -128.0, ALU.min, ALU.max))
                need(e, en, (sVE, sVE.n))
                sVE.inc(e.tensor_scalar(sScr[:, 0:GV], sScr[:, 0:GV], MAGIC, -MAGIC, ALU.add, ALU.add))
                need(e, en, (sVE, sVE.n))
                sVE.inc(e.tensor_tensor(xq[k][:, 0:GV], sScr[:, 0:GV], sT4[:, 0:GV], ALU.mult))
                st[f"{a}xq{k}"] = sVE.n
            st[a + "xqall"] = sVE.n
        on("vector", qv)

    def ffn_mm1_conv(app, extra_per_blk=None):
        a = f"f{app}"
        for blk in range(4):
            if extra_per_blk:
                extra_per_blk(blk)
            def dv(e, en, blk=blk):
                need(e, en, (sVE, st["convw_scaled"]), (sGC, st["consts_built"]))
                key = f"{a}cvu{blk-1}" if blk > 0 else (f"f{app-1}cvu3" if app > 0 else None)
                if key:
                    need(e, en, (sTE, st[key]))
                for t in range(3):
                    for mloc in range(8):
                        m = blk * 8 + mloc
                        sVE.inc(e.tensor_scalar(diag[t * 8 + mloc][:, :], ident[:, :], convw_s[:, t * MH + m:t * MH + m + 1], None, ALU.mult))
                st[f"{a}diag{blk}"] = sVE.n
            on("vector", dv)

            for mloc in range(8):
                m = blk * 8 + mloc
                r = m % 2

                def ws(e, en, m=m, r=r):
                    need(e, en, st["wready0"])
                    need(e, en, (sTE, st.get(f"{a}mmw{m-2}", 0)))
                    qIN.begin(e, en)
                    qIN.inc(e.dma_start(
                        out=wst[r][:, :].rearrange("p (k q) -> p k q", q=128),
                        in_=w1q_d.rearrange("(t p q) -> t p q", p=128, q=128)[m::MH, :, :].rearrange("k p q -> p k q")))
                    st[f"{a}wl{m}"] = qIN.end()

                on("sync", ws)

                def mt0(e, en, m=m, r=r):
                    need(e, en, st[f"{a}wl{m}"], (sVE, st[a + "xqall"]))
                on("tensor", mt0)
                for ti, (c0, w) in enumerate(TT):
                    def mt(e, en, m=m, r=r, ti=ti, c0=c0, w=w):
                        ps = psA if ti % 2 == 0 else psB
                        need(e, en, (sVE, st.get(f"{a}hev{m}_{ti-2}", 0)))
                        lp = ti + 2 * ((NT - 1 - ti) // 2)
                        need(e, en, (sVE, st.get(f"{a}hev{m-1}_{lp}", 0)))
                        for k in range(KD):
                            ins = e.matmul(ps[:, 0:w], wst[r][:, k * 128:(k + 1) * 128], xq[k][:, c0:c0 + w],
                                           start=(k == 0), stop=(k == KD - 1))
                        sTE.inc(ins)
                        st[f"{a}hmm{m}_{ti}"] = sTE.n
                        if ti == NT - 1:
                            st[f"{a}mmw{m}"] = sTE.n

                    def ev(e, en, m=m, mloc=mloc, ti=ti, c0=c0, w=w):
                        need(e, en, (sTE, st.get(f"{a}cmm{m-8}_{len(TK)-1}", 0)))
                        ps = psA if ti % 2 == 0 else psB
                        need(e, en, (sTE, st[f"{a}hmm{m}_{ti}"]))
                        sVE.inc(e.tensor_copy(hB[mloc][:, c0:c0 + w], ps[:, 0:w]))
                        st[f"{a}hev{m}_{ti}"] = sVE.n
                        if ti == NT - 1:
                            st[f"{a}hband{m}"] = sVE.n
                    on("tensor", mt)
                    on("vector", ev)

            for mloc in range(8):
                m = blk * 8 + mloc
                for ti, (c0, w) in enumerate(TK):
                    g = m * len(TK) + ti

                    def ct(e, en, m=m, mloc=mloc, blk=blk, ti=ti, c0=c0, w=w, g=g):
                        need(e, en, (sVE, st[f"{a}hband{m}"]), (sVE, st[f"{a}diag{blk}"]))
                        gp = g - 2
                        if gp >= 0:
                            pm, pt = divmod(gp, len(TK))
                            need(e, en, (sSE, st.get(f"{a}cse{pm}_{pt}", 0)))
                        ps = psC if g % 2 == 0 else psD
                        e.matmul(ps[:, 0:w], diag[0 * 8 + mloc][:, :], hB[mloc][:, c0 - 1:c0 - 1 + w], start=True, stop=False)
                        e.matmul(ps[:, 0:w], diag[1 * 8 + mloc][:, :], hB[mloc][:, c0:c0 + w], start=False, stop=False)
                        ins = e.matmul(ps[:, 0:w], diag[2 * 8 + mloc][:, :], hB[mloc][:, c0 + 1:c0 + 1 + w], start=False, stop=True)
                        sTE.inc(ins)
                        st[f"{a}cmm{m}_{ti}"] = sTE.n
                        if mloc == 7 and ti == len(TK) - 1:
                            st[f"{a}cvu{blk}"] = sTE.n

                    def cs(e, en, m=m, ti=ti, c0=c0, w=w, g=g):
                        ps = psC if g % 2 == 0 else psD
                        need(e, en, (sTE, st[f"{a}cmm{m}_{ti}"]))
                        if ti == 0:
                            need(e, en, st.get(f"{a}h2st{m-1}", 0))
                        sSE.inc(e.activation(h2t[:, c0 - 1:c0 - 1 + w], ps[:, 0:w], AF.Silu, bias=convb_sb[:, m:m + 1]))
                        st[f"{a}cse{m}_{ti}"] = sSE.n
                    on("tensor", ct)
                    on("scalar", cs)

                def csa(e, en, m=m):
                    need(e, en, (sSE, st[f"{a}cse{m}_{len(TK)-1}"]))
                    need(e, en, (sVE, st.get(f"{a}hmax{m-1}", 0)))
                    sSE.inc(e.activation(habs[:, 0:SEG], h2t[:, 0:SEG], AF.Abs))
                    st[f"{a}habs{m}"] = sSE.n

                def cv(e, en, m=m):
                    need(e, en, (sSE, st[f"{a}habs{m}"]))
                    if m == 0:
                        sVE.inc(e.tensor_copy(hmax[:, 0:SEG], habs[:, 0:SEG]))
                    else:
                        sVE.inc(e.tensor_tensor(hmax[:, 0:SEG], habs[:, 0:SEG], hmax[:, 0:SEG], ALU.max))
                    st[f"{a}hmax{m}"] = sVE.n

                def cg(e, en, m=m):
                    need(e, en, (sSE, st[f"{a}habs{m}"]))
                    qGP.begin(e, en)
                    qGP.inc(e.dma_start(out=h2_d[m, :, :], in_=h2t[:, 0:SEG]))
                    st[f"{a}h2st{m}"] = qGP.end()
                on("scalar", csa)
                on("vector", cv)
                on("gpsimd", cg)

    def ffn_quant2(app):
        a = f"f{app}"

        for i in range(NSS):
            def q2t(e, en, i=i):
                need(e, en, (sVE, st[f"{a}hmax{MH-1}"]))
                need(e, en, (sVE, st.get(f"{a}q2red{i-1}", 0)))
                sTE.inc(e.transpose(psT[:, 0:128], hmax[:, i * 128:(i + 1) * 128], ident[:, :]))
                st[f"{a}q2tp{i}"] = sTE.n

            def q2v(e, en, i=i):
                need(e, en, (sTE, st[f"{a}q2tp{i}"]))
                sVE.inc(e.tensor_reduce(stat_tok[:, i:i + 1], psT[:, 0:128], AXX, ALU.max))
                st[f"{a}q2red{i}"] = sVE.n
            on("tensor", q2t)
            on("vector", q2v)

        def q2f(e, en):
            need(e, en, (sVE, st[f"{a}q2red{NSS-1}"]))
            st[a + "q2reds"] = sVE.n
        on("vector", q2f)

        NR2 = ceil_div(NSS, 4)
        for rnd in range(NR2):
            tl = list(range(rnd * 4, min(rnd * 4 + 4, NSS)))
            def qb1t(e, en, rnd=rnd, tl=tl):
                need(e, en, (sVE, st[a + "q2reds"]))
                need(e, en, (sVE, st.get(f"{a}q2rowev{rnd-1}", 0)))
                for ii, i in enumerate(tl):
                    ins = e.matmul(psF[0:1, (i % 4) * 128:(i % 4) * 128 + 128], stat_tok[:, i:i + 1], identf[:, :],
                                   start=(ii == 0), stop=(ii == len(tl) - 1))
                sTE.inc(ins)
                st[f"{a}q2rowmm{rnd}"] = sTE.n
            def qb1v(e, en, rnd=rnd, tl=tl):
                need(e, en, (sTE, st[f"{a}q2rowmm{rnd}"]))
                sVE.inc(e.tensor_copy(rowsc[0:1, 0:len(tl) * 128], psF[0:1, 0:len(tl) * 128]))
                st[f"{a}q2rowev{rnd}"] = sVE.n
            def qb2t(e, en, rnd=rnd, tl=tl):
                need(e, en, (sVE, st[f"{a}q2rowev{rnd}"]))
                need(e, en, (sVE, st.get(f"{a}q2bcev{rnd-1}", 0)))
                for ii, i in enumerate(tl):
                    ins = e.matmul(psE[:, (i % 4) * 128:(i % 4) * 128 + 128], ones1f[0:1, :], rowsc[0:1, (i % 4) * 128:(i % 4) * 128 + 128],
                                   start=(ii == 0), stop=(ii == len(tl) - 1))
                sTE.inc(ins)
                st[f"{a}q2bcmm{rnd}"] = sTE.n
            def qb2v(e, en, rnd=rnd, tl=tl):
                need(e, en, (sTE, st[f"{a}q2bcmm{rnd}"]))
                for i in tl:
                    ins = e.tensor_copy(sT3[:, 1 + i * 128:1 + (i + 1) * 128], psE[:, (i % 4) * 128:(i % 4) * 128 + 128])
                sVE.inc(ins)
                st[f"{a}q2bcev{rnd}"] = sVE.n
            on("tensor", qb1t)
            on("vector", qb1v)
            on("tensor", qb2t)
            on("vector", qb2v)

        def q4v(e, en):
            need(e, en, (sVE, st[f"{a}q2bcev{NR2-1}"]))
            need(e, en, (sVE, sVE.n))
            sVE.inc(e.tensor_scalar(sT3[:, 1:1 + SEG], sT3[:, 1:1 + SEG], 1e-5, None, ALU.max))
            need(e, en, (sVE, sVE.n))
            sVE.inc(e.tensor_scalar(sT4[:, 1:1 + SEG], sT3[:, 1:1 + SEG], 0.5 / 127.0, None, ALU.mult))
            need(e, en, (sVE, sVE.n))
            sVE.inc(e.tensor_scalar(sT4[:, 1:1 + SEG], sT4[:, 1:1 + SEG], scalb[:, 1:2], None, ALU.mult))
            sVE.inc(e.reciprocal(sT3[:, 1:1 + SEG], sT3[:, 1:1 + SEG]))
            need(e, en, (sVE, sVE.n))
            sVE.inc(e.tensor_scalar(sT3[:, 1:1 + SEG], sT3[:, 1:1 + SEG], 127.0, None, ALU.mult))
            st[a + "q2sc"] = sVE.n
        on("vector", q4v)

    def ffn_mm2(app):
        a = f"f{app}"

        def emit_quant(pr, ks):
            pc0 = 1 + pr * PW
            xq2 = xq2A if pr % 2 == 0 else xq2B
            for k in ks:
                hb = h2u[k % NH2U]

                def l2g(e, en, k=k, pr=pr, pc0=pc0, hb=hb):
                    need(e, en, st[f"{a}h2st{k}"], st[f"{a}h2st{MH-1}"], (sSE, st[f"{a}habs{MH-1}"]))
                    need(e, en, (sVE, st.get(f"{a}xq2_{pr}_{k-NH2U}", 0)))
                    if k < NH2U and pr > 0:
                        need(e, en, (sVE, st.get(f"{a}xq2_{pr-1}_{MH-NH2U+k}", 0)))
                    qGP.begin(e, en)
                    qGP.inc(e.dma_start(out=hb[:, 0:PW], in_=h2_d[k, :, pc0 - 1:pc0 - 1 + PW]))
                    st[f"{a}l2{pr}_{k}"] = qGP.end()

                def l2v(e, en, k=k, pr=pr, pc0=pc0, hb=hb, xq2=xq2):
                    need(e, en, st[f"{a}l2{pr}_{k}"], (sVE, st[a + "q2sc"]), (sTE, st[f"{a}cvu3"]))
                    if app == 0 and pr == 1:
                        need(e, en, st["wready1"])
                    need(e, en, (sTE, st.get(f"{a}ymm{pr-2}_{KD-1}", 0)))
                    need(e, en, (sVE, sVE.n))
                    sVE.inc(e.tensor_tensor(ytmp[:, 0:PW], hb[:, 0:PW], sT3[:, pc0:pc0 + PW], ALU.mult))
                    need(e, en, (sVE, sVE.n))
                    sVE.inc(e.tensor_scalar(xq2[k][:, 0:PW], ytmp[:, 0:PW], MAGIC, -MAGIC, ALU.add, ALU.add))
                    st[f"{a}xq2_{pr}_{k}"] = sVE.n
                on("gpsimd", l2g)
                on("vector", l2v)

        _QSPLIT = [5, 5, 5, 5, 4, 4, 4, 0]

        emit_quant(0, range(MH))
        for pr in range(NPAIR):
            pc0 = 1 + pr * PW
            xq2 = xq2A if pr % 2 == 0 else xq2B
            for mo in range(KD):
                r = mo % 2

                def w2s(e, en, mo=mo, r=r, pr=pr):
                    need(e, en, st["wready1"], (sTE, st[f"{a}cvu3"]))
                    if app == 0:
                        need(e, en, (sVE, st[f"{a}xq{KD-1}"]))
                    if mo >= 2:
                        need(e, en, (sTE, st[f"{a}ymm{pr}_{mo-2}"]))
                    elif pr > 0:
                        need(e, en, (sTE, st[f"{a}ymm{pr-1}_{KD-2+mo}"]))
                    qIN.begin(e, en)
                    qIN.inc(e.dma_start(
                        out=w2sb[r][:, :].rearrange("p (k q) -> p k q", q=128),
                        in_=w2q_d.rearrange("(t p q) -> t p q", p=128, q=128)[mo::KD, :, :].rearrange("k p q -> p k q")))
                    st[f"{a}w2l{pr}_{mo}"] = qIN.end()

                def ymt(e, en, mo=mo, r=r, pr=pr, xq2=xq2):
                    need(e, en, st[f"{a}w2l{pr}_{mo}"],
                         (sVE, st[f"{a}xq2_{pr}_{MH-1}"]))
                    ps = psA if mo % 2 == 0 else psB
                    if mo >= 2:
                        need(e, en, (sVE, st[f"{a}yev{pr}_{mo-2}"]))
                    elif pr > 0:
                        need(e, en, (sVE, st[f"{a}yev{pr-1}_{KD-2+mo}"]))
                    for k in range(MH):
                        ins = e.matmul(ps[:, 0:PW], w2sb[r][:, k * 128:(k + 1) * 128], xq2[k][:, 0:PW],
                                       start=(k == 0), stop=(k == MH - 1))
                    sTE.inc(ins)
                    st[f"{a}ymm{pr}_{mo}"] = sTE.n
                    if mo == KD - 1:
                        st[f"{a}ymms{pr}"] = sTE.n

                def yev(e, en, mo=mo, pr=pr, pc0=pc0):
                    ps = psA if mo % 2 == 0 else psB
                    need(e, en, (sTE, st[f"{a}ymm{pr}_{mo}"]), (sVE, st[a + "q2sc"]))
                    need(e, en, (sVE, sVE.n))
                    sVE.inc(e.tensor_tensor(ytmp[:, 0:PW], ps[:, 0:PW], sT4[:, pc0:pc0 + PW], ALU.mult))
                    need(e, en, (sVE, sVE.n))
                    sVE.inc(e.tensor_tensor(xg[mo][:, pc0:pc0 + PW], ytmp[:, 0:PW], xg[mo][:, pc0:pc0 + PW], ALU.add))
                    st[f"{a}yev{pr}_{mo}"] = sVE.n
                on("sync", w2s)
                on("tensor", ymt)
                on("vector", yev)
                if pr + 1 < NPAIR:
                    s0 = sum(_QSPLIT[:mo])
                    emit_quant(pr + 1, range(s0, s0 + _QSPLIT[mo]))

        def fin(e, en):
            for mo in range(KD):
                need(e, en, (sVE, st[f"{a}yev{NPAIR-1}_{mo}"]))
            st[a + "resid"] = max(st[f"{a}yev{pr}_{mo}"] for pr in range(NPAIR) for mo in range(KD))
        on("vector", fin)

    def ffn_sublayer(app, dep=()):
        ffn_stats(app, with_quant=True, dep=dep)
        ffn_quant_apply(app)
        ffn_mm1_conv(app)
        ffn_quant2(app)
        ffn_mm2(app)

    def tap_sb(name, tiles, cols, deps, eng="sync"):
        if name not in tap_t:
            return
        q = qOUT if eng == "sync" else qGP
        def tf(e, en):
            for d in deps:
                need(e, en, d)
            q.begin(e, en)
            for i, t in enumerate(tiles):
                q.inc(e.dma_start(out=tap_t[name][i * 128:(i + 1) * 128, 0:cols], in_=t[:, 0:cols]))
            st["tap_" + name] = q.end()
        on(eng, tf)

    ffn_stats(0, with_quant=True)
    ffn_quant_apply(0)
    qw_scale(1)
    ffn_mm1_conv(0, extra_per_blk=lambda blk: [qw_pass2_step(1, blk * 8 + j) for j in range(8)])
    st["wready1"] = [st[f"wstr1_{NWT-2}"], st[f"wstr1_{NWT-1}"]]
    ffn_quant2(0)
    ffn_mm2(0)
    tap_sb("x1", xg, G, [(sVE, st["f0resid"])])
    tap_sb("xq0", xq, GV, [(sVE, st["f0xqall"])], eng="gpsimd")
    if "w1q" in tap_t:
        def tf_w1(e, en):
            need(e, en, st["wready0"])
            qGP.begin(e, en)
            for m in range(MH):
                for k in range(KD):
                    off = (k * MH + m) * 16384
                    qGP.inc(e.dma_start(out=tap_t["w1q"][(m * KD + k) * 128:(m * KD + k + 1) * 128, :],
                                        in_=w1q_d[off:off + 16384].rearrange("(p q) -> p q", q=128)))
            st["tap_w1q"] = qGP.end()
        on("gpsimd", tf_w1)
    if "h2d" in tap_t:
        def tf_h2(e, en):
            need(e, en, st["f0h2st31"])
            qGP.begin(e, en)
            for m in range(MH):
                qGP.inc(e.dma_start(out=tap_t["h2d"][m * 128:(m + 1) * 128, :], in_=h2_d[m, :, :]))
            st["tap_h2d"] = qGP.end()
        on("gpsimd", tf_h2)

    ffn_stats("ln2", with_quant=False, dep=[(sVE, st["f0resid"])])

    def xn2v(e, en):
        need(e, en, (sSE, st["fln2rstd"]), st.get("tap_xq0", 0))
        for k in range(KD):
            need(e, en, (sVE, sVE.n))
            sVE.inc(e.tensor_sub(sT1[:, 1:1 + SEG], xg[k][:, 1:1 + SEG], sScr[:, 1:1 + SEG]))
            need(e, en, (sVE, sVE.n))
            sVE.inc(e.tensor_tensor(xn2[k][:, 0:SEG], sT1[:, 1:1 + SEG], sT2[:, 1:1 + SEG], ALU.mult))
            st[f"xn2_{k}"] = sVE.n
        st["xn2all"] = sVE.n
    on("vector", xn2v)
    tap_sb("xn2", xn2, SEG, [(sVE, st["xn2all"])], eng="gpsimd")

    _SPORD = [3, 4, 5, 6, 7, 0, 1, 2]

    def spx(e, en):
        need(e, en, st.get("tap_x1", 0), st.get("tap_xn2", 0))
        for k in _SPORD:
            need(e, en, (sVE, st[f"f0yev{NPAIR-1}_{k}"]))
            qGP.begin(e, en)
            qGP.inc(e.dma_start(out=x1_d[k * 128:(k + 1) * 128, :], in_=xg[k][:, :]))
            st[f"xsp{k}"] = qGP.end()
        st["xspill"] = [st["xsp1"], st["xsp2"]]
        st["xsp_qk"] = [st["xsp6"], st["xsp7"]]
    on("gpsimd", spx)

    def awl(e, en):
        for k in range(KD):
            qGP.begin(e, en)
            qGP.inc(e.dma_start(out=wgk1a[k][:, :], in_=wgk1_t[k * 128:(k + 1) * 128, :]))
        qGP.inc(e.dma_start(out=wgk2a[:, :], in_=wgk2_t))
        st["wgkl"] = qGP.end()
        for k in range(KD):
            qGP.begin(e, en)
            qGP.inc(e.dma_start(out=wva[k][:, :], in_=wv_t[k * 128:(k + 1) * 128, :]))
        st["wvl"] = qGP.end()
        need(e, en, st["xsp_qk"], (sVE, st["xn2all"]))
        for k in range(KD):
            qGP.begin(e, en)
            qGP.inc(e.dma_start(out=wqa[k][:, :], in_=wq_t[k * 128:(k + 1) * 128, :]))
            qGP.inc(e.dma_start(out=wka[k][:, :], in_=wk_t[k * 128:(k + 1) * 128, :]))
        st["wqkl"] = qGP.end()
    on("gpsimd", awl)

    NT2 = SEG // 512
    for ti in range(NT2):
        def ut(e, en, ti=ti):
            need(e, en, st["wgkl"], (sVE, st["xn2all"]))
            need(e, en, (sVE, st.get(f"uev{ti-1}", 0)))
            for k in range(KD):
                ins = e.matmul(psC[0:32, 0:512], wgk1a[k][:, :], xn2[k][:, ti * 512:(ti + 1) * 512],
                               start=(k == 0), stop=(k == KD - 1))
            sTE.inc(ins)
            st[f"umm{ti}"] = sTE.n

        def uv(e, en, ti=ti):
            need(e, en, (sTE, st[f"umm{ti}"]))
            sVE.inc(e.tensor_copy(u_bf[:, ti * 512:(ti + 1) * 512], psC[0:32, 0:512]))
            st[f"uev{ti}"] = sVE.n
        on("tensor", ut)
        on("vector", uv)

    def uf(e, en):
        need(e, en, (sVE, st[f"uev{NT2-1}"]))
        st["u_done"] = sVE.n
    on("vector", uf)

    for half in range(2):
        hw = SEG // 2
        hc0 = half * hw
        NTH = hw // 512 if hw >= 512 else 1
        TW = hw // NTH
        for m in range(KD):
            for ti in range(NTH):
                g = (half * KD + m) * NTH + ti

                def zt(e, en, half=half, hc0=hc0, m=m, ti=ti, g=g):
                    need(e, en, (sVE, st["u_done"]))
                    gp = g - 2
                    if gp >= 0:
                        need(e, en, (sSE, st.get(f"sgk{gp}", 0)))
                    ps = psD if g % 2 == 0 else psE
                    sTE.inc(e.matmul(ps[:, 0:TW if TW <= 512 else 512], wgk2a[:, m * 128:(m + 1) * 128],
                                     u_bf[:, hc0 + ti * TW:hc0 + (ti + 1) * TW], start=True, stop=True))
                    st[f"zmm{g}"] = sTE.n

                def zs(e, en, half=half, hc0=hc0, m=m, ti=ti, g=g):
                    need(e, en, (sTE, st[f"f0ymms{NPAIR-1}"]), (sVE, st["f0resid"]))
                    need(e, en, (sSE, st.get(f"ghalf{half-1}", 0)))
                    need(e, en, (sTE, st[f"zmm{g}"]))
                    ps = psD if g % 2 == 0 else psE
                    sSE.inc(e.activation(sgs[m][:, ti * TW:(ti + 1) * TW], ps[:, 0:TW], AF.Sigmoid, bias=bgk2_sb[:, m:m + 1]))
                    st[f"sgk{g}"] = sSE.n
                on("tensor", zt)
                on("scalar", zs)

        def zl(e, en, half=half, hc0=hc0):
            need(e, en, st["xspill"], (sVE, st["xn2all"]))
            for m in range(KD):
                need(e, en, (sSE, st[f"sgk{(half * KD + m) * NTH + NTH - 1}"]))
                sSE.inc(e.activation(g16[m][:, hc0:hc0 + hw], sgs[m][:, 0:hw], AF.Ln))
            st[f"ghalf{half}"] = sSE.n
        on("scalar", zl)
    st["g16done"] = st["ghalf1"]

    for j in range(NJ):
        for n in range(4):
            g = j * 4 + n

            def vt(e, en, j=j, n=n, g=g):
                need(e, en, st["wvl"], (sVE, st["xn2all"]))
                gp = g - 2
                if gp >= 0:
                    need(e, en, (sVE, st.get(f"vev{gp}", 0)))
                ps = psF if g % 2 == 0 else psG
                for k in range(KD):
                    ins = e.matmul(ps[:, 0:512], xn2[k][:, j * 128:(j + 1) * 128], wva[k][:, n * 512:(n + 1) * 512],
                                   start=(k == 0), stop=(k == KD - 1))
                sTE.inc(ins)
                st[f"vmm{g}"] = sTE.n

            def vv(e, en, j=j, n=n, g=g):
                need(e, en, (sTE, st[f"vmm{g}"]))
                if n == 0:
                    need(e, en, st.get(f"vst{j-1}", 0))
                ps = psF if g % 2 == 0 else psG
                sVE.inc(e.tensor_copy(vtmp[:, n * 512:(n + 1) * 512], ps[:, 0:512]))
                st[f"vev{g}"] = sVE.n
            on("tensor", vt)
            on("vector", vv)

        def vg(e, en, j=j):
            need(e, en, (sVE, st[f"vev{j * 4 + 3}"]))
            qGP.begin(e, en)
            qGP.inc(e.dma_start(out=v_d[j, :, :], in_=vtmp[:, :]))
            st[f"vst{j}"] = qGP.end()
        on("gpsimd", vg)
    def vfin(e, en):
        st["v_done"] = qGP.fence()
    on("gpsimd", vfin)

    def tz(e, en):
        need(e, en, st["v_done"], st["wready1"])
        sGC.inc(e.memset(T32[:, :], 0.0))
        st["Tzero"] = sGC.n
    on("gpsimd", tz)

    PS_O0, PS_O1, PS_A, PS_TD0, PS_TD1 = psA, psB, psC, psD, psE
    for c in range(OC):
        c0 = c * 256
        def l1v(e, en, c=c, c0=c0):
            need(e, en, (sSE, st["g16done"]), (sGC, st["Tzero"]))
            need(e, en, (sVE, st.get(f"c{c-1}brun", 0)))
            for k in range(KD):
                e.tensor_scalar(ebias[:, k:k + 1], Brun[:, k:k + 1], 1.0 / 32.0, LN16, ALU.mult, ALU.add)
                sVE.inc(e.tensor_scalar(ebias[:, KD + k:KD + k + 1], Brun[:, k:k + 1], -1.0 / 32.0, None, ALU.mult))
            st[f"c{c}ebias"] = sVE.n
            need(e, en, (sSE, st.get(f"c{c-1}exps", 0)))
            for k in range(KD):
                sVE.inc(e.tensor_tensor_scan(csc[:, k * 256:(k + 1) * 256], g16[k][:, c0:c0 + 256],
                                             zz256[:, 0:256], 0.0, ALU.add, ALU.add))
                st[f"c{c}scan{k}"] = sVE.n
            st[f"c{c}scans"] = sVE.n
        on("vector", l1v)

        def l1s(e, en, c=c):
            need(e, en, (sVE, st[f"c{c}scans"]), (sVE, st[f"c{c}ebias"]))
            need(e, en, (sVE, st.get(f"c{c-1}qk", 0)))
            for k in range(KD):
                e.activation(eqc[:, k * 256:(k + 1) * 256], csc[:, k * 256:(k + 1) * 256], AF.Exp,
                             bias=ebias[:, k:k + 1], scale=1.0 / 32.0)
                sSE.inc(e.activation(ekc[:, k * 256:(k + 1) * 256], csc[:, k * 256:(k + 1) * 256], AF.Exp,
                                     bias=ebias[:, KD + k:KD + k + 1], scale=-1.0 / 32.0))
            st[f"c{c}exps"] = sSE.n
        on("scalar", l1s)

        def l1b(e, en, c=c):
            need(e, en, (sSE, st[f"c{c}exps"]))
            for k in range(KD):
                sVE.inc(e.tensor_tensor(Brun[:, k:k + 1], Brun[:, k:k + 1], csc[:, k * 256 + 255:k * 256 + 256], ALU.add))
            st[f"c{c}brun"] = sVE.n
        on("vector", l1b)

        def lvg(e, en, c=c):
            need(e, en, st["v_done"])
            need(e, en, (sVE, st.get(f"c{c-2}oall", 0)))
            qGP.begin(e, en)
            qGP.inc(e.dma_start(out=vtk[c % 2][:, :].rearrange("p (j q) -> p j q", q=DV),
                                in_=v_d[2 * c:2 * c + 2, :, :].rearrange("j p q -> p j q")))
            st[f"c{c}vl"] = qGP.end()
        on("gpsimd", lvg)

        for qk in range(2):
            wt, key, ec = (wqa, "q", eqc) if qk == 0 else (wka, "k", ekc)
            tg = qtc if qk == 0 else ktc
            for m in range(KD):
                g = qk * KD + m

                def lq1(e, en, c=c, c0=c0, m=m, g=g, wt=wt, key=key):
                    if g == 0:
                        need(e, en, st["wqkl"], (sVE, st["xn2all"]))
                        need(e, en, (sVE, st.get(f"vev{NJ * 4 - 1}", 0)))
                    gp = g - 2
                    if gp >= 0:
                        need(e, en, (sVE, st.get(f"c{c}qkev{gp}", 0)))
                    ps = psF if g % 2 == 0 else psG
                    for k in range(KD):
                        ins = e.matmul(ps[:, 0:256], wt[k][:, m * 128:(m + 1) * 128], xn2[k][:, c0:c0 + 256],
                                       start=(k == 0), stop=(k == KD - 1))
                    sTE.inc(ins)
                    st[f"c{c}qkmm{g}"] = sTE.n

                def lq2(e, en, c=c, m=m, g=g, ec=ec, tg=tg):
                    if g == 0:
                        need(e, en, (sSE, st[f"c{c}exps"]))
                        need(e, en, (sTE, st.get(f"c{c-1}omm{H-1}", 0)), st.get(f"c{c-1}qsp", 0),
                             (sTE, st.get(f"c{c-1}tdmm{H-1}", 0)))
                    need(e, en, (sTE, st[f"c{c}qkmm{g}"]))
                    ps = psF if g % 2 == 0 else psG
                    sVE.inc(e.tensor_tensor(tg[:, m * 256:(m + 1) * 256], ps[:, 0:256], ec[:, m * 256:(m + 1) * 256], ALU.mult))
                    st[f"c{c}qkev{g}"] = sVE.n
                on("tensor", lq1)
                on("vector", lq2)

        def lqf(e, en, c=c):
            need(e, en, (sVE, st[f"c{c}qkev{2 * KD - 1}"]))
            st[f"c{c}qk"] = sVE.n
            st[f"c{c}qev{KD-1}"] = sVE.n
        on("vector", lqf)

        def lqg(e, en, c=c, c0=c0):
            need(e, en, (sVE, st[f"c{c}qev{KD-1}"]))
            qGP.begin(e, en)
            qGP.inc(e.dma_start(out=qt_d[:, :, c0:c0 + 256].rearrange("k p q -> p k q"),
                                in_=qtc[:, :].rearrange("p (k q) -> p k q", q=256)))
            st[f"c{c}qsp"] = qGP.end()
        on("gpsimd", lqg)

        for jt in range(2):
            def ltt(e, en, c=c, jt=jt):
                need(e, en, (sVE, st[f"c{c}qk"]))
                need(e, en, (sVE, st.get(f"c{c}ktokev{jt-1}", 0)))
                if jt == 0:
                    need(e, en, (sVE, st.get(f"c{c-1}ktokev1", 0)))
                for m in range(KD):
                    ins = e.transpose(psT[:, m * 128:(m + 1) * 128], ktc[:, m * 256 + jt * 128:m * 256 + jt * 128 + 128], ident[:, :])
                sTE.inc(ins)
                st[f"c{c}ktok{jt}"] = sTE.n

            def ltv(e, en, c=c, jt=jt):
                need(e, en, (sTE, st[f"c{c}ktok{jt}"]))
                need(e, en, (sTE, st.get(f"c{c-1}tdmm{H-1}", 0)))
                sVE.inc(e.tensor_copy(ktok[:, jt * 1024:(jt + 1) * 1024], psT[:, 0:1024]))
                st[f"c{c}ktokev{jt}"] = sVE.n
            on("tensor", ltt)
            on("vector", ltv)

        def lt16(e, en, c=c):
            need(e, en, (sVE, st.get(f"c{c-1}tupd", 0)), (sTE, st.get(f"c{c-1}omm{H-1}", 0)))
            need(e, en, (sGC, st["Tzero"]), st["wready1"])
            sVE.inc(e.tensor_copy(T16[:, :], T32[:, :]))
            st[f"c{c}t16"] = sVE.n
        on("vector", lt16)

        for h in range(H):
            def lat(e, en, c=c, h=h):
                need(e, en, (sVE, st[f"c{c}qk"]))
                need(e, en, (sVE, st.get(f"c{c}aev{h-1}", 0)))
                first = True
                for bi, (jt, tt_) in enumerate(((0, 0), (1, 1), (0, 1))):
                    for kk in range(2):
                        ch = 2 * h + kk
                        ins = e.matmul(PS_A[:, bi * 128:(bi + 1) * 128],
                                       ktc[:, ch * 256 + jt * 128:ch * 256 + jt * 128 + 128],
                                       qtc[:, ch * 256 + tt_ * 128:ch * 256 + tt_ * 128 + 128],
                                       start=first, stop=(bi == 2 and kk == 1))
                        first = False
                sTE.inc(ins)
                st[f"c{c}amm{h}"] = sTE.n
            on("tensor", lat)

            def lav(e, en, c=c, h=h):
                need(e, en, (sTE, st[f"c{c}amm{h}"]), (sGC, st["consts_built"]))
                need(e, en, (sTE, st.get(f"c{c}omm{h-1}", 0)))
                e.tensor_tensor(Abuf[:, 0:128], PS_A[:, 0:128], maskJT[:, :], ALU.mult)
                e.tensor_tensor(Abuf[:, 128:256], PS_A[:, 128:256], maskJT[:, :], ALU.mult)
                sVE.inc(e.tensor_copy(Abuf[:, 256:384], PS_A[:, 256:384]))
                st[f"c{c}aev{h}"] = sVE.n
            on("vector", lav)

            def lot(e, en, c=c, h=h):
                need(e, en, (sVE, st[f"c{c}aev{h}"]), st[f"c{c}vl"], (sVE, st[f"c{c}t16"]))
                need(e, en, (sVE, st.get(f"c{c}oev{h-1}", 0)))
                vb = vtk[c % 2]
                first = True
                for cv in range(4):
                    e.matmul(PS_O0[:, cv * 128:(cv + 1) * 128], vb[:, 0 * DV + h * 512 + cv * 128:0 * DV + h * 512 + (cv + 1) * 128],
                             Abuf[:, 0:128], start=first, stop=False)
                    first = False
                for kk in range(2):
                    ch = 2 * h + kk
                    for cv in range(4):
                        e.matmul(PS_O0[:, cv * 128:(cv + 1) * 128],
                                 T16[:, ch * 512 + cv * 128:ch * 512 + (cv + 1) * 128],
                                 qtc[:, ch * 256:ch * 256 + 128], start=False, stop=(kk == 1 and cv == 3))
                first = True
                for cv in range(4):
                    e.matmul(PS_O1[:, cv * 128:(cv + 1) * 128], vb[:, 1 * DV + h * 512 + cv * 128:1 * DV + h * 512 + (cv + 1) * 128],
                             Abuf[:, 128:256], start=first, stop=False)
                    first = False
                for cv in range(4):
                    e.matmul(PS_O1[:, cv * 128:(cv + 1) * 128], vb[:, 0 * DV + h * 512 + cv * 128:0 * DV + h * 512 + (cv + 1) * 128],
                             Abuf[:, 256:384], start=False, stop=False)
                last = None
                for kk in range(2):
                    ch = 2 * h + kk
                    for cv in range(4):
                        last = e.matmul(PS_O1[:, cv * 128:(cv + 1) * 128],
                                        T16[:, ch * 512 + cv * 128:ch * 512 + (cv + 1) * 128],
                                        qtc[:, ch * 256 + 128:ch * 256 + 256], start=False, stop=(kk == 1 and cv == 3))
                sTE.inc(last)
                st[f"c{c}omm{h}"] = sTE.n
                need(e, en, (sVE, st[f"c{c}ktokev1"]))
                need(e, en, (sVE, st.get(f"c{c}tup{h-1}", 0)))
                for kk in range(2):
                    ch = 2 * h + kk
                    ps = PS_TD0 if kk == 0 else PS_TD1
                    for jt in range(2):
                        ins = e.matmul(ps[:, 0:512], ktok[:, jt * 1024 + ch * 128:jt * 1024 + (ch + 1) * 128],
                                       vb[:, jt * DV + h * 512:jt * DV + (h + 1) * 512], start=(jt == 0), stop=(jt == 1))
                    sTE.inc(ins)
                st[f"c{c}tdmm{h}"] = sTE.n
            on("tensor", lot)

            def lov(e, en, c=c, h=h):
                need(e, en, (sTE, st[f"c{c}omm{h}"]))
                need(e, en, st.get(f"c{c-1}osp", 0))
                for tau in range(2):
                    ps = PS_O0 if tau == 0 else PS_O1
                    dst = och[:, h * 1024:(h + 1) * 1024].rearrange("p (cv t) -> p cv t", t=256)[:, :, tau * 128:tau * 128 + 128]
                    src = ps[:, 0:512].rearrange("p (cv t) -> p cv t", t=128)
                    sVE.inc(e.tensor_copy(dst, src))
                st[f"c{c}oev{h}"] = sVE.n
                need(e, en, (sTE, st[f"c{c}tdmm{h}"]))
                for kk in range(2):
                    ch = 2 * h + kk
                    ps = PS_TD0 if kk == 0 else PS_TD1
                    sVE.inc(e.tensor_tensor(T32[:, ch * 512:(ch + 1) * 512], ps[:, 0:512], T32[:, ch * 512:(ch + 1) * 512], ALU.add))
                st[f"c{c}tup{h}"] = sVE.n
            on("vector", lov)
        st[f"c{c}tupd"] = st[f"c{c}tup{H-1}"]

        def log_(e, en, c=c, c0=c0):
            need(e, en, (sVE, st[f"c{c}oev{H-1}"]))
            qGP.begin(e, en)
            qGP.inc(e.dma_start(out=o_d[:, :, c0:c0 + 256].rearrange("cv p t -> p cv t"),
                                in_=och[:, :].rearrange("p (cv t) -> p cv t", t=256)))
            st[f"c{c}osp"] = qGP.end()
        on("gpsimd", log_)

        def lfin(e, en, c=c):
            need(e, en, (sVE, st[f"c{c}tupd"]), (sVE, st[f"c{c}oev{H-1}"]))
            st[f"c{c}oall"] = sVE.n
        on("vector", lfin)

    def pcs(e, en):
        need(e, en, (sVE, st[f"c{OC-1}tupd"]), (sVE, st[f"c{OC-1}brun"]))
        for h in range(H):
            for kk in range(2):
                ch = 2 * h + kk
                off = (h * 257 + kk * 128) * 512
                qGP.begin(e, en)
                qGP.inc(e.dma_start(out=cc_s[off:off + 128 * 512].rearrange("(p q) -> p q", q=512),
                                    in_=T32[:, ch * 512:(ch + 1) * 512]))
            offl = (h * 257 + 256) * 512
            for kk in range(2):
                ch = 2 * h + kk
                qGP.inc(e.dma_start(out=cc_s[offl + kk * 128:offl + kk * 128 + 128].rearrange("(p o) -> p o", o=1),
                                    in_=Brun[:, ch:ch + 1]))
            qGP.inc(e.dma_start(out=cc_s[offl + 256:offl + 512].rearrange("(p o) -> p o", o=2),
                                in_=zzf2[:, 0:2]))
        st["ccs"] = qGP.end()
        need(e, en, st["ccs"])
        sCC.inc(e.collective_compute(
            "AllGather", ALU.bypass,
            replica_groups=[[0, 1, 2, 3], [4, 5, 6, 7]],
            ins=[cc_s], outs=[cc_r]))
        st["ag1"] = sCC.n
    on("gpsimd", pcs)

    def gwl(e, en):
        need(e, en, (sVE, st[f"c{OC-1}oall"]))
        need(e, en, st["ccs"])
        for k in range(KD):
            qGP.begin(e, en)
            qGP.inc(e.dma_start(out=wga[k][:, :], in_=wg_t[k * 128:(k + 1) * 128, :]))
        st["wgl"] = qGP.end()
    on("gpsimd", gwl)

    def xrl(e, en):
        need(e, en, st["ccs"])
        need(e, en, st["wqkl"])
        for k in range(KD):
            qIN.begin(e, en)
            qIN.inc(e.dma_start(out=xg[k][:, :], in_=x1_d[k * 128:(k + 1) * 128, :]))
        st["xreload"] = qIN.end()
    on("sync", xrl)

    for mg in range(MV):
        for ti in range(NT2):
            g = mg * NT2 + ti

            def gt(e, en, mg=mg, ti=ti, g=g):
                need(e, en, st["wgl"], (sVE, st["xn2all"]))
                gp = g - 2
                if gp >= 0:
                    need(e, en, (sSE, st.get(f"gse{gp}", 0)))
                ps = psF if g % 2 == 0 else psG
                for k in range(KD):
                    ins = e.matmul(ps[:, 0:512], wga[k][:, mg * 128:(mg + 1) * 128], xn2[k][:, ti * 512:(ti + 1) * 512],
                                   start=(k == 0), stop=(k == KD - 1))
                sTE.inc(ins)
                st[f"gmm{g}"] = sTE.n

            def gs(e, en, mg=mg, ti=ti, g=g):
                need(e, en, (sTE, st[f"gmm{g}"]))
                if ti == 0:
                    need(e, en, st.get(f"gst{mg-1}", 0))
                ps = psF if g % 2 == 0 else psG
                sSE.inc(e.activation(h2t[:, ti * 512:(ti + 1) * 512], ps[:, 0:512], AF.Silu))
                st[f"gse{g}"] = sSE.n
            on("tensor", gt)
            on("scalar", gs)

        def gg(e, en, mg=mg):
            need(e, en, (sSE, st[f"gse{mg * NT2 + NT2 - 1}"]))
            qGP.begin(e, en)
            qGP.inc(e.dma_start(out=gate_d[mg, :, :], in_=h2t[:, 0:SEG]))
            st[f"gst{mg}"] = qGP.end()
        on("gpsimd", gg)
    def gfin(e, en):
        st["gate_done"] = qGP.fence()
    on("gpsimd", gfin)

    def cmb_g(e, en):
        need(e, en, (sCC, st["ag1"]), st["gate_done"])
        for j in range(4):
            for ch in range(KD):
                h, kk = ch // 2, ch % 2
                off = j * H * 257 * 512 + (h * 257 + 256) * 512 + kk * 128
                qGP.begin(e, en)
                qGP.inc(e.dma_start(out=Lg[:, j * KD + ch:j * KD + ch + 1],
                                    in_=cc_r[off:off + 128].rearrange("(p o) -> p o", o=1)))
        st["Lgl"] = qGP.end()
        sGC.inc(e.memset(Sin[:, :], 0.0))
        st["sin0"] = sGC.n
    on("gpsimd", cmb_g)

    for j in range(4):
        def cj_v(e, en, j=j):
            need(e, en, st["Lgl"], st["consts_loaded"])
            need(e, en, (sSE, st.get(f"ec{j-1}", 0)))
            for ch in range(KD):
                for l in range(4):
                    if l == 0:
                        ins = e.tensor_scalar(eacc[:, ch:ch + 1], Lg[:, l * KD + ch:l * KD + ch + 1], ctl_sb[:, 4 * j + l:4 * j + l + 1], None, ALU.mult)
                    else:
                        need(e, en, (sVE, sVE.n))
                        ins = e.scalar_tensor_tensor(eacc[:, ch:ch + 1], Lg[:, l * KD + ch:l * KD + ch + 1], ctl_sb[:, 4 * j + l:4 * j + l + 1], eacc[:, ch:ch + 1], ALU.mult, ALU.add)
                    sVE.inc(ins)
            st[f"eacc{j}"] = sVE.n
        on("vector", cj_v)

        def cj_s(e, en, j=j):
            need(e, en, (sVE, st[f"eacc{j}"]))
            sSE.inc(e.activation(ecol[:, :], eacc[:, :], AF.Exp, bias=ctl_sb[:, 16 + j:17 + j], scale=1.0 / 32.0))
            st[f"ec{j}"] = sSE.n
        on("scalar", cj_s)

        for ch in range(KD):
            def cj_g(e, en, j=j, ch=ch):
                need(e, en, (sCC, st["ag1"]))
                h, kk = ch // 2, ch % 2
                prev = st.get(f"sacc{j}_{ch-1}", st.get(f"sacc{j-1}_{KD-1}", 0) if ch == 0 else 0)
                need(e, en, (sVE, prev))
                off = j * H * 257 * 512 + (h * 257 + kk * 128) * 512
                r = ch % 2
                qGP.begin(e, en)
                qGP.inc(e.dma_start(out=sjb[:, 0:512] if r == 0 else sjb[:, 0:512],
                                    in_=cc_r[off:off + 128 * 512].rearrange("(p q) -> p q", q=512)))
                st[f"sjl{j}_{ch}"] = qGP.end()
            def cj_a(e, en, j=j, ch=ch):
                need(e, en, st[f"sjl{j}_{ch}"], (sSE, st[f"ec{j}"]), (sGC, st["sin0"]))
                need(e, en, (sVE, st.get(f"sacc{j-1}_{ch}", 0)))
                sVE.inc(e.scalar_tensor_tensor(Sin[:, ch * 512:(ch + 1) * 512], sjb[:, 0:512], ecol[:, ch:ch + 1],
                                               Sin[:, ch * 512:(ch + 1) * 512], ALU.mult, ALU.add))
                st[f"sacc{j}_{ch}"] = sVE.n
            on("gpsimd", cj_g)
            on("vector", cj_a)

    def s16v(e, en):
        for ch in range(KD):
            need(e, en, (sVE, st[f"sacc3_{ch}"]))
        sVE.inc(e.tensor_copy(S16b[:, :], Sin[:, :]))
        st["s16"] = sVE.n
    on("vector", s16v)

    def wol(e, en):
        need(e, en, st["gate_done"])
        for cv in range(MV):
            qGP.begin(e, en)
            qGP.inc(e.dma_start(out=woa[cv][:, :], in_=wout_t[cv * 128:(cv + 1) * 128, :]))
        st["wol"] = qGP.end()
    on("gpsimd", wol)

    for c in range(OC):
        c0 = c * 256
        def pb_g(e, en, c=c, c0=c0):
            need(e, en, (sVE, st["s16"]), st[f"c{OC-1}osp"], st[f"c{OC-1}qsp"])
            need(e, en, (sVE, st.get(f"pb{c-1}done", 0)))
            qGP.begin(e, en)
            qGP.inc(e.dma_start(out=och[:, :].rearrange("p (cv t) -> p cv t", t=256),
                                in_=o_d[:, :, c0:c0 + 256].rearrange("cv p t -> p cv t")))
            qGP.inc(e.dma_start(out=eqc[:, :].rearrange("p (k q) -> p k q", q=256),
                                in_=qt_d[:, :, c0:c0 + 256].rearrange("k p q -> p k q")))
            qGP.inc(e.dma_start(out=gch[:, :].rearrange("p (cv t) -> p cv t", t=256),
                                in_=gate_d[:, :, c0:c0 + 256].rearrange("cv p t -> p cv t")))
            st[f"pb{c}in"] = qGP.end()
        on("gpsimd", pb_g)

        for cv in range(MV):
            def pb_t1(e, en, c=c, cv=cv):
                need(e, en, st[f"pb{c}in"], (sVE, st["s16"]))
                need(e, en, (sVE, st.get(f"pb{c}o1_{cv-2}", 0)))
                h = cv // 4
                ps = psF if cv % 2 == 0 else psG
                for kk in range(2):
                    ch = 2 * h + kk
                    ins = e.matmul(ps[:, 0:256], S16b[:, ch * 512 + (cv % 4) * 128:ch * 512 + (cv % 4) * 128 + 128],
                                   eqc[:, ch * 256:(ch + 1) * 256], start=(kk == 0), stop=(kk == 1))
                sTE.inc(ins)
                st[f"pb{c}corr{cv}"] = sTE.n

            def pb_v1(e, en, c=c, cv=cv):
                ps = psF if cv % 2 == 0 else psG
                need(e, en, (sTE, st[f"pb{c}corr{cv}"]))
                sVE.inc(e.tensor_tensor(och[:, cv * 256:(cv + 1) * 256], ps[:, 0:256], och[:, cv * 256:(cv + 1) * 256], ALU.add))
                st[f"pb{c}o1_{cv}"] = sVE.n
            on("tensor", pb_t1)
            on("vector", pb_v1)

        for cv in range(MV):
            h = cv // 4

            def pb_s1(e, en, c=c, cv=cv):
                need(e, en, (sVE, st[f"pb{c}o1_{cv}"]))
                need(e, en, (sTE, st.get(f"pb{c}msmm{cv-2}", 0)))
                sSE.inc(e.activation(o2scr[:, (cv % 2) * 256:(cv % 2) * 256 + 256],
                                     och[:, cv * 256:(cv + 1) * 256], AF.Square))
                st[f"pb{c}sq{cv}"] = sSE.n

            def pb_t2(e, en, c=c, cv=cv, h=h):
                need(e, en, (sSE, st[f"pb{c}sq{cv}"]))
                if cv % 4 == 0:
                    need(e, en, (sVE, st.get(f"pb{c}msv{h-1}", 0)))
                ins = e.matmul(psC[:, 0:256], ones_b[:, :], o2scr[:, (cv % 2) * 256:(cv % 2) * 256 + 256],
                               start=(cv % 4 == 0), stop=(cv % 4 == 3))
                sTE.inc(ins)
                st[f"pb{c}msmm{cv}"] = sTE.n
                if cv % 4 == 3:
                    st[f"pb{c}ms{h}"] = sTE.n
            on("scalar", pb_s1)
            on("tensor", pb_t2)

            if cv % 4 == 3:
                def pb_v2(e, en, c=c, h=h):
                    need(e, en, (sTE, st[f"pb{c}ms{h}"]))
                    sVE.inc(e.tensor_scalar(msb_t[:, h * 256:(h + 1) * 256], psC[:, 0:256], 1.0 / HV, float(EPS), ALU.mult, ALU.add))
                    st[f"pb{c}msv{h}"] = sVE.n
                on("vector", pb_v2)

        def pb_s2(e, en, c=c):
            ins = None
            for h in range(H):
                need(e, en, (sVE, st[f"pb{c}msv{h}"]))
                ins = e.activation(msb_t[:, h * 256:(h + 1) * 256], msb_t[:, h * 256:(h + 1) * 256], AF.Ln)
            sSE.inc(ins)
            need(e, en, (sSE, sSE.n))
            for h in range(H):
                ins = e.activation(msb_t[:, h * 256:(h + 1) * 256], msb_t[:, h * 256:(h + 1) * 256], AF.Exp, scale=-0.5)
            sSE.inc(ins)
            st[f"pb{c}rstd"] = sSE.n
        on("scalar", pb_s2)

        def pb_v3(e, en, c=c):
            need(e, en, (sSE, st[f"pb{c}rstd"]), st[f"pb{c}in"])
            for cv in range(MV):
                h = cv // 4
                need(e, en, (sVE, sVE.n))
                sVE.inc(e.scalar_tensor_tensor(och[:, cv * 256:(cv + 1) * 256], och[:, cv * 256:(cv + 1) * 256],
                                               rmsw_sb[:, cv % 4:cv % 4 + 1], msb_t[:, h * 256:(h + 1) * 256], ALU.mult, ALU.mult))
            need(e, en, (sVE, sVE.n))
            for cv in range(MV):
                need(e, en, (sVE, sVE.n))
                sVE.inc(e.tensor_tensor(och[:, cv * 256:(cv + 1) * 256], och[:, cv * 256:(cv + 1) * 256],
                                        gch[:, cv * 256:(cv + 1) * 256], ALU.mult))
            st[f"pb{c}of"] = sVE.n
        on("vector", pb_v3)

        for mo in range(KD):
            def pb_t3(e, en, c=c, mo=mo):
                need(e, en, (sVE, st[f"pb{c}of"]), st["wol"], st["xreload"])
                need(e, en, (sVE, st.get(f"pb{c}xr{mo-2}", 0)))
                ps = psD if mo % 2 == 0 else psE
                for cv in range(MV):
                    ins = e.matmul(ps[:, 0:256], woa[cv][:, mo * 128:(mo + 1) * 128], och[:, cv * 256:(cv + 1) * 256],
                                   start=(cv == 0), stop=(cv == MV - 1))
                sTE.inc(ins)
                st[f"pb{c}wo{mo}"] = sTE.n

            def pb_v4(e, en, c=c, c0=c0, mo=mo):
                ps = psD if mo % 2 == 0 else psE
                need(e, en, (sTE, st[f"pb{c}wo{mo}"]))
                sVE.inc(e.tensor_tensor(xg[mo][:, 1 + c0:1 + c0 + 256], ps[:, 0:256], xg[mo][:, 1 + c0:1 + c0 + 256], ALU.add))
                st[f"pb{c}xr{mo}"] = sVE.n
            on("tensor", pb_t3)
            on("vector", pb_v4)

        def pb_fin(e, en, c=c):
            need(e, en, (sVE, st[f"pb{c}xr{KD-1}"]))
            st[f"pb{c}done"] = sVE.n
        on("vector", pb_fin)
    st["x2done"] = st[f"pb{OC-1}done"]
    tap_sb("x2", xg, G, [(sVE, st["x2done"])])

    def c2g(e, en):
        need(e, en, (sVE, st["pb0done"]), (sVE, st["x2done"]))
        for k in range(KD):
            qGP.begin(e, en)
            qGP.inc(e.dma_start(out=cc2_s[k * 128:(k + 1) * 128].rearrange("(p o) -> p o", o=1), in_=xg[k][:, 1:2]))
            qGP.inc(e.dma_start(out=cc2_s[D + k * 128:D + (k + 1) * 128].rearrange("(p o) -> p o", o=1), in_=xg[k][:, SEG:SEG + 1]))
        st["cc2s"] = qGP.end()
        need(e, en, st["cc2s"])
        sCC.inc(e.collective_compute(
            "AllGather", ALU.bypass,
            replica_groups=[[0, 1, 2, 3], [4, 5, 6, 7]],
            ins=[cc2_s], outs=[cc2_r]))
        st["ag2"] = sCC.n
        need(e, en, (sCC, st["ag2"]))
        for k in range(KD):
            qGP.begin(e, en)
            qGP.inc(e.dma_start(out=rows8[:, k * 8:(k + 1) * 8].rearrange("p (j f) -> p j f", f=2),
                                in_=cc2_r.rearrange("(j f p) -> p j f", f=2, p=D)[k * 128:(k + 1) * 128, :, :],
                                allow_slow_non_contiguous=True))
        st["rows8l"] = qGP.end()
    on("gpsimd", c2g)

    def c2v(e, en):
        need(e, en, st["rows8l"], st["consts_loaded"], st.get("tap_x2", 0))
        for k in range(KD):
            for j in range(4):
                if j == 0:
                    e.tensor_scalar(xg[k][:, 0:1], rows8[:, k * 8 + j * 2 + 1:k * 8 + j * 2 + 2], ctl_sb[:, 20 + j:21 + j], None, ALU.mult)
                    ins = e.tensor_scalar(xg[k][:, G - 3:G - 2], rows8[:, k * 8 + j * 2:k * 8 + j * 2 + 1], ctl_sb[:, 24 + j:25 + j], None, ALU.mult)
                else:
                    e.scalar_tensor_tensor(xg[k][:, 0:1], rows8[:, k * 8 + j * 2 + 1:k * 8 + j * 2 + 2], ctl_sb[:, 20 + j:21 + j], xg[k][:, 0:1], ALU.mult, ALU.add)
                    ins = e.scalar_tensor_tensor(xg[k][:, G - 3:G - 2], rows8[:, k * 8 + j * 2:k * 8 + j * 2 + 1], ctl_sb[:, 24 + j:25 + j], xg[k][:, G - 3:G - 2], ALU.mult, ALU.add)
                sVE.inc(ins)
                need(e, en, (sVE, sVE.n))
        st["halo_fixed_v"] = sVE.n
    on("vector", c2v)
    st["halo_fixed"] = st["halo_fixed_v"]

    ffn_sublayer(1, dep=[(sVE, st["x2done"]), (sVE, st["halo_fixed"])])

    def out_s(e, en):
        need(e, en, (sVE, st["f1resid"]))
        for k in range(KD):
            qOUT.begin(e, en)
            qOUT.inc(e.dma_start(out=y_out[k * 128:(k + 1) * 128, :], in_=xg[k][:, 1:1 + SEG]))
        st["yout"] = qOUT.end()
        qOUT.drain(e, en)
        qGP.drain(e, en)
        qIN.drain(e, en)
    on("sync", out_s)

    if merge:
        for ename in ("sync", "gpsimd", "tensor", "vector", "scalar"):
            def run_all(e, fs=tuple(equeue[ename]), en=ename):
                for f in fs:
                    f(e, en)
            getattr(block, ename)(run_all)
    block_cm.__exit__(None, None, None)
    return nc, st


def _prep_core_inputs(core, SEG, x_full, w):
    b, s = core // 4, core % 4
    N = 4 * SEG
    t0 = s * SEG
    G = SEG + 4
    xg = np.zeros((G, 1024), np.float32)
    lo, hi = t0 - 1, t0 + SEG + 1
    slo, shi = max(lo, 0), min(hi, N)
    xg[slo - lo:shi - lo, :] = x_full[b, slo:shi, :]
    d = {
        "x": np.ascontiguousarray(xg.T),
        "wq_t": np.ascontiguousarray(w["Wq"].T),
        "wk_t": np.ascontiguousarray(w["Wk"].T),
        "wv_t": np.ascontiguousarray(w["Wv"].T),
        "wg_t": np.ascontiguousarray(w["Wg"].T),
        "wgk1_t": np.ascontiguousarray(w["Wgk1"].T),
        "wgk2_t": np.ascontiguousarray(w["Wgk2"].T),
        "bgk2": np.ascontiguousarray(w["bgk2"]),
        "wout_t": np.ascontiguousarray(w["Wout"].T),
        "w1_t": np.ascontiguousarray(w["w1"].T),
        "w2_t": np.ascontiguousarray(w["w2"].T),
        "conv_w": np.ascontiguousarray(w["conv_w"][:, 0, :].T.reshape(3, -1) if w["conv_w"].ndim == 3 else w["conv_w"]),
        "conv_b": np.ascontiguousarray(w["conv_b"]),
        "rms_w": np.ascontiguousarray(w["rms_w"]),
    }
    cw = w["conv_w"].reshape(4096, 3)
    d["conv_w"] = np.ascontiguousarray(cw.T)
    ctl = np.zeros(32, np.float32)
    g = s
    for j in range(4):
        for l in range(4):
            ctl[4 * j + l] = 1.0 if (j <= l < g) else 0.0
    for j in range(4):
        ctl[16 + j] = 0.0 if j < g else NEG
    for j in range(4):
        ctl[20 + j] = 1.0 if j == g - 1 else 0.0
        ctl[24 + j] = 1.0 if j == g + 1 else 0.0
    d["ctl"] = np.tile(ctl[None, :], (128, 1)).astype(np.float32)
    return d


def kernel(L0_attn_x, L0_attn_Wq, L0_attn_Wk, L0_attn_Wv, L0_attn_Wg,
           L0_attn_Wgk1, L0_attn_Wgk2, L0_attn_bgk2, L0_attn_Wout, L0_attn_rms_w,
           L0_ffn_w1, L0_ffn_conv_w, L0_ffn_conv_b, L0_ffn_w2):
    SEG = 2048
    x_full = np.asarray(L0_attn_x, np.float32)
    w = {
        "Wq": np.asarray(L0_attn_Wq, np.float32), "Wk": np.asarray(L0_attn_Wk, np.float32),
        "Wv": np.asarray(L0_attn_Wv, np.float32), "Wg": np.asarray(L0_attn_Wg, np.float32),
        "Wgk1": np.asarray(L0_attn_Wgk1, np.float32), "Wgk2": np.asarray(L0_attn_Wgk2, np.float32),
        "bgk2": np.asarray(L0_attn_bgk2, np.float32), "Wout": np.asarray(L0_attn_Wout, np.float32),
        "rms_w": np.asarray(L0_attn_rms_w, np.float32),
        "w1": np.asarray(L0_ffn_w1, np.float32), "conv_w": np.asarray(L0_ffn_conv_w, np.float32),
        "conv_b": np.asarray(L0_ffn_conv_b, np.float32), "w2": np.asarray(L0_ffn_w2, np.float32),
    }
    nc = build_program(SEG=SEG)
    in_maps = [_prep_core_inputs(c, SEG, x_full, w) for c in range(8)]
    res = run_bass_kernel_spmd(nc, in_maps, core_ids=list(range(8)))
    out = np.empty((2, 8192, 1024), np.float32)
    for c in range(8):
        b, s = c // 4, c % 4
        out[b, s * SEG:(s + 1) * SEG, :] = res.results[c]["y"].T
    return out
